# version 04
# baseline (speedup 1.0000x reference)
import contextlib
import numpy as np
import ml_dtypes
import concourse.bass as bass
import concourse.mybir as mybir
from concourse.bass_utils import run_bass_kernel_spmd

F32 = mybir.dt.float32
BF16 = mybir.dt.bfloat16
U32 = mybir.dt.uint32
AF = mybir.ActivationFunctionType
ALU = mybir.AluOpType
AX = mybir.AxisListType

T = 2048
D = 1024
NT = T // 128
ND = D // 128
H = 16
HD = 64
NP = 8
SCALE = HD ** -0.5
DH = 4096
NDH = DH // 128
E = 8
CC = 384
SLOTS = E * CC
NCC = SLOTS // 128
CAPACITY = 3072
EPS = 1e-5
KB = 1024

UPTO = "full"
DBG_MOE = False



def _blk_lhs(w):
    I, O = w.shape
    return np.ascontiguousarray(
        w.reshape(I // 128, 128, O // 128, 128).transpose(2, 1, 0, 3))


def _blk_rhs(w, ncol):
    I, O = w.shape
    return np.ascontiguousarray(
        w.reshape(I // 128, 128, O // ncol, ncol).transpose(2, 1, 0, 3))


def _cols(b):
    return np.ascontiguousarray(b.reshape(-1, 128).T)


def prep_inputs(core, **ins):
    g = lambda n: np.asarray(ins[n])
    d = {}
    d["x"] = np.ascontiguousarray(g("L0_attn_x")[core])
    d["rmask"] = (np.arange(8) < core).astype(np.float32)
    d["a0_wq"] = _blk_lhs(g("L0_attn_Wq"))
    d["a0_wk"] = _blk_lhs(g("L0_attn_Wk"))
    d["a0_wv"] = _blk_lhs(g("L0_attn_Wv"))
    d["a0_wo"] = _blk_rhs(g("L0_attn_Wo"), 256)
    d["a0_bqkv"] = np.concatenate(
        [_cols(g("L0_attn_bq")), _cols(g("L0_attn_bk")), _cols(g("L0_attn_bv"))],
        axis=1)
    d["a0_bo"] = np.asarray(g("L0_attn_bo"))
    d["f0_w1t"] = _blk_lhs(np.ascontiguousarray(g("L0_ffn_w1_w").T))
    d["f0_w2t"] = _blk_lhs(np.ascontiguousarray(g("L0_ffn_w2_w").T))
    d["f0_h1t"] = _blk_lhs(np.ascontiguousarray(g("L0_ffn_h1_w").T))
    d["f0_h2t"] = _blk_lhs(np.ascontiguousarray(g("L0_ffn_h2_w").T))
    d["f0_w3t"] = _blk_rhs(np.ascontiguousarray(g("L0_ffn_w3_w").T), 128)
    d["f0_s"] = np.concatenate(
        [_cols(g("L0_ffn_w1_s")), _cols(g("L0_ffn_w2_s")),
         _cols(g("L0_ffn_h1_s")), _cols(g("L0_ffn_h2_s")),
         _cols(g("L0_ffn_w3_s"))], axis=1)
    d["a1_wq"] = _blk_lhs(g("L1_attn_Wq"))
    d["a1_wl"] = _blk_lhs(g("L1_attn_Wl"))
    d["a1_wk"] = _blk_lhs(g("L1_attn_Wk"))
    d["a1_wv"] = _blk_lhs(g("L1_attn_Wv"))
    d["a1_wo"] = _blk_rhs(g("L1_attn_Wo"), 256)
    d["a1_bqkv"] = np.concatenate(
        [_cols(g("L1_attn_bq")), _cols(g("L1_attn_bk")), _cols(g("L1_attn_bv")),
         _cols(g("L1_attn_bl"))], axis=1)
    d["a1_bo"] = np.asarray(g("L1_attn_bo"))
    d["m_rwt"] = np.ascontiguousarray(g("L1_ffn_router_w").T)
    w1 = g("L1_ffn_w1")[core]
    d["m_w1"] = np.ascontiguousarray(
        w1.reshape(ND, 128, 8, 512).transpose(2, 1, 0, 3))
    w2 = g("L1_ffn_w2")[core]
    d["m_w2"] = np.ascontiguousarray(
        w2.reshape(4, 8, 128, 2, 512).transpose(3, 0, 2, 1, 4))
    d["m_b1"] = np.ascontiguousarray(
        g("L1_ffn_b1")[core].reshape(NDH, 128).T)
    d["m_b2"] = np.ascontiguousarray(g("L1_ffn_b2")[core])
    esel = np.zeros(8, np.float32); esel[core] = 1.0
    d["esel"] = esel
    d["cbase"] = np.array([core * T], np.float32)
    return d



def build_nc(upto="full"):
    nc = bass.Bass("TRN2", target_bir_lowering=False)
    io = {}

    def ein(name, shape):
        io[name] = nc.dram_tensor(name, list(shape), F32, kind="ExternalInput").ap()

    ein("x", [T, D]); ein("rmask", [8])
    ein("a0_wq", [NP, 128, ND, 128]); ein("a0_wk", [NP, 128, ND, 128])
    ein("a0_wv", [NP, 128, ND, 128]); ein("a0_wo", [4, 128, ND, 256])
    ein("a0_bqkv", [128, 3 * ND]); ein("a0_bo", [D])
    ein("f0_w1t", [NDH, 128, ND, 128]); ein("f0_w2t", [NDH, 128, ND, 128])
    ein("f0_h1t", [NDH, 128, NDH, 128]); ein("f0_h2t", [NDH, 128, NDH, 128])
    ein("f0_w3t", [8, 128, NDH, 128])
    ein("f0_s", [128, 4 * NDH + 8])
    ein("a1_wq", [NP, 128, ND, 128]); ein("a1_wl", [2, 128, ND, 128])
    ein("a1_wk", [NP, 128, 2, 128]); ein("a1_wv", [NP, 128, 2, 128])
    ein("a1_wo", [4, 128, ND, 256])
    ein("a1_bqkv", [128, 3 * ND + 2]); ein("a1_bo", [D])
    ein("m_rwt", [D, E])
    ein("m_w1", [8, 128, ND, 512])
    ein("m_w2", [2, 4, 128, 8, 512])
    ein("m_b1", [128, NDH]); ein("m_b2", [D])
    ein("esel", [8]); ein("cbase", [1])

    out_t = nc.dram_tensor("out", [T, D], F32, kind="ExternalOutput").ap()
    xres = nc.dram_tensor("xres", [T, D], F32, kind="Internal").ap()
    bcz = nc.dram_tensor("bcz", [H, T], F32, kind="Internal").ap()
    bcb = nc.dram_tensor("bcb", [H], F32, kind="Internal").ap()
    bcm = nc.dram_tensor("bcm", [2, T], F32, kind="Internal").ap()
    cc_in = nc.dram_tensor("cc_in", [E], F32, kind="Internal")
    cc_out = nc.dram_tensor("cc_out", [8 * E], F32, kind="Internal")
    xag_in_t = nc.dram_tensor("xag_in", [T, D], BF16, kind="Internal")
    xag_in = xag_in_t.ap()
    xall_t = nc.dram_tensor("xall", [8 * T, D], BF16, kind="Internal")
    xall = xall_t.ap()
    idxb_t = nc.dram_tensor("idxb", [E * 8 * T], F32, kind="Internal")
    idxb = idxb_t.ap()
    idxall_t = nc.dram_tensor("idxall", [E * 8 * T], F32, kind="Internal")
    eog_t = nc.dram_tensor("eog", [CAPACITY, D], BF16, kind="Internal")
    eog = eog_t.ap()
    eoall_t = nc.dram_tensor("eoall", [8 * CAPACITY, D], BF16, kind="Internal")
    eoall = eoall_t.ap()
    if DBG_MOE:
        dbg_xb = nc.dram_tensor("dbg_xb", [512, D], F32, kind="ExternalOutput").ap()
        dbg_u24 = nc.dram_tensor("dbg_u24", [128, 24], U32, kind="ExternalOutput").ap()
        dbg_h = nc.dram_tensor("dbg_h", [128, 1024], F32, kind="ExternalOutput").ap()
        dbg_i24 = nc.dram_tensor("dbg_i24", [24, 128], F32, kind="ExternalOutput").ap()

    st = contextlib.ExitStack()
    with st:
        class Arena:
            def __init__(self, name, kib):
                self.nbytes = kib * KB
                self.t = st.enter_context(
                    nc.sbuf_tensor(name, [128, self.nbytes // 4], F32))

            def f32(self, off, n, parts=slice(None)):
                assert off % 4 == 0 and off + 4 * n <= self.nbytes, (off, n)
                return self.t.ap()[parts, off // 4: off // 4 + n]

            def bf16(self, off, n, parts=slice(None)):
                assert off % 2 == 0 and off + 2 * n <= self.nbytes, (off, n)
                return self.t.ap().bitcast(BF16)[parts, off // 2: off // 2 + n]

        ar1 = Arena("ar1", 64)
        ar2 = Arena("ar2", 64)
        ar3 = Arena("ar3", 32)
        ar4 = Arena("ar4", 22)
        wst = Arena("wst", 16)

        sbt = lambda name, shape, dt=F32: st.enter_context(
            nc.sbuf_tensor(name, shape, dt))
        ident = sbt("ident", [128, 128], BF16)
        identf = sbt("identf", [128, 128], F32)
        iota512 = sbt("iota512", [128, 512], F32)
        ciota = sbt("ciota", [128, NCC], F32)
        ones2c = sbt("ones2c", [128, 2], BF16)
        b_a0 = sbt("b_a0", [128, 3 * ND], F32)
        b_a1 = sbt("b_a1", [128, 3 * ND + 2], F32)
        f0s = sbt("f0s", [128, 4 * NDH + 8], F32)
        mb1 = sbt("mb1", [128, NDH], F32)
        stats = sbt("stats", [128, 2 * NT], F32)
        mrs = sbt("mrs", [128, 2 * NT], F32)
        bh2 = sbt("bh2", [128, 2], F32)
        small = sbt("small", [128, 96], F32)
        smallu = sbt("smallu", [128, 44], U32)
        sm2 = sbt("sm2", [128, 112], F32)
        er16 = sbt("er16", [16, 256], F32)
        i24 = sbt("i24", [24, 128], F32)
        u24 = sbt("u24", [128, 24], U32)
        esel_sb = sbt("esel_sb", [8, 1], F32)
        ZEROT = sbt("zerot", [128, 256], F32)
        cst = sbt("cst", [128, 2], F32)
        rmk = sbt("rmk", [8, 1], F32)
        agg = sbt("agg", [8, 8], F32)
        off8 = sbt("off8", [8, 1], F32)
        ones8 = sbt("ones8", [8, 1], F32)

        greg = st.enter_context(nc.gpsimd.register("greg"))
        psA = st.enter_context(nc.psum_tensor("psA", [128, 1024], F32))
        psB = st.enter_context(nc.psum_tensor("psB", [128, 1024], F32))
        pav = st.enter_context(nc.psum_tensor("pav", [128, 512], F32))
        pax = st.enter_context(nc.psum_tensor("pax", [128, 512], F32))
        pex = st.enter_context(nc.psum_tensor("pex", [128, 512], F32))
        ptr = st.enter_context(nc.psum_tensor("ptr", [128, 1024], BF16))
        PS = {"A0": psA[:, 0:512], "A1": psA[:, 512:1024],
              "B0": psB[:, 0:512], "B1": psB[:, 512:1024],
              "av": pav[:], "ax": pax[:], "ex": pex[:],
              "trf": ptr.ap().bitcast(F32)}
        ROT4 = ["A0", "A1", "B0", "B1"]

        SEMS = ["te", "ve", "se", "ge"]
        semh = {n: st.enter_context(nc.semaphore(n)) for n in SEMS}
        semc = {n: 0 for n in SEMS}

        def sem_of(name):
            if name not in semh:
                semh[name] = st.enter_context(nc.semaphore(name))
                semc[name] = 0
            return semh[name]

        def inc(instr, name, n=1):
            sem_of(name)
            semc[name] += n
            instr.then_inc(semh[name], n)
            return semc[name]

        def wait(eng, name, val=None):
            sem_of(name)
            eng.wait_ge(semh[name], semc[name] if val is None else val)

        freed = {}

        def pm_produce(eng, bank):
            f = freed.pop(bank, None)
            if f is not None:
                eng.wait_ge(semh[f[0]], f[1])

        def pm_consumed(bank, sem, val=None):
            freed[bank] = (sem, semc[sem] if val is None else val)

        blk = st.enter_context(nc.Block())

        def sec(name, fn):
            getattr(blk, name)(fn)

        X_T = lambda ti: ar1.f32(ti * 4096, D)
        XH_TM = lambda ti: ar3.bf16(ti * 2048, D)
        XH_FM = lambda dc: ar2.bf16(dc * 4096, T)
        QSL = lambda buf: ar2.bf16(32 * KB + buf * 8192, T)
        KSL = lambda buf: ar2.bf16(32 * KB + buf * 8192 + 4096, T)
        ZRT = ar2.f32(48 * KB, T, parts=slice(64, 65))
        LArow = lambda lc: ar2.bf16(56 * KB + lc * 4096, T)
        VAUG = lambda tc: ar4.bf16(tc * 272, 136)
        PP = lambda buf: ar4.bf16(4608 + buf * 2048, 1024)
        QKSQ = ar4.bf16(8704, 2048)
        ZBC = ar4.f32(12800, 2048)
        AV_FM = lambda dc: ar3.bf16(dc * 4096, T)
        BOBC = ar2.f32(56 * KB, D)
        BTMP = ar2.bf16(48 * KB, 2048, parts=slice(0, 64))
        MASKS = lambda m: wst.bf16(12 * KB + m * 1024, 512)
        H_T = lambda oc: ar1.bf16(oc * 2048, 1024)
        G_T = lambda oc: ar2.bf16(oc * 2048, 1024)
        XHF_H = lambda dc: ar2.bf16(dc * 2048, 1024)
        XIO_F = lambda b: ar4.f32(b * 4096, D)
        Y_FMF = lambda ob: ar2.f32(ob * 4096, 1024)
        SILT = lambda b: ar4.bf16(12288 + b * 1024, 512)
        XHM_FM = lambda dc: ar1.bf16(dc * 4096, T)
        XB_TM = lambda sc: ar1.bf16(sc * 2048, D)
        XB_FM = lambda dc: ar2.bf16(dc * 6144, SLOTS)
        HB_T = lambda fcg: ar1.bf16(fcg * 2048, 1024)
        EOT3 = lambda cc: ar3.bf16(cc * 2048, D)
        EO_T = lambda cc: ar1.bf16(cc * 2048, 1024)
        D_T = lambda tc: ar1.bf16(48 * KB + tc * 1024, 512)
        XGF = lambda dc: ar2.bf16(dc * 6144, SLOTS)
        B2BC = ar2.f32(48 * KB, D)
        ONEHOT = ar2.f32(0, T, parts=slice(0, 8))
        CNTI = ar2.f32(8 * KB, T, parts=slice(0, 8))
        EROW8 = ar2.f32(16 * KB, T, parts=slice(0, 8))
        TMP8 = ar2.f32(24 * KB, T, parts=slice(0, 8))
        RANKR = ar2.f32(32 * KB, T, parts=slice(0, 1))
        OFFR = ar2.f32(40 * KB, T, parts=slice(32, 33))
        XIO_M = lambda b: ar2.f32(56 * KB + b * 4096, D)
        HH_T = lambda fc: ar4.bf16(fc * 768, CC)
        SLOTBC = ar4.f32(12288, 2048)
        RT = lambda b: ar4.bf16(20480 + b * 1024, 512)
        RWTS = wst.bf16(12 * KB, ND * 8)
        WS = lambda off, n: wst.bf16(off, n)
        WSF = lambda off, n: wst.f32(off, n)

        S_PSTAR, S_ESTAR, S_G, S_NMAX, S_TMP = 0, 16, 64, 80, 83
        C_LG, C_PROB, C_Z8, C_MX8 = 0, 8, 16, 24
        C_RANK, C_OFFT, C_KEPT, C_SLOTP, C_SLSH = 32, 48, 64, 80, 96

        def p0_sync(eng):
            i_ = eng.dma_start(
                out=ar1.f32(0, NT * D).rearrange("p (i d) -> p i d", i=NT),
                in_=io["x"].rearrange("(i p) d -> p i d", p=128))
            inc(i_, "xl", 16)
            i_ = eng.dma_start(out=b_a0[:], in_=io["a0_bqkv"]); inc(i_, "pp", 16)
            i_ = eng.dma_start(out=b_a1[:], in_=io["a1_bqkv"]); inc(i_, "pp", 16)
            i_ = eng.dma_start(out=f0s[:], in_=io["f0_s"]); inc(i_, "pp", 16)
            i_ = eng.dma_start(out=mb1[:], in_=io["m_b1"]); inc(i_, "pp", 16)
            i_ = eng.dma_start(out=esel_sb[:], in_=io["esel"].unsqueeze(1))
            inc(i_, "pp", 16)
            i_ = eng.dma_start(out=rmk[:], in_=io["rmask"].unsqueeze(1))
            inc(i_, "pp", 16)

        def p0_gp(eng):
            eng.memset(ident[:], 1.0)
            eng.memset(identf[:], 1.0)
            eng.drain()
            eng.affine_select(ident[:], ident[:], pattern=[[-1, 128]],
                              compare_op=ALU.is_equal, fill=0.0, base=0,
                              channel_multiplier=1)
            eng.drain()
            eng.affine_select(identf[:], identf[:], pattern=[[-1, 128]],
                              compare_op=ALU.is_equal, fill=0.0, base=0,
                              channel_multiplier=1)
            eng.memset(ones2c[:], 0.0)
            eng.drain()
            eng.memset(ones2c[0:64, 0:1], 1.0)
            eng.drain()
            eng.memset(ones2c[64:128, 1:2], 1.0)
            eng.memset(ones8[:], 1.0)
            eng.memset(ZEROT[:], 0.0)
            eng.memset(cst[:, 0:1], float(EPS))
            eng.memset(cst[:, 1:2], float(SCALE))
            eng.memset(wst.bf16(12 * KB, 2048), 1.0)
            eng.drain()
            for m in range(4):
                eng.affine_select(MASKS(m), MASKS(m), pattern=[[1, 512]],
                                  compare_op=ALU.is_ge, fill=0.0,
                                  base=-128 * m, channel_multiplier=-1)
            eng.iota(iota512[:], pattern=[[1, 512]], base=0, channel_multiplier=0,
                     allow_small_or_imprecise_dtypes=True)
            i_ = eng.iota(ciota[:], pattern=[[128, NCC]], base=0,
                          channel_multiplier=1,
                          allow_small_or_imprecise_dtypes=True)
            inc(i_, "ge", 1)

        sec("sync", p0_sync)
        sec("gpsimd", p0_gp)

        def emit_ln(sq_ap):
            sem_of("xl")
            dxv, vev, tev = semc["xl"], semc["ve"], semc["te"]

            def s_sq(eng):
                eng.wait_ge(semh["xl"], dxv)
                eng.wait_ge(semh["ve"], vev)
                eng.wait_ge(semh["te"], tev)
                i_ = None
                for ti in range(NT):
                    if ti:
                        eng.drain()
                    i_ = eng.activation(sq_ap, X_T(ti), AF.Square,
                                        accum_out=stats[:, NT + ti:NT + ti + 1])
                inc(i_, "se", 1)

            def v_stats(eng):
                eng.wait_ge(semh["xl"], dxv)
                wait(eng, "se")
                for ti in range(NT):
                    eng.reduce_sum(stats[:, ti:ti + 1], X_T(ti), axis=AX.X)
                eng.drain()
                eng.tensor_scalar(mrs[:, 0:NT], stats[:, 0:NT], 1.0 / D, None,
                                  ALU.mult)
                eng.drain()
                eng.tensor_scalar(stats[:, NT:2 * NT], stats[:, NT:2 * NT],
                                  1.0 / D, None, ALU.mult)
                eng.drain()
                eng.tensor_tensor(stats[:, 0:NT], mrs[:, 0:NT], mrs[:, 0:NT],
                                  ALU.mult)
                eng.drain()
                i_ = eng.tensor_tensor(stats[:, NT:2 * NT], stats[:, NT:2 * NT],
                                       stats[:, 0:NT], ALU.subtract)
                inc(i_, "ve", 1)

            def s_rstd(eng):
                wait(eng, "ve")
                eng.wait_ge(semh["ge"], 1)
                i_ = eng.activation(mrs[:, NT:2 * NT], stats[:, NT:2 * NT],
                                    AF.Sqrt, bias=cst[:, 0:1], scale=1.0)
                inc(i_, "se", 1)

            def v_apply(eng):
                wait(eng, "se")
                eng.reciprocal(mrs[:, NT:2 * NT], mrs[:, NT:2 * NT])
                eng.drain()
                for ti in range(NT):
                    i_ = eng.tensor_scalar(XH_TM(ti), X_T(ti), mrs[:, ti:ti + 1],
                                           mrs[:, NT + ti:NT + ti + 1],
                                           ALU.subtract, ALU.mult)
                    inc(i_, "ve", 1)

            sec("scalar", s_sq)
            sec("vector", v_stats)
            sec("scalar", s_rstd)
            sec("vector", v_apply)
            return semc["ve"] - NT

        def emit_tm2fm(ve_base, src, dst_fm, ntiles=NT, dchunks=ND):
            veE0 = semc["ve"]

            def t_tr(eng):
                for ti in range(ntiles):
                    eng.wait_ge(semh["ve"], ve_base + ti + 1)
                    eng.wait_ge(semh["ve"], veE0 + ti)
                    i_ = None
                    for dc in range(dchunks):
                        i_ = eng.transpose(
                            ptr[:, dc * 128:dc * 128 + 128],
                            src(ti)[:, dc * 128:dc * 128 + 128], ident[:])
                    inc(i_, "te", 1)

            def v_ev(eng):
                base = semc["te"] - ntiles
                for ti in range(ntiles):
                    eng.wait_ge(semh["te"], base + ti + 1)
                    i_ = None
                    for dc in range(dchunks):
                        i_ = eng.tensor_copy(
                            dst_fm(dc)[:, ti * 128:ti * 128 + 128],
                            ptr[:, dc * 128:dc * 128 + 128])
                    inc(i_, "ve", 1)

            sec("tensor", t_tr)
            sec("vector", v_ev)

        wstbar = [0, 0]

        def set_wstbar():
            wstbar[0] = semc["te"]
            wstbar[1] = semc["ve"]

        def fetch(eng, dst_ap, src_ap, bufkey):
            if wstbar[0]:
                eng.wait_ge(semh["te"], wstbar[0])
            if wstbar[1]:
                eng.wait_ge(semh["ve"], wstbar[1])
            f = freed.pop(bufkey, None)
            if f is not None:
                eng.wait_ge(semh[f[0]], f[1])
            i_ = eng.dma_start(out=dst_ap, in_=src_ap)
            return inc(i_, "F" + bufkey, 16)

        def fwait(eng, *keys):
            for k in keys:
                wait(eng, "F" + k)

        def emit_attention(L):
            pfx = "a0_" if L == 0 else "a1_"
            bqk = b_a0 if L == 0 else b_a1
            causal = (L == 0)

            set_wstbar()
            ve0 = emit_ln(ar2.f32(0, D))
            emit_tm2fm(ve0, XH_TM, XH_FM)

            if L == 1:
                def g_wl(eng):
                    for lc in range(2):
                        fetch(eng, WS(lc * 2048, ND * 128).rearrange(
                            "p (i c) -> p i c", i=ND), io["a1_wl"][lc], f"wq{lc}")
                sec("gpsimd", g_wl)

                seL0 = semc["se"]

                def t_lat(eng, seL0=seL0):
                    fwait(eng, "wq0", "wq1")
                    wait(eng, "ve")
                    for lc in range(2):
                        for sc in range(4):
                            k = lc * 4 + sc
                            bank = ROT4[k % 4]
                            if k < 4:
                                pm_produce(eng, bank)
                            else:
                                eng.wait_ge(semh["se"], seL0 + k - 3)
                            i_ = None
                            for ic in range(ND):
                                i_ = eng.matmul(
                                    PS[bank],
                                    WS(lc * 2048, ND * 128).rearrange(
                                        "p (i c) -> p i c", i=ND)[:, ic, :],
                                    XH_FM(ic)[:, sc * 512:sc * 512 + 512],
                                    start=(ic == 0), stop=(ic == ND - 1))
                            inc(i_, "te", 1)

                def s_lat(eng):
                    base = semc["te"] - 8
                    for k in range(8):
                        lc, sc = divmod(k, 4)
                        bank = ROT4[k % 4]
                        eng.wait_ge(semh["te"], base + k + 1)
                        i_ = eng.activation(
                            LArow(lc)[:, sc * 512:sc * 512 + 512], PS[bank],
                            AF.Identity, bias=bqk[:, 3 * ND + lc:3 * ND + lc + 1],
                            scale=1.0)
                        inc(i_, "se", 1)
                        pm_consumed(bank, "se")

                sec("tensor", t_lat)
                freed["wq0"] = ("te", semc["te"])
                freed["wq1"] = ("te", semc["te"])
                sec("scalar", s_lat)

            for c in range(NP):
                buf = c % 2
                nkv = ND if L == 0 else 2
                kvsrc = XH_FM if L == 0 else LArow

                def g_w(eng, c=c, buf=buf, nkv=nkv):
                    fetch(eng, WS(buf * 2048, ND * 128).rearrange(
                        "p (i c) -> p i c", i=ND), io[pfx + "wq"][c], f"wq{buf}")
                    fetch(eng, WS(4096 + buf * 2048, nkv * 128).rearrange(
                        "p (i c) -> p i c", i=nkv), io[pfx + "wk"][c], f"wk{buf}")
                    fetch(eng, WS(4 * 2048 + buf * 2048, nkv * 128).rearrange(
                        "p (i c) -> p i c", i=nkv), io[pfx + "wv"][c], f"wv{buf}")
                sec("gpsimd", g_w)

                seQ0 = semc["se"]

                def t_qk(eng, c=c, buf=buf, nkv=nkv, kvsrc=kvsrc, seQ0=seQ0):
                    fwait(eng, f"wq{buf}", f"wk{buf}")
                    if c == 0:
                        wait(eng, "ve")
                        if L == 1:
                            wait(eng, "se")
                    for which in range(2):
                        nic = ND if which == 0 else nkv
                        srcf = XH_FM if which == 0 else kvsrc
                        woff = buf * 2048 if which == 0 else 4096 + buf * 2048
                        for sc in range(4):
                            k = which * 4 + sc
                            bank = ROT4[k % 4]
                            if k < 4:
                                pm_produce(eng, bank)
                            else:
                                eng.wait_ge(semh["se"], seQ0 + k - 3)
                            i_ = None
                            for ic in range(nic):
                                i_ = eng.matmul(
                                    PS[bank],
                                    WS(woff, nic * 128).rearrange(
                                        "p (i c) -> p i c", i=nic)[:, ic, :],
                                    srcf(ic)[:, sc * 512:sc * 512 + 512],
                                    start=(ic == 0), stop=(ic == nic - 1))
                            inc(i_, "te", 1)

                def s_qk(eng, c=c, buf=buf):
                    wait(eng, "pp")
                    base = semc["te"] - 8
                    for k in range(8):
                        which, sc = divmod(k, 4)
                        bank = ROT4[k % 4]
                        dst = QSL(buf) if which == 0 else KSL(buf)
                        eng.wait_ge(semh["te"], base + k + 1)
                        i_ = eng.activation(
                            dst[:, sc * 512:sc * 512 + 512], PS[bank], AF.Identity,
                            bias=bqk[:, which * ND + c:which * ND + c + 1],
                            scale=1.0)
                        inc(i_, "se", 1)
                        pm_consumed(bank, "se")

                sec("tensor", t_qk)
                freed[f"wq{buf}"] = ("te", semc["te"])
                freed[f"wk{buf}"] = ("te", semc["te"])
                sec("scalar", s_qk)

                def s_sq_q(eng, buf=buf):
                    wait(eng, "se")
                    i_ = eng.activation(QKSQ, QSL(buf), AF.Square)
                    inc(i_, "se", 1)
                sec("scalar", s_sq_q)

                def t_nq_fix(eng):
                    wait(eng, "se")
                    for bank in ("A0", "A1", "B0", "B1"):
                        pm_produce(eng, bank)
                    i_ = None
                    for sc in range(4):
                        dst = psA if sc < 2 else psB
                        i_ = eng.matmul(dst[0:2, (sc % 2) * 512:(sc % 2) * 512 + 512],
                                        ones2c[:], QKSQ[:, sc * 512:sc * 512 + 512],
                                        start=True, stop=True)
                    inc(i_, "te", 1)

                def v_nq(eng, which=0):
                    wait(eng, "te")
                    eng.reduce_max(small[0:2, S_TMP + which:S_TMP + which + 1],
                                   psA[0:2, :], axis=AX.X)
                    eng.drain()
                    i_ = eng.reduce_max(small[0:2, S_TMP + 2 + which:S_TMP + 3 + which],
                                        psB[0:2, :], axis=AX.X)
                    eng.drain()
                    i_ = eng.tensor_tensor(
                        small[0:2, S_TMP + which:S_TMP + which + 1],
                        small[0:2, S_TMP + which:S_TMP + which + 1],
                        small[0:2, S_TMP + 2 + which:S_TMP + 3 + which], ALU.max)
                    inc(i_, "ve", 1)
                    for bank in ("A0", "A1", "B0", "B1"):
                        pm_consumed(bank, "ve")

                def s_sq_k(eng, buf=buf):
                    wait(eng, "ve")
                    i_ = eng.activation(QKSQ, KSL(buf), AF.Square)
                    inc(i_, "se", 1)

                sec("tensor", t_nq_fix)
                sec("vector", lambda eng: v_nq(eng, 0))
                sec("scalar", s_sq_k)
                sec("tensor", t_nq_fix)
                sec("vector", lambda eng: v_nq(eng, 1))

                def v_bh(eng):
                    wait(eng, "ve")
                    i_ = eng.tensor_tensor(small[0:2, S_TMP + 4:S_TMP + 5],
                                           small[0:2, S_TMP:S_TMP + 1],
                                           small[0:2, S_TMP + 1:S_TMP + 2],
                                           ALU.mult)
                    inc(i_, "ve", 1)

                def s_bh(eng):
                    wait(eng, "ve")
                    i_ = eng.activation(small[0:2, S_TMP + 5:S_TMP + 6],
                                        small[0:2, S_TMP + 4:S_TMP + 5], AF.Sqrt)
                    inc(i_, "se", 1)

                def v_bh2(eng):
                    wait(eng, "se")
                    i_ = eng.tensor_scalar(small[0:2, S_TMP + 6:S_TMP + 7],
                                           small[0:2, S_TMP + 5:S_TMP + 6],
                                           -float(SCALE), None, ALU.mult)
                    inc(i_, "ve", 1)

                sec("vector", v_bh)
                sec("scalar", s_bh)
                sec("vector", v_bh2)
                vev = semc["ve"]

                def d_bh(eng, c=c, vev=vev):
                    eng.wait_ge(semh["ve"], vev)
                    i_ = eng.dma_start(out=bcb[2 * c:2 * c + 2],
                                       in_=small[0:2, S_TMP + 6:S_TMP + 7].squeeze(1))
                    inc(i_, "bb", 16)
                    wait(eng, "bb")
                    i_ = eng.dma_start(
                        out=bh2[:], in_=bcb[2 * c:2 * c + 2].unsqueeze(0)
                        .broadcast_to([128, 2]))
                    inc(i_, "bb", 16)
                sec("sync", d_bh)
                sem_of("bb")
                bh_dxv = semc["bb"]

                def v_vone(eng):
                    wait(eng, "te")
                    i_ = eng.memset(ar4.bf16(0, 16 * 136), 1.0)
                    inc(i_, "ve", 1)
                sec("vector", v_vone)
                vsetv = semc["ve"]

                veV0 = semc["ve"]

                def t_v(eng, c=c, buf=buf, nkv=nkv, kvsrc=kvsrc, vsetv=vsetv,
                        veV0=veV0):
                    fwait(eng, f"wv{buf}")
                    eng.wait_ge(semh["ve"], vsetv)
                    for tc in range(NT):
                        bank = ["av", "ax"][tc % 2]
                        if tc < 2:
                            pm_produce(eng, bank)
                        else:
                            eng.wait_ge(semh["ve"], veV0 + tc - 1)
                        i_ = None
                        for ic in range(nkv):
                            i_ = eng.matmul(
                                PS[bank][:, 0:128],
                                kvsrc(ic)[:, tc * 128:tc * 128 + 128],
                                WS(4 * 2048 + buf * 2048, nkv * 128).rearrange(
                                    "p (i c) -> p i c", i=nkv)[:, ic, :],
                                start=(ic == 0), stop=(ic == nkv - 1))
                        inc(i_, "te", 1)

                def v_vev(eng, c=c):
                    base = semc["te"] - NT
                    for tc in range(NT):
                        bank = ["av", "ax"][tc % 2]
                        eng.wait_ge(semh["te"], base + tc + 1)
                        eng.tensor_copy(VAUG(tc)[:, 0:64], PS[bank][:, 0:64])
                        eng.drain()
                        i_ = eng.tensor_copy(VAUG(tc)[:, 65:129],
                                             PS[bank][:, 64:128])
                        inc(i_, "ve", 1)
                        pm_consumed(bank, "ve")

                sec("tensor", t_v)
                freed[f"wv{buf}"] = ("te", semc["te"])
                sec("vector", v_vev)
                vaug_ve = semc["ve"]

                for h in range(2):
                    hr = slice(64 * h, 64 * h + 64)
                    for qc in range(4):
                        nkc = 4 * qc + 4 if causal else NT
                        ngrp = nkc // 2
                        pv_cnt = 0
                        exp_ses = []
                        mask_ves = []

                        def t_s(eng, c=c, buf=buf, h=h, hr=hr, qc=qc, nkc=nkc,
                                ngrp=ngrp, bh_dxv=bh_dxv, vaug_ve=vaug_ve):
                            eng.wait_ge(semh["ve"], vaug_ve)
                            eng.wait_ge(semh["bb"], bh_dxv)
                            pm_produce(eng, "av")

                            def emit_S(g):
                                gb = "A" if g % 2 == 0 else "B"
                                if g < 2:
                                    pm_produce(eng, gb + "0")
                                    pm_produce(eng, gb + "1")
                                else:
                                    eng.wait_ge(semh["se"], exp_ses[g - 2])
                                ps_full = psA if g % 2 == 0 else psB
                                i_ = None
                                for j in range(2):
                                    kc = 2 * g + j
                                    i_ = eng.matmul(
                                        ps_full[:, j * 512:j * 512 + 512],
                                        KSL(buf)[hr, kc * 128:kc * 128 + 128],
                                        QSL(buf)[hr, qc * 512:qc * 512 + 512],
                                        start=True, stop=True)
                                inc(i_, "te", 1)

                            emit_S(0)
                            for g in range(ngrp):
                                if g + 1 < ngrp:
                                    emit_S(g + 1)
                                eng.wait_ge(semh["se"], exp_ses[g])
                                if mask_ves[g]:
                                    eng.wait_ge(semh["ve"], mask_ves[g])
                                i_ = None
                                for j in range(2):
                                    kc = 2 * g + j
                                    i_ = eng.matmul(
                                        pav[0:65, :],
                                        VAUG(kc)[:, 65 * h:65 * h + 65],
                                        PP(g % 2)[:, j * 512:j * 512 + 512],
                                        start=(kc == 0), stop=(kc == nkc - 1))
                                inc(i_, "te", 1)

                        te0 = semc["te"]
                        se0 = semc["se"]
                        ve0_ = semc["ve"]
                        def s_exp2(eng, c=c, h=h, qc=qc, ngrp=ngrp, te0=te0):
                            for g in range(ngrp):
                                eng.wait_ge(semh["te"],
                                            te0 + (1 if g == 0 else 2 * g))
                                ps_full = psA if g % 2 == 0 else psB
                                i_ = eng.activation(PP(g % 2), ps_full[:],
                                                    AF.Exp,
                                                    bias=bh2[:, h:h + 1],
                                                    scale=cst[:, 1:2])
                                inc(i_, "se", 1)
                        def v_mask(eng, qc=qc, ngrp=ngrp, se0=se0, causal=causal):
                            if not causal:
                                return
                            for gi, g in enumerate((ngrp - 2, ngrp - 1)):
                                eng.wait_ge(semh["se"], se0 + g + 1)
                                i_ = None
                                for j in range(2):
                                    m = 2 * gi + j
                                    i_ = eng.tensor_tensor(
                                        PP(g % 2)[:, j * 512:j * 512 + 512],
                                        PP(g % 2)[:, j * 512:j * 512 + 512],
                                        MASKS(m), ALU.mult)
                                inc(i_, "ve", 1)
                        sec("scalar", s_exp2)
                        sec("vector", v_mask)
                        exp_ses.extend(se0 + g + 1 for g in range(ngrp))
                        mask_ves.extend([0] * ngrp)
                        if causal:
                            mask_ves[ngrp - 2] = ve0_ + 1
                            mask_ves[ngrp - 1] = ve0_ + 2
                        sec("tensor", t_s)
                        for gb, gg in (("A", ngrp - 2 if ngrp >= 2 else 0),
                                       ("B", ngrp - 1)):
                            if gg >= 0 and gg < ngrp:
                                tgt = ("A" if gg % 2 == 0 else "B")
                                freed[tgt + "0"] = ("se", exp_ses[gg])
                                freed[tgt + "1"] = ("se", exp_ses[gg])

                        def v_av(eng, c=c, h=h, qc=qc):
                            wait(eng, "te")
                            if h == 1:
                                wait(eng, "zr")
                            eng.reciprocal(ar2.f32(48 * KB + qc * 2048, 512,
                                                   parts=slice(64, 65)),
                                           pav[64:65, :])
                            eng.drain()
                            if h == 0:
                                i_ = eng.tensor_copy(
                                    AV_FM(c)[0:64, qc * 512:qc * 512 + 512],
                                    pav[0:64, :])
                            else:
                                i_ = eng.tensor_copy(
                                    BTMP[:, qc * 512:qc * 512 + 512],
                                    pav[0:64, :])
                            inc(i_, "ve", 1)
                            pm_consumed("av", "ve")
                        sec("vector", v_av)

                        def d_z(eng, c=c, h=h, qc=qc):
                            wait(eng, "ve")
                            i_ = eng.dma_start(
                                out=bcz[2 * c + h, qc * 512:qc * 512 + 512]
                                .unsqueeze(0),
                                in_=ar2.f32(48 * KB + qc * 2048, 512,
                                            parts=slice(64, 65)))
                            inc(i_, "zr", 16)
                        sec("sync", d_z)

                def d_shift(eng, c=c):
                    wait(eng, "ve")
                    wait(eng, "zr")
                    i_ = eng.dma_start(out=ar3.bf16(c * 4096, T,
                                                    parts=slice(64, 128)),
                                       in_=BTMP[:])
                    inc(i_, "sh", 16)
                    i_ = eng.dma_start(
                        out=ar4.f32(12800, 2048, parts=slice(0, 64)),
                        in_=bcz[2 * c].unsqueeze(0).broadcast_to([64, T]))
                    inc(i_, "sh", 16)
                    i_ = eng.dma_start(
                        out=ar4.f32(12800, 2048, parts=slice(64, 128)),
                        in_=bcz[2 * c + 1].unsqueeze(0).broadcast_to([64, T]))
                    inc(i_, "sh", 16)
                sec("sync", d_shift)
                sem_of("sh")
                dxv2 = semc["sh"]

                def v_norm(eng, c=c, dxv2=dxv2):
                    eng.wait_ge(semh["sh"], dxv2)
                    eng.tensor_tensor(AV_FM(c), AV_FM(c), ZBC, ALU.mult)
                    eng.drain()
                    i_ = eng.tensor_scalar(AV_FM(c), AV_FM(c),
                                           bqk[:, 2 * ND + c:2 * ND + c + 1],
                                           None, ALU.add)
                    inc(i_, "ve", 1)
                sec("vector", v_norm)

            set_wstbar()

            def s_bobc(eng):
                wait(eng, "ve")
                wait(eng, "te")
                wait(eng, "sh")
                i_ = eng.dma_start(out=BOBC, in_=io[pfx + "bo"].unsqueeze(0)
                                   .broadcast_to([128, D]))
                inc(i_, "bc", 16)
            sec("sync", s_bobc)
            sem_of("bc")
            bobc_dxv = semc["bc"]

            for oq in range(4):
                def g_wo(eng, oq=oq):
                    fetch(eng, WS((oq % 2) * 4096, ND * 256).rearrange(
                        "p (i c) -> p i c", i=ND), io[pfx + "wo"][oq],
                        f"wo{oq % 2}")
                sec("gpsimd", g_wo)

                veO0 = semc["ve"]

                def t_o(eng, oq=oq, veO0=veO0):
                    fwait(eng, f"wo{oq % 2}")
                    if oq == 0:
                        wait(eng, "ve")
                    for tc in range(NT):
                        bank = ROT4[tc % 4]
                        if tc < 4:
                            pm_produce(eng, bank)
                        else:
                            eng.wait_ge(semh["ve"], veO0 + tc - 3)
                        i_ = None
                        for dc in range(ND):
                            i_ = eng.matmul(
                                PS[bank][:, 0:256],
                                AV_FM(dc)[:, tc * 128:tc * 128 + 128],
                                WS((oq % 2) * 4096, ND * 256).rearrange(
                                    "p (i c) -> p i c", i=ND)[:, dc, :],
                                start=(dc == 0), stop=(dc == ND - 1))
                        inc(i_, "te", 1)

                def v_o(eng, oq=oq, bobc_dxv=bobc_dxv):
                    eng.wait_ge(semh["bc"], bobc_dxv)
                    base = semc["te"] - NT
                    for tc in range(NT):
                        bank = ROT4[tc % 4]
                        eng.wait_ge(semh["te"], base + tc + 1)
                        eng.tensor_tensor(X_T(tc)[:, oq * 256:oq * 256 + 256],
                                          X_T(tc)[:, oq * 256:oq * 256 + 256],
                                          PS[bank][:, 0:256], ALU.add)
                        eng.drain()
                        i_ = eng.tensor_tensor(
                            X_T(tc)[:, oq * 256:oq * 256 + 256],
                            X_T(tc)[:, oq * 256:oq * 256 + 256],
                            BOBC[:, oq * 256:oq * 256 + 256], ALU.add)
                        inc(i_, "ve", 1)
                        pm_consumed(bank, "ve")

                sec("tensor", t_o)
                freed[f"wo{oq % 2}"] = ("te", semc["te"])
                sec("vector", v_o)

        def emit_ffn():
            ve0 = emit_ln(ar2.f32(0, D))
            lnve = semc["ve"]

            def d_spill(eng, lnve=lnve):
                eng.wait_ge(semh["ve"], lnve)
                wait(eng, "se")
                i_ = eng.dma_start(
                    out=xres.rearrange("(i p) d -> p i d", p=128),
                    in_=ar1.f32(0, NT * D).rearrange("p (i d) -> p i d", i=NT))
                inc(i_, "xl", 16)
            sec("sync", d_spill)
            sem_of("xl")
            spill_dxv = semc["xl"]

            for half in range(2):
                hb = half * 8
                set_wstbar()
                veT0 = semc["ve"]

                def t_trh(eng, half=half, hb=hb, spill_dxv=spill_dxv, veT0=veT0):
                    if half == 0:
                        eng.wait_ge(semh["xl"], spill_dxv)
                    else:
                        wait(eng, "se")
                    for ti in range(8):
                        eng.wait_ge(semh["ve"], veT0 + ti)
                        i_ = None
                        for dc in range(ND):
                            i_ = eng.transpose(
                                ptr[:, dc * 128:dc * 128 + 128],
                                XH_TM(hb + ti)[:, dc * 128:dc * 128 + 128],
                                ident[:])
                        inc(i_, "te", 1)

                def v_trh(eng, half=half):
                    base = semc["te"] - 8
                    for ti in range(8):
                        eng.wait_ge(semh["te"], base + ti + 1)
                        i_ = None
                        for dc in range(ND):
                            i_ = eng.tensor_copy(
                                XHF_H(dc)[:, ti * 128:ti * 128 + 128],
                                ptr[:, dc * 128:dc * 128 + 128])
                        inc(i_, "ve", 1)

                sec("tensor", t_trh)
                sec("vector", v_trh)
                trve = semc["ve"]

                for oc in range(NDH):
                    def g_w12(eng, oc=oc):
                        fetch(eng, WS((oc % 2) * 2048, ND * 128).rearrange(
                            "p (i c) -> p i c", i=ND), io["f0_w1t"][oc],
                            f"w1{oc % 2}")
                        fetch(eng, WS(4096 + (oc % 2) * 2048, ND * 128).rearrange(
                            "p (i c) -> p i c", i=ND), io["f0_w2t"][oc],
                            f"w2{oc % 2}")
                    sec("gpsimd", g_w12)

                    def t_12(eng, oc=oc, trve=trve):
                        fwait(eng, f"w1{oc % 2}", f"w2{oc % 2}")
                        if oc == 0:
                            eng.wait_ge(semh["ve"], trve)
                        for sc in range(2):
                            for w in range(2):
                                bank = ROT4[2 * sc + w]
                                pm_produce(eng, bank)
                                i_ = None
                                for ic in range(ND):
                                    i_ = eng.matmul(
                                        PS[bank],
                                        WS(w * 4096 + (oc % 2) * 2048,
                                           ND * 128).rearrange(
                                            "p (i c) -> p i c", i=ND)[:, ic, :],
                                        XHF_H(ic)[:, sc * 512:sc * 512 + 512],
                                        start=(ic == 0), stop=(ic == ND - 1))
                                inc(i_, "te", 1)

                    def s_sil(eng, oc=oc):
                        if oc == 0:
                            wait(eng, "pp")
                        base = semc["te"] - 4
                        for sc in range(2):
                            eng.wait_ge(semh["te"], base + 2 * sc + 1)
                            i_ = eng.activation(
                                SILT(sc), PS[ROT4[2 * sc]], AF.Silu,
                                scale=f0s[:, oc:oc + 1])
                            inc(i_, "se", 1)

                    def v_h(eng, oc=oc, half=half):
                        base_t = semc["te"] - 4
                        base_s = semc["se"] - 2
                        for sc in range(2):
                            eng.wait_ge(semh["te"], base_t + 2 * sc + 2)
                            eng.wait_ge(semh["se"], base_s + sc + 1)
                            i_ = eng.scalar_tensor_tensor(
                                H_T(oc)[:, sc * 512:sc * 512 + 512],
                                PS[ROT4[2 * sc + 1]],
                                f0s[:, NDH + oc:NDH + oc + 1], SILT(sc),
                                ALU.mult, ALU.mult)
                            inc(i_, "ve", 1)
                            pm_consumed(ROT4[2 * sc], "ve")
                            pm_consumed(ROT4[2 * sc + 1], "ve")

                    sec("tensor", t_12)
                    freed[f"w1{oc % 2}"] = ("te", semc["te"])
                    freed[f"w2{oc % 2}"] = ("te", semc["te"])
                    sec("scalar", s_sil)
                    sec("vector", v_h)

                set_wstbar()
                for stage in range(2):
                    wname = "f0_h1t" if stage == 0 else "f0_h2t"
                    soff = 2 * NDH if stage == 0 else 3 * NDH
                    srcT = H_T if stage == 0 else G_T
                    dstT = G_T if stage == 0 else H_T
                    for oc in range(NDH):
                        def g_wh(eng, oc=oc, wname=wname):
                            fetch(eng, WS((oc % 2) * 8192, NDH * 128).rearrange(
                                "p (i c) -> p i c", i=NDH), io[wname][oc],
                                f"wh{oc % 2}")
                        sec("gpsimd", g_wh)

                        def t_h(eng, oc=oc, srcT=srcT):
                            fwait(eng, f"wh{oc % 2}")
                            if oc == 0:
                                wait(eng, "ve")
                                wait(eng, "se")
                            for sc in range(2):
                                bank = ROT4[(2 * oc + sc) % 4]
                                pm_produce(eng, bank)
                                i_ = None
                                for ic in range(NDH):
                                    i_ = eng.matmul(
                                        PS[bank],
                                        WS((oc % 2) * 8192, NDH * 128).rearrange(
                                            "p (i c) -> p i c", i=NDH)[:, ic, :],
                                        srcT(ic)[:, sc * 512:sc * 512 + 512],
                                        start=(ic == 0), stop=(ic == NDH - 1))
                                inc(i_, "te", 1)

                        def s_g(eng, oc=oc, soff=soff, dstT=dstT):
                            base = semc["te"] - 2
                            for sc in range(2):
                                bank = ROT4[(2 * oc + sc) % 4]
                                eng.wait_ge(semh["te"], base + sc + 1)
                                i_ = eng.activation(
                                    dstT(oc)[:, sc * 512:sc * 512 + 512],
                                    PS[bank], AF.Gelu,
                                    scale=f0s[:, soff + oc:soff + oc + 1])
                                inc(i_, "se", 1)
                                pm_consumed(bank, "se")

                        sec("tensor", t_h)
                        freed[f"wh{oc % 2}"] = ("te", semc["te"])
                        sec("scalar", s_g)

                set_wstbar()
                seW0 = semc["se"]
                for op_ in range(4):
                    def g_w3(eng, op_=op_):
                        for k in range(2):
                            fetch(eng, WS(k * 8192, NDH * 128)
                                  .rearrange("p (i c) -> p i c", i=NDH),
                                  io["f0_w3t"][2 * op_ + k], f"w3{k}")
                    sec("gpsimd", g_w3)

                    for k in range(2):
                        ob = 2 * op_ + k
                        for sc in range(2):
                            bank = ROT4[(2 * ob + sc) % 4]

                            def t_3(eng, k=k, sc=sc, op_=op_, bank=bank):
                                fwait(eng, "w30", "w31")
                                if k == 0 and sc == 0 and op_ == 0:
                                    wait(eng, "se")
                                pm_produce(eng, bank)
                                i_ = None
                                for hc in range(NDH):
                                    i_ = eng.matmul(
                                        PS[bank],
                                        WS(k * 8192, NDH * 128).rearrange(
                                            "p (i c) -> p i c", i=NDH)[:, hc, :],
                                        H_T(hc)[:, sc * 512:sc * 512 + 512],
                                        start=(hc == 0), stop=(hc == NDH - 1))
                                inc(i_, "te", 1)

                            def s_3(eng, ob=ob, sc=sc, bank=bank):
                                wait(eng, "te")
                                i_ = eng.activation(
                                    Y_FMF(ob)[:, sc * 512:sc * 512 + 512],
                                    PS[bank], AF.Identity,
                                    scale=f0s[:, 4 * NDH + ob:4 * NDH + ob + 1])
                                inc(i_, "se", 1)
                                pm_consumed(bank, "se")

                            sec("tensor", t_3)
                            sec("scalar", s_3)
                    freed["w30"] = ("te", semc["te"])
                    freed["w31"] = ("te", semc["te"])

                veW0 = semc["ve"]
                TRB = [("av", "ax"), ("A0", "A1")]
                for tb in range(8):
                    tc = hb + tb

                    def d_xin(eng, tc=tc, tb=tb):
                        wait(eng, "xl")
                        wait(eng, f"xo{tb % 2}")
                        i_ = eng.dma_start(
                            out=XIO_F(tb % 2)[:],
                            in_=xres[tc * 128:tc * 128 + 128, :])
                        inc(i_, f"xi{tb % 2}", 16)
                    sec("sync", d_xin)
                    sem_of(f"xi{tb % 2}")
                    dxv = semc[f"xi{tb % 2}"]

                    b0n, b1n = TRB[tb % 2]

                    def t_w3tr(eng, tb=tb, b0n=b0n, b1n=b1n, veW0=veW0,
                               seW0=seW0):
                        if tb == 0:
                            eng.wait_ge(semh["se"], seW0 + 16)
                            pm_produce(eng, "av")
                            pm_produce(eng, "ax")
                        if tb == 1:
                            pm_produce(eng, "A0")
                            pm_produce(eng, "A1")
                        if tb >= 2:
                            eng.wait_ge(semh["ve"], veW0 + tb - 1)
                        i_ = None
                        for ob in range(8):
                            dst = PS[b0n] if ob < 4 else PS[b1n]
                            i_ = eng.transpose(
                                dst[:, (ob % 4) * 128:(ob % 4) * 128 + 128],
                                Y_FMF(ob)[:, tb * 128:tb * 128 + 128],
                                identf[:])
                        inc(i_, "te", 1)

                    def v_3n(eng, tb=tb, dxv=dxv, b0n=b0n, b1n=b1n):
                        eng.wait_ge(semh[f"xi{tb % 2}"], dxv)
                        wait(eng, "te")
                        eng.tensor_tensor(XIO_F(tb % 2)[:, 0:512],
                                          XIO_F(tb % 2)[:, 0:512],
                                          PS[b0n], ALU.add)
                        i_ = eng.tensor_tensor(XIO_F(tb % 2)[:, 512:1024],
                                               XIO_F(tb % 2)[:, 512:1024],
                                               PS[b1n], ALU.add)
                        inc(i_, "ve", 1)

                    def d_xout(eng, tc=tc, tb=tb):
                        wait(eng, "ve")
                        i_ = eng.dma_start(
                            out=xres[tc * 128:tc * 128 + 128, :],
                            in_=XIO_F(tb % 2)[:])
                        inc(i_, f"xo{tb % 2}", 16)

                    sec("tensor", t_w3tr)
                    sec("vector", v_3n)
                    sec("sync", d_xout)
                freed["av"] = ("ve", semc["ve"])
                freed["ax"] = ("ve", semc["ve"])
                freed["A0"] = ("ve", semc["ve"])
                freed["A1"] = ("ve", semc["ve"])

            def d_xre(eng):
                wait(eng, "te")
                wait(eng, "xo0")
                wait(eng, "xo1")
                i_ = eng.dma_start(
                    out=ar1.f32(0, NT * D).rearrange("p (i d) -> p i d", i=NT),
                    in_=xres.rearrange("(i p) d -> p i d", p=128))
                inc(i_, "xl", 16)
            sec("sync", d_xre)

        def emit_moe():
            set_wstbar()
            ve0 = emit_ln(ar2.f32(0, D))
            lnve = semc["ve"]

            def d_spill4(eng, lnve=lnve):
                eng.wait_ge(semh["ve"], lnve)
                wait(eng, "se")
                i_ = eng.dma_start(
                    out=xres.rearrange("(i p) d -> p i d", p=128),
                    in_=ar1.f32(0, NT * D).rearrange("p (i d) -> p i d", i=NT))
                inc(i_, "xl", 16)
                i_ = eng.dma_start(
                    out=xag_in.rearrange("(i p) d -> p i d", p=128),
                    in_=ar3.bf16(0, NT * D).rearrange("p (i d) -> p i d", i=NT))
                inc(i_, "xg", 16)
            sec("sync", d_spill4)
            sem_of("xl"); sem_of("xg")
            spillv = semc["xl"]
            xgv = semc["xg"]

            def g_agx(eng, xgv=xgv):
                eng.wait_ge(semh["xg"], xgv)
                i_ = eng.collective_compute(
                    "AllGather", ALU.bypass, replica_groups=[list(range(8))],
                    ins=[xag_in_t.ap().opt()], outs=[xall_t.ap().opt()])
                inc(i_, "ag", 1)
            sec("gpsimd", g_agx)
            sem_of("ag")
            agx_v = semc["ag"]

            veE0 = semc["ve"]

            def t_tr4(eng, veE0=veE0, lnve=lnve):
                for ti in range(NT):
                    eng.wait_ge(semh["ve"], lnve - NT + ti + 1)
                    eng.wait_ge(semh["ve"], veE0 + ti)
                    i_ = None
                    for dc in range(ND):
                        i_ = eng.transpose(
                            ptr[:, dc * 128:dc * 128 + 128],
                            XH_TM(ti)[:, dc * 128:dc * 128 + 128], ident[:])
                    inc(i_, "te", 1)

            def v_ev4(eng, spillv=spillv, xgv=xgv):
                base = semc["te"] - NT
                for ti in range(NT):
                    if ti == 0:
                        eng.wait_ge(semh["xl"], spillv)
                        eng.wait_ge(semh["xg"], xgv)
                    eng.wait_ge(semh["te"], base + ti + 1)
                    i_ = None
                    for dc in range(ND):
                        i_ = eng.tensor_copy(
                            XHM_FM(dc)[:, ti * 128:ti * 128 + 128],
                            ptr[:, dc * 128:dc * 128 + 128])
                    inc(i_, "ve", 1)

            sec("tensor", t_tr4)
            sec("vector", v_ev4)
            xhfm_ve = semc["ve"]

            def g_rwt(eng):
                fetch(eng, wst.bf16(12 * KB, ND * 8).rearrange(
                    "p (i c) -> p i c", i=ND),
                    io["m_rwt"].rearrange("(i p) e -> p i e", p=128), "rwt")
            sec("gpsimd", g_rwt)

            veR0 = semc["ve"]
            seR0 = semc["se"]

            def t_rt(eng, xhfm_ve=xhfm_ve, veR0=veR0):
                fwait(eng, "rwt")
                eng.wait_ge(semh["ve"], xhfm_ve)
                for tc in range(NT):
                    bank = ["ax", "ex"][tc % 2]
                    if tc < 2:
                        pm_produce(eng, bank)
                    else:
                        eng.wait_ge(semh["ve"], veR0 + (tc - 2) * 5 + 1)
                    i_ = None
                    for dc in range(ND):
                        i_ = eng.matmul(
                            PS[bank][:, 0:8],
                            XHM_FM(dc)[:, tc * 128:tc * 128 + 128],
                            wst.bf16(12 * KB, ND * 8).rearrange(
                                "p (i c) -> p i c", i=ND)[:, dc, :],
                            start=(dc == 0), stop=(dc == ND - 1))
                    inc(i_, "te", 1)

            def v_sm1(eng, veR0=veR0, seR0=seR0):
                base = semc["te"] - NT
                for tc in range(NT):
                    bank = ["ax", "ex"][tc % 2]
                    eng.wait_ge(semh["te"], base + tc + 1)
                    if tc:
                        eng.wait_ge(semh["se"], seR0 + tc)
                    eng.tensor_copy(sm2[:, C_LG:C_LG + 8], PS[bank][:, 0:8])
                    eng.drain()
                    i_ = eng.reduce_max(small[:, S_NMAX:S_NMAX + 1],
                                        sm2[:, C_LG:C_LG + 8], axis=AX.X,
                                        negate=True)
                    inc(i_, "ve", 1)
                    pm_consumed(bank, "ve")
                    eng.wait_ge(semh["se"], seR0 + tc + 1)
                    eng.reciprocal(small[:, S_PSTAR + tc:S_PSTAR + tc + 1],
                                   sm2[:, C_Z8:C_Z8 + 1])
                    eng.drain()
                    eng.max(sm2[:, C_MX8:C_MX8 + 8], sm2[:, C_LG:C_LG + 8])
                    eng.drain()
                    eng.max_index(smallu[:, 0:8], sm2[:, C_MX8:C_MX8 + 8],
                                  sm2[:, C_LG:C_LG + 8])
                    eng.drain()
                    i_ = eng.tensor_copy(
                        small[:, S_ESTAR + tc:S_ESTAR + tc + 1],
                        smallu[:, 0:1])
                    inc(i_, "ve", 4)

            def s_sm(eng, veR0=veR0):
                for tc in range(NT):
                    eng.wait_ge(semh["ve"], veR0 + tc * 5 + 1)
                    i_ = eng.activation(sm2[:, C_PROB:C_PROB + 8],
                                        sm2[:, C_LG:C_LG + 8], AF.Exp,
                                        bias=small[:, S_NMAX:S_NMAX + 1],
                                        accum_out=sm2[:, C_Z8:C_Z8 + 1])
                    inc(i_, "se", 1)

            sec("tensor", t_rt)
            sec("scalar", s_sm)
            sec("vector", v_sm1)
            argve = semc["ve"]

            def t_er(eng, argve=argve):
                eng.wait_ge(semh["ve"], argve)
                pm_produce(eng, "ax")
                i_ = eng.transpose(pax[0:16, 0:128],
                                   small[:, S_ESTAR:S_ESTAR + 16], identf[:])
                inc(i_, "te", 1)
            sec("tensor", t_er)

            def v_er(eng):
                wait(eng, "te")
                i_ = eng.tensor_copy(er16[:, 0:128], pax[0:16, 0:128])
                inc(i_, "ve", 1)
                pm_consumed("ax", "ve")
            sec("vector", v_er)
            erve = semc["ve"]

            def d_er(eng, erve=erve):
                eng.wait_ge(semh["ve"], erve)
                i_ = eng.dma_start(
                    out=bcm[0:1, :].rearrange("o (t p) -> o t p", p=128),
                    in_=er16[:, 0:128])
                inc(i_, "mo", 16)
                wait(eng, "mo")
                i_ = eng.dma_start(out=EROW8,
                                   in_=bcm[0:1, :].broadcast_to([8, T]))
                inc(i_, "mo", 16)
            sec("sync", d_er)
            sem_of("mo")
            erow_mo = semc["mo"]

            def v_oh(eng, erow_mo=erow_mo):
                eng.wait_ge(semh["mo"], erow_mo)
                wait(eng, "se")
                eng.tensor_scalar(ONEHOT, EROW8, ciota[0:8, 0:1], None,
                                  ALU.is_equal)
                eng.drain()
                eng.tensor_tensor_scan(CNTI, ONEHOT, ONEHOT, 0.0, ALU.add,
                                       ALU.max)
                eng.drain()
                i_ = eng.tensor_tensor(TMP8, ONEHOT, CNTI, ALU.mult)
                inc(i_, "ve", 1)
            sec("vector", v_oh)
            scanve = semc["ve"]

            def g_cc(eng, scanve=scanve):
                eng.wait_ge(semh["ve"], scanve)
                i_ = eng.dma_start(out=cc_in.ap().unsqueeze(1),
                                   in_=CNTI[:, T - 1:T])
                inc(i_, "ci", 16)
                wait(eng, "ci")
                i_ = eng.collective_compute(
                    "AllGather", ALU.bypass,
                    replica_groups=[list(range(8))],
                    ins=[cc_in.ap().opt()], outs=[cc_out.ap().opt()])
                inc(i_, "ag", 1)
                wait(eng, "ag")
                i_ = eng.dma_start(
                    out=agg[:], in_=cc_out.ap().rearrange("(r e) -> r e", e=8))
                inc(i_, "co", 16)
            sec("gpsimd", g_cc)
            sem_of("co")
            aggco = semc["co"]

            def t_off(eng, aggco=aggco):
                eng.wait_ge(semh["co"], aggco)
                wait(eng, "pp")
                pm_produce(eng, "ax")
                i_ = eng.matmul(pax[0:8, 0:1], agg[:], rmk[:],
                                start=True, stop=True)
                inc(i_, "te", 1)
                pm_produce(eng, "ex")
                i_ = eng.matmul(pex[0:1, 0:1], esel_sb[:], ciota[0:8, 0:1],
                                start=True, stop=True)
                inc(i_, "te", 1)
            sec("tensor", t_off)

            def v_off(eng):
                wait(eng, "te")
                eng.tensor_copy(off8[:], pax[0:8, 0:1])
                eng.drain()
                eng.tensor_scalar(small[0:1, S_TMP + 8:S_TMP + 9],
                                  pex[0:1, 0:1], float(8 * T), None,
                                  ALU.mult)
                eng.drain()
                i_ = eng.tensor_copy(smallu[0:1, 40:41],
                                     small[0:1, S_TMP + 8:S_TMP + 9])
                inc(i_, "ve", 1)
                pm_consumed("ax", "ve")
                pm_consumed("ex", "ve")
            sec("vector", v_off)

            veQ0 = semc["ve"]

            def t_q(eng, veQ0=veQ0):
                wait(eng, "ve")
                for q in range(4):
                    bank = ["ax", "ex"][q % 2]
                    if q < 2:
                        pm_produce(eng, bank)
                    else:
                        eng.wait_ge(semh["ve"], veQ0 + q - 1)
                    eng.matmul(PS[bank][0:1, 0:512], ones8[:],
                               TMP8[:, q * 512:q * 512 + 512],
                               start=True, stop=True)
                    i_ = eng.matmul(PS[bank][32:33, 0:512], off8[:],
                                    ONEHOT[:, q * 512:q * 512 + 512],
                                    start=True, stop=True)
                    inc(i_, "te", 1)

            def v_q(eng):
                base = semc["te"] - 4
                for q in range(4):
                    bank = ["ax", "ex"][q % 2]
                    eng.wait_ge(semh["te"], base + q + 1)
                    eng.tensor_copy(RANKR[:, q * 512:q * 512 + 512],
                                    PS[bank][0:1, 0:512])
                    eng.drain()
                    i_ = eng.tensor_copy(OFFR[:, q * 512:q * 512 + 512],
                                         PS[bank][32:33, 0:512])
                    inc(i_, "ve", 1)
                    pm_consumed(bank, "ve")
            sec("tensor", t_q)
            sec("vector", v_q)
            rowve = semc["ve"]

            def d_rc(eng, rowve=rowve):
                eng.wait_ge(semh["ve"], rowve)
                i_ = eng.dma_start(
                    out=er16[:, 0:128],
                    in_=RANKR.rearrange("o (t p) -> o t p", p=128))
                inc(i_, "mo", 16)
                i_ = eng.dma_start(
                    out=er16[:, 128:256],
                    in_=OFFR.rearrange("o (t p) -> o t p", p=128))
                inc(i_, "mo", 16)
                i_ = eng.dma_start(
                    out=small[:, S_TMP + 9:S_TMP + 10],
                    in_=io["cbase"].unsqueeze(0).broadcast_to([128, 1]))
                inc(i_, "mo", 16)
            sec("sync", d_rc)
            rc_mo = semc["mo"]

            def t_rc(eng, rc_mo=rc_mo):
                eng.wait_ge(semh["mo"], rc_mo)
                pm_produce(eng, "ax")
                eng.transpose(pax[:, 0:16], er16[0:16, 0:128],
                              identf[0:16, 0:16])
                i_ = eng.transpose(pax[:, 16:32], er16[0:16, 128:256],
                                   identf[0:16, 0:16])
                inc(i_, "te", 1)
            sec("tensor", t_rc)

            def v_rc(eng):
                wait(eng, "te")
                eng.tensor_copy(sm2[:, C_RANK:C_RANK + 16], pax[:, 0:16])
                eng.drain()
                i_ = eng.tensor_copy(sm2[:, C_OFFT:C_OFFT + 16], pax[:, 16:32])
                inc(i_, "ve", 1)
                pm_consumed("ax", "ve")
            sec("vector", v_rc)
            rc_ve = semc["ve"]

            def v_cols(eng, rc_ve=rc_ve):
                eng.wait_ge(semh["ve"], rc_ve)
                eng.tensor_scalar(sm2[:, C_RANK:C_RANK + 16],
                                  sm2[:, C_RANK:C_RANK + 16], 1.0, None,
                                  ALU.subtract)
                eng.drain()
                eng.tensor_tensor(sm2[:, C_OFFT:C_OFFT + 16],
                                  sm2[:, C_OFFT:C_OFFT + 16],
                                  sm2[:, C_RANK:C_RANK + 16], ALU.add)
                eng.drain()
                eng.tensor_scalar(sm2[:, C_KEPT:C_KEPT + 16],
                                  sm2[:, C_OFFT:C_OFFT + 16],
                                  float(CAPACITY), None, ALU.is_lt)
                eng.drain()
                eng.tensor_tensor(small[:, S_G:S_G + 16],
                                  small[:, S_PSTAR:S_PSTAR + 16],
                                  sm2[:, C_KEPT:C_KEPT + 16], ALU.mult)
                eng.drain()
                eng.tensor_scalar(sm2[:, C_SLOTP:C_SLOTP + 16],
                                  small[:, S_ESTAR:S_ESTAR + 16],
                                  float(8 * T), None, ALU.mult)
                eng.drain()
                eng.tensor_tensor(sm2[:, C_SLOTP:C_SLOTP + 16],
                                  sm2[:, C_SLOTP:C_SLOTP + 16],
                                  sm2[:, C_OFFT:C_OFFT + 16], ALU.add)
                eng.drain()
                eng.tensor_scalar(sm2[:, C_SLSH:C_SLSH + 16],
                                  sm2[:, C_OFFT:C_OFFT + 16],
                                  float(CAPACITY - 1), None, ALU.min)
                eng.drain()
                eng.tensor_scalar(sm2[:, C_KEPT:C_KEPT + 16],
                                  small[:, S_ESTAR:S_ESTAR + 16],
                                  float(CAPACITY), None, ALU.mult)
                eng.drain()
                eng.tensor_tensor(sm2[:, C_SLSH:C_SLSH + 16],
                                  sm2[:, C_SLSH:C_SLSH + 16],
                                  sm2[:, C_KEPT:C_KEPT + 16], ALU.add)
                eng.drain()
                eng.tensor_scalar(sm2[:, C_LG:C_LG + 16], ciota[:, 0:16],
                                  small[:, S_TMP + 9:S_TMP + 10], None,
                                  ALU.add)
                eng.drain()
                eng.tensor_copy(smallu[:, 8:24], sm2[:, C_SLOTP:C_SLOTP + 16])
                eng.drain()
                i_ = eng.tensor_copy(smallu[:, 24:40],
                                     sm2[:, C_SLSH:C_SLSH + 16])
                inc(i_, "ve", 1)
            sec("vector", v_cols)
            colve = semc["ve"]

            def g_idx(eng, colve=colve):
                for k in range(4):
                    i_ = eng.dma_start(
                        out=idxb[k * 32768:(k + 1) * 32768].rearrange(
                            "(p c) -> p c", p=128),
                        in_=ZEROT[:, 0:256])
                    inc(i_, "ix", 16)
                wait(eng, "ix")
                eng.wait_ge(semh["ve"], colve)
                for tc in range(NT):
                    i_ = eng.indirect_dma_start(
                        out=idxb.unsqueeze(1),
                        out_offset=bass.IndirectOffsetOnAxis(
                            smallu[:, 8 + tc:9 + tc], 0),
                        in_=sm2[:, C_LG + tc:C_LG + tc + 1], in_offset=None)
                    inc(i_, "ix", 16)
                wait(eng, "ix")
                i_ = eng.collective_compute(
                    "AllReduce", ALU.add, replica_groups=[list(range(8))],
                    ins=[idxb_t.ap().opt()], outs=[idxall_t.ap().opt()])
                inc(i_, "ag", 1)
            sec("gpsimd", g_idx)
            sem_of("ag")
            idx_ag = semc["ag"]

            def g_ld(eng, idx_ag=idx_ag):
                eng.wait_ge(semh["ag"], idx_ag)
                wait(eng, "ve")
                eng.reg_load(greg, smallu[0:1, 40:41])
                i_ = eng.dma_start(
                    out=i24[:],
                    in_=bass.AP(idxall_t, greg, [[128, 24], [1, 128]]))
                inc(i_, "il", 16)
            sec("gpsimd", g_ld)
            sem_of("il")
            il_v = semc["il"]

            def t_i24(eng, il_v=il_v):
                eng.wait_ge(semh["il"], il_v)
                pm_produce(eng, "ax")
                i_ = eng.transpose(pax[:, 0:24], i24[0:24, 0:128],
                                   identf[0:24, 0:24])
                inc(i_, "te", 1)
            sec("tensor", t_i24)

            def v_i24(eng):
                wait(eng, "te")
                i_ = eng.tensor_copy(u24[:], pax[:, 0:24])
                inc(i_, "ve", 1)
                pm_consumed("ax", "ve")
            sec("vector", v_i24)
            u24ve = semc["ve"]

            def g_gx(eng, u24ve=u24ve, agx_v=agx_v):
                eng.wait_ge(semh["ve"], u24ve)
                eng.wait_ge(semh["ag"], agx_v)
                wait(eng, "te")
                for sc in range(NCC):
                    i_ = eng.indirect_dma_start(
                        out=XB_TM(sc), out_offset=None,
                        in_=xall,
                        in_offset=bass.IndirectOffsetOnAxis(
                            u24[:, sc:sc + 1], 0))
                    inc(i_, "gx", 16)
            sec("gpsimd", g_gx)
            sem_of("gx")
            gx_all = semc["gx"]

            veB0 = semc["ve"]

            def t_trb(eng, gx_all=gx_all, veB0=veB0):
                for ti in range(NCC):
                    eng.wait_ge(semh["gx"], gx_all)
                    eng.wait_ge(semh["ve"], veB0 + ti)
                    i_ = None
                    for dc in range(ND):
                        i_ = eng.transpose(
                            ptr[:, dc * 128:dc * 128 + 128],
                            XB_TM(ti)[:, dc * 128:dc * 128 + 128], ident[:])
                    inc(i_, "te", 1)

            def v_evb(eng):
                base = semc["te"] - NCC
                for ti in range(NCC):
                    if ti == 0:
                        wait(eng, "mo")
                        wait(eng, "ix")
                    eng.wait_ge(semh["te"], base + ti + 1)
                    i_ = None
                    for dc in range(ND):
                        i_ = eng.tensor_copy(
                            XB_FM(dc)[:, ti * 128:ti * 128 + 128],
                            ptr[:, dc * 128:dc * 128 + 128])
                    inc(i_, "ve", 1)

            sec("tensor", t_trb)
            sec("vector", v_evb)
            xbve = semc["ve"]

            if DBG_MOE:
                def g_dbg(eng, xbve=xbve):
                    eng.wait_ge(semh["ve"], xbve)
                    i_ = eng.dma_start(
                        out=dbg_xb.rearrange("(s p) d -> p s d", p=128),
                        in_=ar1.bf16(0, 4 * 1024).rearrange(
                            "p (s d) -> p s d", s=4))
                    inc(i_, "dbg", 16)
                    i_ = eng.dma_start(out=dbg_u24, in_=u24[:])
                    inc(i_, "dbg", 16)
                    i_ = eng.dma_start(out=dbg_i24, in_=i24[:])
                    inc(i_, "dbg", 16)
                    i_ = eng.dma_start(
                        out=er16[0:16, 0:128],
                        in_=bass.AP(idxall_t, 0, [[128, 16], [1, 128]]))
                    inc(i_, "dbg", 16)
                    eng.reg_load(greg, smallu[0:1, 40:41])
                    i_ = eng.dma_start(
                        out=er16[0:16, 128:256],
                        in_=bass.AP(idxall_t, greg, [[128, 16], [1, 128]]))
                    inc(i_, "dbg", 16)
                    wait(eng, "dbg")
                    i_ = eng.dma_start(out=dbg_h[0:16, 0:256], in_=er16[:])
                    inc(i_, "dbg", 16)
                    i_ = eng.dma_start(out=dbg_h[100:101, 0:1],
                                       in_=smallu[0:1, 40:41])
                    inc(i_, "dbg", 16)
                    i_ = eng.dma_start(out=dbg_h[101:102, 0:1],
                                       in_=small[0:1, 91:92])
                    inc(i_, "dbg", 16)
                    wait(eng, "dbg")
                sec("gpsimd", g_dbg)

            def d_b2(eng):
                wait(eng, "ve")
                i_ = eng.dma_start(out=B2BC, in_=io["m_b2"].unsqueeze(0)
                                   .broadcast_to([128, D]))
                inc(i_, "b2", 16)
            sec("sync", d_b2)
            sem_of("b2")
            b2v = semc["b2"]

            set_wstbar()
            YB8 = ["A0", "A1", "B0", "B1", "av", "ax", "ex", "trf"]
            for grp in range(3):
                teG0 = semc["te"]
                seH0 = semc["se"]
                veG0 = semc["ve"]
                for fq in range(8):
                    def g_w1(eng, grp=grp, fq=fq):
                        fetch(eng, WS((fq % 2) * 8192, ND * 512).rearrange(
                            "p (i c) -> p i c", i=ND),
                            io["m_w1"][fq], f"mw{fq % 2}")
                    sec("gpsimd", g_w1)

                    def t_w1(eng, grp=grp, fq=fq, seH0=seH0, xbve=xbve,
                             veG0=veG0):
                        fwait(eng, f"mw{fq % 2}")
                        if grp == 0 and fq == 0:
                            eng.wait_ge(semh["ve"], xbve)
                        if grp > 0 and fq == 0:
                            eng.wait_ge(semh["ve"], veG0)
                        for fc in range(4):
                            for hh in range(2):
                                k = fq * 8 + fc * 2 + hh
                                bank = ROT4[k % 4]
                                if grp == 0 and fq == 0 and k < 4:
                                    pm_produce(eng, bank)
                                elif k >= 4:
                                    eng.wait_ge(semh["se"], seH0 + k - 3)
                                i_ = None
                                for dc in range(ND):
                                    i_ = eng.matmul(
                                        PS[bank],
                                        WS((fq % 2) * 8192, ND * 512).rearrange(
                                            "p (i c) -> p i c", i=ND)[
                                            :, dc, fc * 128:fc * 128 + 128],
                                        XB_FM(dc)[:, grp * 1024 + hh * 512:
                                                  grp * 1024 + hh * 512 + 512],
                                        start=(dc == 0), stop=(dc == ND - 1))
                                inc(i_, "te", 1)
                    sec("tensor", t_w1)
                    freed[f"mw{fq % 2}"] = ("te", semc["te"])

                    def s_h(eng, grp=grp, fq=fq):
                        if grp == 0 and fq == 0:
                            wait(eng, "pp")
                        base = semc["te"] - 8
                        for fc in range(4):
                            for hh in range(2):
                                k = fc * 2 + hh
                                bank = ROT4[(fq * 8 + k) % 4]
                                eng.wait_ge(semh["te"], base + k + 1)
                                fcg = fq * 4 + fc
                                i_ = eng.activation(
                                    HB_T(fcg)[:, hh * 512:hh * 512 + 512],
                                    PS[bank], AF.Gelu,
                                    bias=mb1[:, fcg:fcg + 1])
                                inc(i_, "se", 1)
                                pm_consumed(bank, "se")
                    sec("scalar", s_h)
                hready = semc["se"]

                for dh in range(2):
                    for fq2 in range(4):
                        def g_w2(eng, grp=grp, dh=dh, fq2=fq2):
                            fetch(eng, WS((fq2 % 2) * 8192, 8 * 512).rearrange(
                                "p (i c) -> p i c", i=8),
                                io["m_w2"][dh, fq2], f"mw{fq2 % 2}")
                        sec("gpsimd", g_w2)

                        def t_w2(eng, grp=grp, dh=dh, fq2=fq2, hready=hready):
                            fwait(eng, f"mw{fq2 % 2}")
                            eng.wait_ge(semh["se"], hready)
                            if fq2 == 0:
                                for cc in range(8):
                                    pm_produce(eng, YB8[cc])
                            for cc in range(8):
                                i_ = None
                                for fc in range(8):
                                    fcg = fq2 * 8 + fc
                                    i_ = eng.matmul(
                                        PS[YB8[cc]],
                                        HB_T(fcg)[:, cc * 128:cc * 128 + 128],
                                        WS((fq2 % 2) * 8192, 8 * 512).rearrange(
                                            "p (i c) -> p i c", i=8)[:, fc, :],
                                        start=(fq2 == 0 and fc == 0),
                                        stop=(fq2 == 3 and fc == 7))
                                inc(i_, "te", 1)
                        sec("tensor", t_w2)
                        freed[f"mw{fq2 % 2}"] = ("te", semc["te"])

                    def v_eo(eng, grp=grp, dh=dh, b2v=b2v):
                        eng.wait_ge(semh["b2"], b2v)
                        wait(eng, "eg")
                        wait(eng, "te")
                        for cc in range(8):
                            i_ = eng.tensor_tensor(
                                EOT3(cc)[:, dh * 512:dh * 512 + 512],
                                PS[YB8[cc]], B2BC[:, dh * 512:dh * 512 + 512],
                                ALU.add)
                            eng.drain()
                            inc(i_, "ve", 1)
                            pm_consumed(YB8[cc], "ve")
                    sec("vector", v_eo)

                def d_eog(eng, grp=grp):
                    sem_of("eg")
                    wait(eng, "ve")
                    i_ = eng.dma_start(
                        out=eog[grp * 1024:(grp + 1) * 1024, :].rearrange(
                            "(c p) d -> p c d", p=128),
                        in_=ar3.bf16(0, 8 * D).rearrange(
                            "p (c d) -> p c d", c=8))
                    inc(i_, "eg", 16)
                sec("sync", d_eog)

            def g_ageo(eng):
                wait(eng, "eg")
                i_ = eng.collective_compute(
                    "AllGather", ALU.bypass, replica_groups=[list(range(8))],
                    ins=[eog_t.ap().opt()], outs=[eoall_t.ap().opt()])
                inc(i_, "ag", 1)
            sec("gpsimd", g_ageo)
            eo_ag = semc["ag"]

            ysb = lambda b: ar3.bf16(16 * KB + b * 2048, D)
            out_ot = {0: 0, 1: 0}
            sem_of("o0"); sem_of("o1"); sem_of("yg")

            def g_gy(eng, eo_ag=eo_ag):
                eng.wait_ge(semh["ag"], eo_ag)
                wait(eng, "ve")
                for tc in range(NT):
                    if tc >= 2:
                        eng.wait_ge(semh["ve"], ygve0 + tc - 1)
                    i_ = eng.indirect_dma_start(
                        out=ysb(tc % 2), out_offset=None,
                        in_=eoall,
                        in_offset=bass.IndirectOffsetOnAxis(
                            smallu[:, 24 + tc:25 + tc], 0))
                    inc(i_, "yg", 16)
                    wait(eng, "yg")
                    i_ = eng.sem_inc(semh["ygr"], 1)
                    semc["ygr"] += 1
            sem_of("ygr")
            ygv0 = semc["yg"]
            ygve0 = semc["ve"]
            sec("gpsimd", g_gy)

            for tc in range(NT):
                def d_xin4(eng, tc=tc, po=out_ot[tc % 2]):
                    wait(eng, "xl")
                    wait(eng, "ve")
                    if po:
                        eng.wait_ge(semh[f"o{tc % 2}"], po)
                    i_ = eng.dma_start(
                        out=XIO_M(tc % 2),
                        in_=xres[tc * 128:tc * 128 + 128, :])
                    inc(i_, f"x4{tc % 2}", 16)
                sec("sync", d_xin4)
                sem_of(f"x4{tc % 2}")
                x4v = semc[f"x4{tc % 2}"]

                def v_y(eng, tc=tc, x4v=x4v):
                    eng.wait_ge(semh[f"x4{tc % 2}"], x4v)
                    eng.wait_ge(semh["ygr"], tc + 1)
                    i_ = eng.scalar_tensor_tensor(
                        XIO_M(tc % 2), ysb(tc % 2),
                        small[:, S_G + tc:S_G + tc + 1],
                        XIO_M(tc % 2), ALU.mult, ALU.add)
                    inc(i_, "ve", 1)
                sec("vector", v_y)

                def d_out4(eng, tc=tc):
                    wait(eng, "ve")
                    i_ = eng.dma_start(
                        out=out_t[tc * 128:tc * 128 + 128, :],
                        in_=XIO_M(tc % 2))
                    inc(i_, f"o{tc % 2}", 16)
                sec("sync", d_out4)
                out_ot[tc % 2] = semc[f"o{tc % 2}"]

            def d_fin(eng):
                wait(eng, "o0")
                wait(eng, "o1")
            sec("sync", d_fin)

        def emit_out_from_ar1():
            def d_out(eng):
                wait(eng, "ve")
                wait(eng, "te")
                wait(eng, "se")
                wait(eng, "xl")
                i_ = eng.dma_start(
                    out=out_t.rearrange("(i p) d -> p i d", p=128),
                    in_=ar1.f32(0, NT * D).rearrange("p (i d) -> p i d", i=NT))
                inc(i_, "ot", 16)
                wait(eng, "ot")
            sec("sync", d_out)

        if upto == "ln1":
            ve0 = emit_ln(ar2.f32(0, D))
            def v_cp(eng, ve0=ve0):
                wait(eng, "ve")
                for ti in range(NT):
                    i_ = eng.tensor_copy(X_T(ti), XH_TM(ti))
                    inc(i_, "ve", 1)
            sec("vector", v_cp)
            emit_out_from_ar1()
        elif upto == "attn0":
            emit_attention(0)
            emit_out_from_ar1()
        elif upto == "ffn0":
            emit_attention(0)
            emit_ffn()
            emit_out_from_ar1()
        elif upto == "attn1":
            emit_attention(0)
            emit_ffn()
            emit_attention(1)
            emit_out_from_ar1()
        else:
            emit_attention(0)
            emit_ffn()
            emit_attention(1)
            emit_moe()

    return nc


def kernel(**inputs):
    nc = build_nc(UPTO)
    in_maps = [prep_inputs(b, **inputs) for b in range(8)]
    res = run_bass_kernel_spmd(nc, in_maps, core_ids=list(range(8)))
    return np.stack([r["out"] for r in res.results], axis=0)


# version 05
# speedup vs baseline: 1.0713x; 1.0713x over previous
import contextlib
import numpy as np
import ml_dtypes
import concourse.bass as bass
import concourse.mybir as mybir
from concourse.bass_utils import run_bass_kernel_spmd

F32 = mybir.dt.float32
BF16 = mybir.dt.bfloat16
U32 = mybir.dt.uint32
AF = mybir.ActivationFunctionType
ALU = mybir.AluOpType
AX = mybir.AxisListType

T = 2048
D = 1024
NT = T // 128
ND = D // 128
H = 16
HD = 64
NP = 8
SCALE = HD ** -0.5
DH = 4096
NDH = DH // 128
E = 8
CC = 384
SLOTS = E * CC
NCC = SLOTS // 128
CAPACITY = 3072
EPS = 1e-5
KB = 1024

UPTO = "full"
DBG_MOE = False



def _blk_lhs(w):
    I, O = w.shape
    return np.ascontiguousarray(
        w.reshape(I // 128, 128, O // 128, 128).transpose(2, 1, 0, 3))


def _blk_rhs(w, ncol):
    I, O = w.shape
    return np.ascontiguousarray(
        w.reshape(I // 128, 128, O // ncol, ncol).transpose(2, 1, 0, 3))


def _cols(b):
    return np.ascontiguousarray(b.reshape(-1, 128).T)


def prep_inputs(core, **ins):
    g = lambda n: np.asarray(ins[n])
    d = {}
    d["x"] = np.ascontiguousarray(g("L0_attn_x")[core])
    d["rmask"] = (np.arange(8) < core).astype(np.float32)
    d["a0_wq"] = _blk_lhs(g("L0_attn_Wq"))
    d["a0_wk"] = _blk_lhs(g("L0_attn_Wk"))
    d["a0_wv"] = _blk_lhs(g("L0_attn_Wv"))
    d["a0_wo"] = _blk_rhs(g("L0_attn_Wo"), 256)
    d["a0_bqkv"] = np.concatenate(
        [_cols(g("L0_attn_bq")), _cols(g("L0_attn_bk")), _cols(g("L0_attn_bv"))],
        axis=1)
    d["a0_bo"] = np.asarray(g("L0_attn_bo"))
    d["f0_w1t"] = _blk_lhs(np.ascontiguousarray(g("L0_ffn_w1_w").T))
    d["f0_w2t"] = _blk_lhs(np.ascontiguousarray(g("L0_ffn_w2_w").T))
    d["f0_h1t"] = _blk_lhs(np.ascontiguousarray(g("L0_ffn_h1_w").T))
    d["f0_h2t"] = _blk_lhs(np.ascontiguousarray(g("L0_ffn_h2_w").T))
    d["f0_w3t"] = _blk_rhs(np.ascontiguousarray(g("L0_ffn_w3_w").T), 128)
    d["f0_s"] = np.concatenate(
        [_cols(g("L0_ffn_w1_s")), _cols(g("L0_ffn_w2_s")),
         _cols(g("L0_ffn_h1_s")), _cols(g("L0_ffn_h2_s")),
         _cols(g("L0_ffn_w3_s"))], axis=1)
    d["a1_wq"] = _blk_lhs(g("L1_attn_Wq"))
    d["a1_wl"] = _blk_lhs(g("L1_attn_Wl"))
    d["a1_wk"] = _blk_lhs(g("L1_attn_Wk"))
    d["a1_wv"] = _blk_lhs(g("L1_attn_Wv"))
    d["a1_wo"] = _blk_rhs(g("L1_attn_Wo"), 256)
    d["a1_bqkv"] = np.concatenate(
        [_cols(g("L1_attn_bq")), _cols(g("L1_attn_bk")), _cols(g("L1_attn_bv")),
         _cols(g("L1_attn_bl"))], axis=1)
    d["a1_bo"] = np.asarray(g("L1_attn_bo"))
    d["m_rwt"] = np.ascontiguousarray(g("L1_ffn_router_w").T)
    w1 = g("L1_ffn_w1")[core]
    d["m_w1"] = np.ascontiguousarray(
        w1.reshape(ND, 128, 8, 512).transpose(2, 1, 0, 3))
    w2 = g("L1_ffn_w2")[core]
    d["m_w2"] = np.ascontiguousarray(
        w2.reshape(4, 8, 128, 2, 512).transpose(3, 0, 2, 1, 4))
    d["m_b1"] = np.ascontiguousarray(
        g("L1_ffn_b1")[core].reshape(NDH, 128).T)
    d["m_b2"] = np.ascontiguousarray(g("L1_ffn_b2")[core])
    esel = np.zeros(8, np.float32); esel[core] = 1.0
    d["esel"] = esel
    d["cbase"] = np.array([core * T], np.float32)
    return d



def build_nc(upto="full"):
    nc = bass.Bass("TRN2", target_bir_lowering=False)
    io = {}

    def ein(name, shape):
        io[name] = nc.dram_tensor(name, list(shape), F32, kind="ExternalInput").ap()

    ein("x", [T, D]); ein("rmask", [8])
    ein("a0_wq", [NP, 128, ND, 128]); ein("a0_wk", [NP, 128, ND, 128])
    ein("a0_wv", [NP, 128, ND, 128]); ein("a0_wo", [4, 128, ND, 256])
    ein("a0_bqkv", [128, 3 * ND]); ein("a0_bo", [D])
    ein("f0_w1t", [NDH, 128, ND, 128]); ein("f0_w2t", [NDH, 128, ND, 128])
    ein("f0_h1t", [NDH, 128, NDH, 128]); ein("f0_h2t", [NDH, 128, NDH, 128])
    ein("f0_w3t", [8, 128, NDH, 128])
    ein("f0_s", [128, 4 * NDH + 8])
    ein("a1_wq", [NP, 128, ND, 128]); ein("a1_wl", [2, 128, ND, 128])
    ein("a1_wk", [NP, 128, 2, 128]); ein("a1_wv", [NP, 128, 2, 128])
    ein("a1_wo", [4, 128, ND, 256])
    ein("a1_bqkv", [128, 3 * ND + 2]); ein("a1_bo", [D])
    ein("m_rwt", [D, E])
    ein("m_w1", [8, 128, ND, 512])
    ein("m_w2", [2, 4, 128, 8, 512])
    ein("m_b1", [128, NDH]); ein("m_b2", [D])
    ein("esel", [8]); ein("cbase", [1])

    out_t = nc.dram_tensor("out", [T, D], F32, kind="ExternalOutput").ap()
    xres = nc.dram_tensor("xres", [T, D], F32, kind="Internal").ap()
    bcz = nc.dram_tensor("bcz", [H, T], F32, kind="Internal").ap()
    bcb = nc.dram_tensor("bcb", [H], F32, kind="Internal").ap()
    bcm = nc.dram_tensor("bcm", [2, T], F32, kind="Internal").ap()
    cc_in = nc.dram_tensor("cc_in", [E], F32, kind="Internal")
    cc_out = nc.dram_tensor("cc_out", [8 * E], F32, kind="Internal")
    xag_in_t = nc.dram_tensor("xag_in", [T, D], BF16, kind="Internal")
    xag_in = xag_in_t.ap()
    xall_t = nc.dram_tensor("xall", [8 * T, D], BF16, kind="Internal")
    xall = xall_t.ap()
    idxb_t = nc.dram_tensor("idxb", [E * 8 * T], F32, kind="Internal")
    idxb = idxb_t.ap()
    idxall_t = nc.dram_tensor("idxall", [E * 8 * T], F32, kind="Internal")
    eog_t = nc.dram_tensor("eog", [CAPACITY, D], BF16, kind="Internal")
    eog = eog_t.ap()
    eoall_t = nc.dram_tensor("eoall", [8 * CAPACITY, D], BF16, kind="Internal")
    eoall = eoall_t.ap()
    if DBG_MOE:
        dbg_xb = nc.dram_tensor("dbg_xb", [512, D], F32, kind="ExternalOutput").ap()
        dbg_u24 = nc.dram_tensor("dbg_u24", [128, 24], U32, kind="ExternalOutput").ap()
        dbg_h = nc.dram_tensor("dbg_h", [128, 1024], F32, kind="ExternalOutput").ap()
        dbg_i24 = nc.dram_tensor("dbg_i24", [24, 128], F32, kind="ExternalOutput").ap()

    st = contextlib.ExitStack()
    with st:
        class Arena:
            def __init__(self, name, kib):
                self.nbytes = kib * KB
                self.t = st.enter_context(
                    nc.sbuf_tensor(name, [128, self.nbytes // 4], F32))

            def f32(self, off, n, parts=slice(None)):
                assert off % 4 == 0 and off + 4 * n <= self.nbytes, (off, n)
                return self.t.ap()[parts, off // 4: off // 4 + n]

            def bf16(self, off, n, parts=slice(None)):
                assert off % 2 == 0 and off + 2 * n <= self.nbytes, (off, n)
                return self.t.ap().bitcast(BF16)[parts, off // 2: off // 2 + n]

        ar1 = Arena("ar1", 64)
        ar2 = Arena("ar2", 64)
        ar3 = Arena("ar3", 32)
        ar4 = Arena("ar4", 22)
        wst = Arena("wst", 16)

        sbt = lambda name, shape, dt=F32: st.enter_context(
            nc.sbuf_tensor(name, shape, dt))
        ident = sbt("ident", [128, 128], BF16)
        identf = sbt("identf", [128, 128], F32)
        iota512 = sbt("iota512", [128, 512], F32)
        ciota = sbt("ciota", [128, NCC], F32)
        ones2c = sbt("ones2c", [128, 2], BF16)
        b_a0 = sbt("b_a0", [128, 3 * ND], F32)
        b_a1 = sbt("b_a1", [128, 3 * ND + 2], F32)
        f0s = sbt("f0s", [128, 4 * NDH + 8], F32)
        mb1 = sbt("mb1", [128, NDH], F32)
        stats = sbt("stats", [128, 2 * NT], F32)
        mrs = sbt("mrs", [128, 2 * NT], F32)
        bh2 = sbt("bh2", [128, 2], F32)
        small = sbt("small", [128, 96], F32)
        smallu = sbt("smallu", [128, 44], U32)
        sm2 = sbt("sm2", [128, 112], F32)
        er16 = sbt("er16", [16, 256], F32)
        i24 = sbt("i24", [24, 128], F32)
        u24 = sbt("u24", [128, 24], U32)
        esel_sb = sbt("esel_sb", [8, 1], F32)
        ZEROT = sbt("zerot", [128, 256], F32)
        cst = sbt("cst", [128, 2], F32)
        rmk = sbt("rmk", [8, 1], F32)
        agg = sbt("agg", [8, 8], F32)
        off8 = sbt("off8", [8, 1], F32)
        ones8 = sbt("ones8", [8, 1], F32)

        greg = st.enter_context(nc.gpsimd.register("greg"))
        psA = st.enter_context(nc.psum_tensor("psA", [128, 1024], F32))
        psB = st.enter_context(nc.psum_tensor("psB", [128, 1024], F32))
        pav = st.enter_context(nc.psum_tensor("pav", [128, 512], F32))
        pax = st.enter_context(nc.psum_tensor("pax", [128, 512], F32))
        pex = st.enter_context(nc.psum_tensor("pex", [128, 512], F32))
        ptr = st.enter_context(nc.psum_tensor("ptr", [128, 1024], BF16))
        PS = {"A0": psA[:, 0:512], "A1": psA[:, 512:1024],
              "B0": psB[:, 0:512], "B1": psB[:, 512:1024],
              "av": pav[:], "ax": pax[:], "ex": pex[:],
              "trf": ptr.ap().bitcast(F32)}
        ROT4 = ["A0", "A1", "B0", "B1"]

        SEMS = ["te", "ve", "se", "ge"]
        semh = {n: st.enter_context(nc.semaphore(n)) for n in SEMS}
        semc = {n: 0 for n in SEMS}

        def sem_of(name):
            if name not in semh:
                semh[name] = st.enter_context(nc.semaphore(name))
                semc[name] = 0
            return semh[name]

        def inc(instr, name, n=1):
            sem_of(name)
            semc[name] += n
            instr.then_inc(semh[name], n)
            return semc[name]

        def wait(eng, name, val=None):
            sem_of(name)
            eng.wait_ge(semh[name], semc[name] if val is None else val)

        freed = {}

        def pm_produce(eng, bank):
            f = freed.pop(bank, None)
            if f is not None:
                eng.wait_ge(semh[f[0]], f[1])

        def pm_consumed(bank, sem, val=None):
            freed[bank] = (sem, semc[sem] if val is None else val)

        blk = st.enter_context(nc.Block())

        def sec(name, fn):
            getattr(blk, name)(fn)

        X_T = lambda ti: ar1.f32(ti * 4096, D)
        XH_TM = lambda ti: ar3.bf16(ti * 2048, D)
        XH_FM = lambda dc: ar2.bf16(dc * 4096, T)
        QSL = lambda buf: ar2.bf16(32 * KB + buf * 8192, T)
        KSL = lambda buf: ar2.bf16(32 * KB + buf * 8192 + 4096, T)
        ZRT = ar2.f32(48 * KB, T, parts=slice(64, 65))
        LArow = lambda lc: ar2.bf16(56 * KB + lc * 4096, T)
        VAUG = lambda tc: ar4.bf16(tc * 272, 136)
        PP = lambda buf: ar4.bf16(4608 + buf * 2048, 1024)
        QKSQ = ar4.bf16(8704, 2048)
        ZBC = ar4.f32(12800, 2048)
        AV_FM = lambda dc: ar3.bf16(dc * 4096, T)
        BOBC = ar2.f32(56 * KB, D)
        BTMP = ar2.bf16(48 * KB, 2048, parts=slice(0, 64))
        MASKS = lambda m: wst.bf16(12 * KB + m * 1024, 512)
        H_T = lambda oc: ar1.bf16(oc * 2048, 1024)
        G_T = lambda oc: ar2.bf16(oc * 2048, 1024)
        XHF_H = lambda dc: ar2.bf16(dc * 2048, 1024)
        XIO_F = lambda b: ar4.f32(b * 4096, D)
        Y_FMF = lambda ob: ar2.f32(ob * 4096, 1024)
        SILT = lambda b: ar4.bf16(12288 + b * 1024, 512)
        XHM_FM = lambda dc: ar1.bf16(dc * 4096, T)
        XB_TM = lambda sc: ar1.bf16(sc * 2048, D)
        XB_FM = lambda dc: ar2.bf16(dc * 6144, SLOTS)
        HB_T = lambda fcg: ar1.bf16(fcg * 2048, 1024)
        EOT3 = lambda cc: ar3.bf16(cc * 2048, D)
        EO_T = lambda cc: ar1.bf16(cc * 2048, 1024)
        D_T = lambda tc: ar1.bf16(48 * KB + tc * 1024, 512)
        XGF = lambda dc: ar2.bf16(dc * 6144, SLOTS)
        B2BC = ar2.f32(48 * KB, D)
        ONEHOT = ar2.f32(0, T, parts=slice(0, 8))
        CNTI = ar2.f32(8 * KB, T, parts=slice(0, 8))
        EROW8 = ar2.f32(16 * KB, T, parts=slice(0, 8))
        TMP8 = ar2.f32(24 * KB, T, parts=slice(0, 8))
        RANKR = ar2.f32(32 * KB, T, parts=slice(0, 1))
        OFFR = ar2.f32(40 * KB, T, parts=slice(32, 33))
        XIO_M = lambda b: ar2.f32(56 * KB + b * 4096, D)
        HH_T = lambda fc: ar4.bf16(fc * 768, CC)
        SLOTBC = ar4.f32(12288, 2048)
        RT = lambda b: ar4.bf16(20480 + b * 1024, 512)
        RWTS = wst.bf16(12 * KB, ND * 8)
        WS = lambda off, n: wst.bf16(off, n)
        WSF = lambda off, n: wst.f32(off, n)

        S_PSTAR, S_ESTAR, S_G, S_NMAX, S_TMP = 0, 16, 64, 80, 83
        C_LG, C_PROB, C_Z8, C_MX8 = 0, 8, 16, 24
        C_RANK, C_OFFT, C_KEPT, C_SLOTP, C_SLSH = 32, 48, 64, 80, 96

        def p0_sync(eng):
            i_ = eng.dma_start(
                out=ar1.f32(0, NT * D).rearrange("p (i d) -> p i d", i=NT),
                in_=io["x"].rearrange("(i p) d -> p i d", p=128))
            inc(i_, "xl", 16)
            i_ = eng.dma_start(out=b_a0[:], in_=io["a0_bqkv"]); inc(i_, "pp", 16)
            i_ = eng.dma_start(out=b_a1[:], in_=io["a1_bqkv"]); inc(i_, "pp", 16)
            i_ = eng.dma_start(out=f0s[:], in_=io["f0_s"]); inc(i_, "pp", 16)
            i_ = eng.dma_start(out=mb1[:], in_=io["m_b1"]); inc(i_, "pp", 16)
            i_ = eng.dma_start(out=esel_sb[:], in_=io["esel"].unsqueeze(1))
            inc(i_, "pp", 16)
            i_ = eng.dma_start(out=rmk[:], in_=io["rmask"].unsqueeze(1))
            inc(i_, "pp", 16)

        def p0_gp(eng):
            eng.memset(ident[:], 1.0)
            eng.memset(identf[:], 1.0)
            eng.drain()
            eng.affine_select(ident[:], ident[:], pattern=[[-1, 128]],
                              compare_op=ALU.is_equal, fill=0.0, base=0,
                              channel_multiplier=1)
            eng.drain()
            eng.affine_select(identf[:], identf[:], pattern=[[-1, 128]],
                              compare_op=ALU.is_equal, fill=0.0, base=0,
                              channel_multiplier=1)
            eng.memset(ones2c[:], 0.0)
            eng.drain()
            eng.memset(ones2c[0:64, 0:1], 1.0)
            eng.drain()
            eng.memset(ones2c[64:128, 1:2], 1.0)
            eng.memset(ones8[:], 1.0)
            eng.memset(ZEROT[:], 0.0)
            eng.memset(cst[:, 0:1], float(EPS))
            eng.memset(cst[:, 1:2], float(SCALE))
            eng.memset(wst.bf16(12 * KB, 2048), 1.0)
            eng.drain()
            for m in range(4):
                eng.affine_select(MASKS(m), MASKS(m), pattern=[[1, 512]],
                                  compare_op=ALU.is_ge, fill=0.0,
                                  base=-128 * m, channel_multiplier=-1)
            eng.iota(iota512[:], pattern=[[1, 512]], base=0, channel_multiplier=0,
                     allow_small_or_imprecise_dtypes=True)
            i_ = eng.iota(ciota[:], pattern=[[128, NCC]], base=0,
                          channel_multiplier=1,
                          allow_small_or_imprecise_dtypes=True)
            inc(i_, "ge", 1)

        sec("sync", p0_sync)
        sec("gpsimd", p0_gp)

        def emit_ln(sq_ap):
            sem_of("xl")
            dxv, vev, tev = semc["xl"], semc["ve"], semc["te"]

            def s_sq(eng):
                eng.wait_ge(semh["xl"], dxv)
                eng.wait_ge(semh["ve"], vev)
                eng.wait_ge(semh["te"], tev)
                i_ = None
                for ti in range(NT):
                    if ti:
                        eng.drain()
                    i_ = eng.activation(sq_ap, X_T(ti), AF.Square,
                                        accum_out=stats[:, NT + ti:NT + ti + 1])
                inc(i_, "se", 1)

            def v_stats(eng):
                eng.wait_ge(semh["xl"], dxv)
                wait(eng, "se")
                for ti in range(NT):
                    eng.reduce_sum(stats[:, ti:ti + 1], X_T(ti), axis=AX.X)
                eng.drain()
                eng.tensor_scalar(mrs[:, 0:NT], stats[:, 0:NT], 1.0 / D, None,
                                  ALU.mult)
                eng.drain()
                eng.tensor_scalar(stats[:, NT:2 * NT], stats[:, NT:2 * NT],
                                  1.0 / D, None, ALU.mult)
                eng.drain()
                eng.tensor_tensor(stats[:, 0:NT], mrs[:, 0:NT], mrs[:, 0:NT],
                                  ALU.mult)
                eng.drain()
                i_ = eng.tensor_tensor(stats[:, NT:2 * NT], stats[:, NT:2 * NT],
                                       stats[:, 0:NT], ALU.subtract)
                inc(i_, "ve", 1)

            def s_rstd(eng):
                wait(eng, "ve")
                eng.wait_ge(semh["ge"], 1)
                i_ = eng.activation(mrs[:, NT:2 * NT], stats[:, NT:2 * NT],
                                    AF.Sqrt, bias=cst[:, 0:1], scale=1.0)
                inc(i_, "se", 1)

            def v_apply(eng):
                wait(eng, "se")
                eng.reciprocal(mrs[:, NT:2 * NT], mrs[:, NT:2 * NT])
                eng.drain()
                for ti in range(NT):
                    i_ = eng.tensor_scalar(XH_TM(ti), X_T(ti), mrs[:, ti:ti + 1],
                                           mrs[:, NT + ti:NT + ti + 1],
                                           ALU.subtract, ALU.mult)
                    inc(i_, "ve", 1)

            sec("scalar", s_sq)
            sec("vector", v_stats)
            sec("scalar", s_rstd)
            sec("vector", v_apply)
            return semc["ve"] - NT

        def emit_tm2fm(ve_base, src, dst_fm, ntiles=NT, dchunks=ND):
            veE0 = semc["ve"]

            def t_tr(eng):
                for ti in range(ntiles):
                    eng.wait_ge(semh["ve"], ve_base + ti + 1)
                    eng.wait_ge(semh["ve"], veE0 + ti)
                    i_ = None
                    for dc in range(dchunks):
                        i_ = eng.transpose(
                            ptr[:, dc * 128:dc * 128 + 128],
                            src(ti)[:, dc * 128:dc * 128 + 128], ident[:])
                    inc(i_, "te", 1)

            def v_ev(eng):
                base = semc["te"] - ntiles
                for ti in range(ntiles):
                    eng.wait_ge(semh["te"], base + ti + 1)
                    i_ = None
                    for dc in range(dchunks):
                        i_ = eng.tensor_copy(
                            dst_fm(dc)[:, ti * 128:ti * 128 + 128],
                            ptr[:, dc * 128:dc * 128 + 128])
                    inc(i_, "ve", 1)

            sec("tensor", t_tr)
            sec("vector", v_ev)

        wstbar = [0, 0]

        def set_wstbar():
            wstbar[0] = semc["te"]
            wstbar[1] = semc["ve"]

        def fetch(eng, dst_ap, src_ap, bufkey):
            if wstbar[0]:
                eng.wait_ge(semh["te"], wstbar[0])
            if wstbar[1]:
                eng.wait_ge(semh["ve"], wstbar[1])
            f = freed.pop(bufkey, None)
            if f is not None:
                eng.wait_ge(semh[f[0]], f[1])
            i_ = eng.dma_start(out=dst_ap, in_=src_ap)
            return inc(i_, "F" + bufkey, 16)

        def fwait(eng, *keys):
            for k in keys:
                wait(eng, "F" + k)

        def emit_attention(L):
            pfx = "a0_" if L == 0 else "a1_"
            bqk = b_a0 if L == 0 else b_a1
            causal = (L == 0)

            set_wstbar()
            ve0 = emit_ln(ar2.f32(0, D))
            emit_tm2fm(ve0, XH_TM, XH_FM)

            if L == 1:
                def g_wl(eng):
                    for lc in range(2):
                        fetch(eng, WS(lc * 2048, ND * 128).rearrange(
                            "p (i c) -> p i c", i=ND), io["a1_wl"][lc], f"wq{lc}")
                sec("gpsimd", g_wl)

                seL0 = semc["se"]

                def t_lat(eng, seL0=seL0):
                    fwait(eng, "wq0", "wq1")
                    wait(eng, "ve")
                    for lc in range(2):
                        for sc in range(4):
                            k = lc * 4 + sc
                            bank = ROT4[k % 4]
                            if k < 4:
                                pm_produce(eng, bank)
                            else:
                                eng.wait_ge(semh["se"], seL0 + k - 3)
                            i_ = None
                            for ic in range(ND):
                                i_ = eng.matmul(
                                    PS[bank],
                                    WS(lc * 2048, ND * 128).rearrange(
                                        "p (i c) -> p i c", i=ND)[:, ic, :],
                                    XH_FM(ic)[:, sc * 512:sc * 512 + 512],
                                    start=(ic == 0), stop=(ic == ND - 1))
                            inc(i_, "te", 1)

                def s_lat(eng):
                    base = semc["te"] - 8
                    for k in range(8):
                        lc, sc = divmod(k, 4)
                        bank = ROT4[k % 4]
                        eng.wait_ge(semh["te"], base + k + 1)
                        i_ = eng.activation(
                            LArow(lc)[:, sc * 512:sc * 512 + 512], PS[bank],
                            AF.Identity, bias=bqk[:, 3 * ND + lc:3 * ND + lc + 1],
                            scale=1.0)
                        inc(i_, "se", 1)
                        pm_consumed(bank, "se")

                sec("tensor", t_lat)
                freed["wq0"] = ("te", semc["te"])
                freed["wq1"] = ("te", semc["te"])
                sec("scalar", s_lat)

            for c in range(NP):
                buf = c % 2
                nkv = ND if L == 0 else 2
                kvsrc = XH_FM if L == 0 else LArow

                def g_w(eng, c=c, buf=buf, nkv=nkv):
                    fetch(eng, WS(buf * 2048, ND * 128).rearrange(
                        "p (i c) -> p i c", i=ND), io[pfx + "wq"][c], f"wq{buf}")
                    fetch(eng, WS(4096 + buf * 2048, nkv * 128).rearrange(
                        "p (i c) -> p i c", i=nkv), io[pfx + "wk"][c], f"wk{buf}")
                    fetch(eng, WS(4 * 2048 + buf * 2048, nkv * 128).rearrange(
                        "p (i c) -> p i c", i=nkv), io[pfx + "wv"][c], f"wv{buf}")
                sec("gpsimd", g_w)

                seQ0 = semc["se"]

                def t_qk(eng, c=c, buf=buf, nkv=nkv, kvsrc=kvsrc, seQ0=seQ0):
                    fwait(eng, f"wq{buf}", f"wk{buf}")
                    if c == 0:
                        wait(eng, "ve")
                        if L == 1:
                            wait(eng, "se")
                    for which in range(2):
                        nic = ND if which == 0 else nkv
                        srcf = XH_FM if which == 0 else kvsrc
                        woff = buf * 2048 if which == 0 else 4096 + buf * 2048
                        for sc in range(4):
                            k = which * 4 + sc
                            bank = ROT4[k % 4]
                            if k < 4:
                                pm_produce(eng, bank)
                            else:
                                eng.wait_ge(semh["se"], seQ0 + k - 3)
                            i_ = None
                            for ic in range(nic):
                                i_ = eng.matmul(
                                    PS[bank],
                                    WS(woff, nic * 128).rearrange(
                                        "p (i c) -> p i c", i=nic)[:, ic, :],
                                    srcf(ic)[:, sc * 512:sc * 512 + 512],
                                    start=(ic == 0), stop=(ic == nic - 1))
                            inc(i_, "te", 1)

                def s_qk(eng, c=c, buf=buf):
                    wait(eng, "pp")
                    base = semc["te"] - 8
                    for k in range(8):
                        which, sc = divmod(k, 4)
                        bank = ROT4[k % 4]
                        dst = QSL(buf) if which == 0 else KSL(buf)
                        eng.wait_ge(semh["te"], base + k + 1)
                        i_ = eng.activation(
                            dst[:, sc * 512:sc * 512 + 512], PS[bank], AF.Identity,
                            bias=bqk[:, which * ND + c:which * ND + c + 1],
                            scale=1.0)
                        inc(i_, "se", 1)
                        pm_consumed(bank, "se")

                sec("tensor", t_qk)
                freed[f"wq{buf}"] = ("te", semc["te"])
                freed[f"wk{buf}"] = ("te", semc["te"])
                sec("scalar", s_qk)

                def s_sq_q(eng, buf=buf):
                    wait(eng, "se")
                    i_ = eng.activation(QKSQ, QSL(buf), AF.Square)
                    inc(i_, "se", 1)
                sec("scalar", s_sq_q)

                def t_nq_fix(eng):
                    wait(eng, "se")
                    for bank in ("A0", "A1", "B0", "B1"):
                        pm_produce(eng, bank)
                    i_ = None
                    for sc in range(4):
                        dst = psA if sc < 2 else psB
                        i_ = eng.matmul(dst[0:2, (sc % 2) * 512:(sc % 2) * 512 + 512],
                                        ones2c[:], QKSQ[:, sc * 512:sc * 512 + 512],
                                        start=True, stop=True)
                    inc(i_, "te", 1)

                def v_nq(eng, which=0):
                    wait(eng, "te")
                    eng.reduce_max(small[0:2, S_TMP + which:S_TMP + which + 1],
                                   psA[0:2, :], axis=AX.X)
                    eng.drain()
                    i_ = eng.reduce_max(small[0:2, S_TMP + 2 + which:S_TMP + 3 + which],
                                        psB[0:2, :], axis=AX.X)
                    eng.drain()
                    i_ = eng.tensor_tensor(
                        small[0:2, S_TMP + which:S_TMP + which + 1],
                        small[0:2, S_TMP + which:S_TMP + which + 1],
                        small[0:2, S_TMP + 2 + which:S_TMP + 3 + which], ALU.max)
                    inc(i_, "ve", 1)
                    for bank in ("A0", "A1", "B0", "B1"):
                        pm_consumed(bank, "ve")

                def s_sq_k(eng, buf=buf):
                    wait(eng, "ve")
                    i_ = eng.activation(QKSQ, KSL(buf), AF.Square)
                    inc(i_, "se", 1)

                sec("tensor", t_nq_fix)
                sec("vector", lambda eng: v_nq(eng, 0))
                sec("scalar", s_sq_k)
                sec("tensor", t_nq_fix)
                sec("vector", lambda eng: v_nq(eng, 1))

                def v_bh(eng):
                    wait(eng, "ve")
                    i_ = eng.tensor_tensor(small[0:2, S_TMP + 4:S_TMP + 5],
                                           small[0:2, S_TMP:S_TMP + 1],
                                           small[0:2, S_TMP + 1:S_TMP + 2],
                                           ALU.mult)
                    inc(i_, "ve", 1)

                def s_bh(eng):
                    wait(eng, "ve")
                    i_ = eng.activation(small[0:2, S_TMP + 5:S_TMP + 6],
                                        small[0:2, S_TMP + 4:S_TMP + 5], AF.Sqrt)
                    inc(i_, "se", 1)

                def v_bh2(eng):
                    wait(eng, "se")
                    i_ = eng.tensor_scalar(small[0:2, S_TMP + 6:S_TMP + 7],
                                           small[0:2, S_TMP + 5:S_TMP + 6],
                                           -float(SCALE), None, ALU.mult)
                    inc(i_, "ve", 1)

                sec("vector", v_bh)
                sec("scalar", s_bh)
                sec("vector", v_bh2)
                vev = semc["ve"]

                def d_bh(eng, c=c, vev=vev):
                    eng.wait_ge(semh["ve"], vev)
                    i_ = eng.dma_start(out=bcb[2 * c:2 * c + 2],
                                       in_=small[0:2, S_TMP + 6:S_TMP + 7].squeeze(1))
                    inc(i_, "bb", 16)
                    wait(eng, "bb")
                    i_ = eng.dma_start(
                        out=bh2[:], in_=bcb[2 * c:2 * c + 2].unsqueeze(0)
                        .broadcast_to([128, 2]))
                    inc(i_, "bb", 16)
                sec("sync", d_bh)
                sem_of("bb")
                bh_dxv = semc["bb"]

                def v_vone(eng):
                    wait(eng, "te")
                    i_ = eng.memset(ar4.bf16(0, 16 * 136), 1.0)
                    inc(i_, "ve", 1)
                sec("vector", v_vone)
                vsetv = semc["ve"]

                veV0 = semc["ve"]

                def t_v(eng, c=c, buf=buf, nkv=nkv, kvsrc=kvsrc, vsetv=vsetv,
                        veV0=veV0):
                    fwait(eng, f"wv{buf}")
                    eng.wait_ge(semh["ve"], vsetv)
                    for tc in range(NT):
                        bank = ["av", "ax"][tc % 2]
                        if tc < 2:
                            pm_produce(eng, bank)
                        else:
                            eng.wait_ge(semh["ve"], veV0 + tc - 1)
                        i_ = None
                        for ic in range(nkv):
                            i_ = eng.matmul(
                                PS[bank][:, 0:128],
                                kvsrc(ic)[:, tc * 128:tc * 128 + 128],
                                WS(4 * 2048 + buf * 2048, nkv * 128).rearrange(
                                    "p (i c) -> p i c", i=nkv)[:, ic, :],
                                start=(ic == 0), stop=(ic == nkv - 1))
                        inc(i_, "te", 1)

                def v_vev(eng, c=c):
                    base = semc["te"] - NT
                    for tc in range(NT):
                        bank = ["av", "ax"][tc % 2]
                        eng.wait_ge(semh["te"], base + tc + 1)
                        eng.tensor_copy(VAUG(tc)[:, 0:64], PS[bank][:, 0:64])
                        eng.drain()
                        i_ = eng.tensor_copy(VAUG(tc)[:, 65:129],
                                             PS[bank][:, 64:128])
                        inc(i_, "ve", 1)
                        pm_consumed(bank, "ve")

                sec("tensor", t_v)
                freed[f"wv{buf}"] = ("te", semc["te"])
                sec("vector", v_vev)
                vaug_ve = semc["ve"]

                for h in range(2):
                    hr = slice(64 * h, 64 * h + 64)
                    for qc in range(4):
                        nkc = 4 * qc + 4 if causal else NT
                        ngrp = nkc // 2
                        pv_cnt = 0
                        exp_ses = []
                        mask_ves = []
                        pvb = "av" if (4 * h + qc) % 2 == 0 else "ex"
                        pvt = pav if (4 * h + qc) % 2 == 0 else pex
                        zoff = 48 * KB + ((4 * h + qc) % 4) * 2048

                        def t_s(eng, c=c, buf=buf, h=h, hr=hr, qc=qc, nkc=nkc,
                                ngrp=ngrp, bh_dxv=bh_dxv, vaug_ve=vaug_ve,
                                pvb=pvb, pvt=pvt):
                            eng.wait_ge(semh["ve"], vaug_ve)
                            eng.wait_ge(semh["bb"], bh_dxv)
                            pm_produce(eng, pvb)

                            def emit_S(g):
                                gb = "A" if g % 2 == 0 else "B"
                                if g < 2:
                                    pm_produce(eng, gb + "0")
                                    pm_produce(eng, gb + "1")
                                else:
                                    eng.wait_ge(semh["se"], exp_ses[g - 2])
                                ps_full = psA if g % 2 == 0 else psB
                                i_ = None
                                for j in range(2):
                                    kc = 2 * g + j
                                    i_ = eng.matmul(
                                        ps_full[:, j * 512:j * 512 + 512],
                                        KSL(buf)[hr, kc * 128:kc * 128 + 128],
                                        QSL(buf)[hr, qc * 512:qc * 512 + 512],
                                        start=True, stop=True)
                                inc(i_, "te", 1)

                            emit_S(0)
                            for g in range(ngrp):
                                if g + 1 < ngrp:
                                    emit_S(g + 1)
                                eng.wait_ge(semh["se"], exp_ses[g])
                                if mask_ves[g]:
                                    eng.wait_ge(semh["ve"], mask_ves[g])
                                i_ = None
                                for j in range(2):
                                    kc = 2 * g + j
                                    i_ = eng.matmul(
                                        pvt[0:65, :],
                                        VAUG(kc)[:, 65 * h:65 * h + 65],
                                        PP(g % 2)[:, j * 512:j * 512 + 512],
                                        start=(kc == 0), stop=(kc == nkc - 1))
                                inc(i_, "te", 1)

                        te0 = semc["te"]
                        se0 = semc["se"]
                        ve0_ = semc["ve"]
                        def s_exp2(eng, c=c, h=h, qc=qc, ngrp=ngrp, te0=te0):
                            for g in range(ngrp):
                                eng.wait_ge(semh["te"],
                                            te0 + (1 if g == 0 else 2 * g))
                                ps_full = psA if g % 2 == 0 else psB
                                i_ = eng.activation(PP(g % 2), ps_full[:],
                                                    AF.Exp,
                                                    bias=bh2[:, h:h + 1],
                                                    scale=cst[:, 1:2])
                                inc(i_, "se", 1)
                        def v_mask(eng, qc=qc, ngrp=ngrp, se0=se0, causal=causal):
                            if not causal:
                                return
                            for gi, g in enumerate((ngrp - 2, ngrp - 1)):
                                eng.wait_ge(semh["se"], se0 + g + 1)
                                i_ = None
                                for j in range(2):
                                    m = 2 * gi + j
                                    i_ = eng.tensor_tensor(
                                        PP(g % 2)[:, j * 512:j * 512 + 512],
                                        PP(g % 2)[:, j * 512:j * 512 + 512],
                                        MASKS(m), ALU.mult)
                                inc(i_, "ve", 1)
                        sec("scalar", s_exp2)
                        sec("vector", v_mask)
                        exp_ses.extend(se0 + g + 1 for g in range(ngrp))
                        mask_ves.extend([0] * ngrp)
                        if causal:
                            mask_ves[ngrp - 2] = ve0_ + 1
                            mask_ves[ngrp - 1] = ve0_ + 2
                        sec("tensor", t_s)
                        for gb, gg in (("A", ngrp - 2 if ngrp >= 2 else 0),
                                       ("B", ngrp - 1)):
                            if gg >= 0 and gg < ngrp:
                                tgt = ("A" if gg % 2 == 0 else "B")
                                freed[tgt + "0"] = ("se", exp_ses[gg])
                                freed[tgt + "1"] = ("se", exp_ses[gg])

                        def v_av(eng, c=c, h=h, qc=qc, pvb=pvb, pvt=pvt,
                                 zoff=zoff):
                            wait(eng, "te")
                            wait(eng, "zr")
                            eng.reciprocal(ar2.f32(zoff, 512,
                                                   parts=slice(64, 65)),
                                           pvt[64:65, :])
                            eng.drain()
                            if h == 0:
                                i_ = eng.tensor_copy(
                                    AV_FM(c)[0:64, qc * 512:qc * 512 + 512],
                                    pvt[0:64, :])
                            else:
                                i_ = eng.tensor_copy(
                                    BTMP[:, qc * 512:qc * 512 + 512],
                                    pvt[0:64, :])
                            inc(i_, "ve", 1)
                            pm_consumed(pvb, "ve")
                        sec("vector", v_av)

                        def d_z(eng, c=c, h=h, qc=qc, zoff=zoff):
                            wait(eng, "ve")
                            i_ = eng.dma_start(
                                out=bcz[2 * c + h, qc * 512:qc * 512 + 512]
                                .unsqueeze(0),
                                in_=ar2.f32(zoff, 512,
                                            parts=slice(64, 65)))
                            inc(i_, "zr", 16)
                        sec("sync", d_z)

                def d_shift(eng, c=c):
                    wait(eng, "ve")
                    wait(eng, "zr")
                    i_ = eng.dma_start(out=ar3.bf16(c * 4096, T,
                                                    parts=slice(64, 128)),
                                       in_=BTMP[:])
                    inc(i_, "sh", 16)
                    i_ = eng.dma_start(
                        out=ar4.f32(12800, 2048, parts=slice(0, 64)),
                        in_=bcz[2 * c].unsqueeze(0).broadcast_to([64, T]))
                    inc(i_, "sh", 16)
                    i_ = eng.dma_start(
                        out=ar4.f32(12800, 2048, parts=slice(64, 128)),
                        in_=bcz[2 * c + 1].unsqueeze(0).broadcast_to([64, T]))
                    inc(i_, "sh", 16)
                sec("sync", d_shift)
                sem_of("sh")
                dxv2 = semc["sh"]

                def v_norm(eng, c=c, dxv2=dxv2):
                    eng.wait_ge(semh["sh"], dxv2)
                    eng.tensor_tensor(AV_FM(c), AV_FM(c), ZBC, ALU.mult)
                    eng.drain()
                    i_ = eng.tensor_scalar(AV_FM(c), AV_FM(c),
                                           bqk[:, 2 * ND + c:2 * ND + c + 1],
                                           None, ALU.add)
                    inc(i_, "ve", 1)
                sec("vector", v_norm)

            set_wstbar()

            def s_bobc(eng):
                wait(eng, "ve")
                wait(eng, "te")
                wait(eng, "sh")
                i_ = eng.dma_start(out=BOBC, in_=io[pfx + "bo"].unsqueeze(0)
                                   .broadcast_to([128, D]))
                inc(i_, "bc", 16)
            sec("sync", s_bobc)
            sem_of("bc")
            bobc_dxv = semc["bc"]

            for oq in range(4):
                def g_wo(eng, oq=oq):
                    fetch(eng, WS((oq % 2) * 4096, ND * 256).rearrange(
                        "p (i c) -> p i c", i=ND), io[pfx + "wo"][oq],
                        f"wo{oq % 2}")
                sec("gpsimd", g_wo)

                veO0 = semc["ve"]

                def t_o(eng, oq=oq, veO0=veO0):
                    fwait(eng, f"wo{oq % 2}")
                    if oq == 0:
                        wait(eng, "ve")
                    for tc in range(NT):
                        bank = ROT4[tc % 4]
                        if tc < 4:
                            pm_produce(eng, bank)
                        else:
                            eng.wait_ge(semh["ve"], veO0 + tc - 3)
                        i_ = None
                        for dc in range(ND):
                            i_ = eng.matmul(
                                PS[bank][:, 0:256],
                                AV_FM(dc)[:, tc * 128:tc * 128 + 128],
                                WS((oq % 2) * 4096, ND * 256).rearrange(
                                    "p (i c) -> p i c", i=ND)[:, dc, :],
                                start=(dc == 0), stop=(dc == ND - 1))
                        inc(i_, "te", 1)

                def v_o(eng, oq=oq, bobc_dxv=bobc_dxv):
                    eng.wait_ge(semh["bc"], bobc_dxv)
                    base = semc["te"] - NT
                    for tc in range(NT):
                        bank = ROT4[tc % 4]
                        eng.wait_ge(semh["te"], base + tc + 1)
                        eng.tensor_tensor(X_T(tc)[:, oq * 256:oq * 256 + 256],
                                          X_T(tc)[:, oq * 256:oq * 256 + 256],
                                          PS[bank][:, 0:256], ALU.add)
                        eng.drain()
                        i_ = eng.tensor_tensor(
                            X_T(tc)[:, oq * 256:oq * 256 + 256],
                            X_T(tc)[:, oq * 256:oq * 256 + 256],
                            BOBC[:, oq * 256:oq * 256 + 256], ALU.add)
                        inc(i_, "ve", 1)
                        pm_consumed(bank, "ve")

                sec("tensor", t_o)
                freed[f"wo{oq % 2}"] = ("te", semc["te"])
                sec("vector", v_o)

        def emit_ffn():
            ve0 = emit_ln(ar2.f32(0, D))
            lnve = semc["ve"]

            def d_spill(eng, lnve=lnve):
                eng.wait_ge(semh["ve"], lnve)
                wait(eng, "se")
                i_ = eng.dma_start(
                    out=xres.rearrange("(i p) d -> p i d", p=128),
                    in_=ar1.f32(0, NT * D).rearrange("p (i d) -> p i d", i=NT))
                inc(i_, "xl", 16)
            sec("sync", d_spill)
            sem_of("xl")
            spill_dxv = semc["xl"]

            for half in range(2):
                hb = half * 8
                set_wstbar()
                veT0 = semc["ve"]

                def t_trh(eng, half=half, hb=hb, spill_dxv=spill_dxv, veT0=veT0):
                    if half == 0:
                        eng.wait_ge(semh["xl"], spill_dxv)
                    else:
                        wait(eng, "se")
                    for ti in range(8):
                        eng.wait_ge(semh["ve"], veT0 + ti)
                        i_ = None
                        for dc in range(ND):
                            i_ = eng.transpose(
                                ptr[:, dc * 128:dc * 128 + 128],
                                XH_TM(hb + ti)[:, dc * 128:dc * 128 + 128],
                                ident[:])
                        inc(i_, "te", 1)

                def v_trh(eng, half=half):
                    base = semc["te"] - 8
                    for ti in range(8):
                        eng.wait_ge(semh["te"], base + ti + 1)
                        i_ = None
                        for dc in range(ND):
                            i_ = eng.tensor_copy(
                                XHF_H(dc)[:, ti * 128:ti * 128 + 128],
                                ptr[:, dc * 128:dc * 128 + 128])
                        inc(i_, "ve", 1)

                sec("tensor", t_trh)
                sec("vector", v_trh)
                trve = semc["ve"]

                for oc in range(NDH):
                    def g_w12(eng, oc=oc):
                        fetch(eng, WS((oc % 2) * 2048, ND * 128).rearrange(
                            "p (i c) -> p i c", i=ND), io["f0_w1t"][oc],
                            f"w1{oc % 2}")
                        fetch(eng, WS(4096 + (oc % 2) * 2048, ND * 128).rearrange(
                            "p (i c) -> p i c", i=ND), io["f0_w2t"][oc],
                            f"w2{oc % 2}")
                    sec("gpsimd", g_w12)

                    def t_12(eng, oc=oc, trve=trve):
                        fwait(eng, f"w1{oc % 2}", f"w2{oc % 2}")
                        if oc == 0:
                            eng.wait_ge(semh["ve"], trve)
                        for sc in range(2):
                            for w in range(2):
                                bank = ROT4[2 * sc + w]
                                pm_produce(eng, bank)
                                i_ = None
                                for ic in range(ND):
                                    i_ = eng.matmul(
                                        PS[bank],
                                        WS(w * 4096 + (oc % 2) * 2048,
                                           ND * 128).rearrange(
                                            "p (i c) -> p i c", i=ND)[:, ic, :],
                                        XHF_H(ic)[:, sc * 512:sc * 512 + 512],
                                        start=(ic == 0), stop=(ic == ND - 1))
                                inc(i_, "te", 1)

                    def s_sil(eng, oc=oc):
                        if oc == 0:
                            wait(eng, "pp")
                        base = semc["te"] - 4
                        for sc in range(2):
                            eng.wait_ge(semh["te"], base + 2 * sc + 1)
                            i_ = eng.activation(
                                SILT(sc), PS[ROT4[2 * sc]], AF.Silu,
                                scale=f0s[:, oc:oc + 1])
                            inc(i_, "se", 1)

                    def v_h(eng, oc=oc, half=half):
                        base_t = semc["te"] - 4
                        base_s = semc["se"] - 2
                        for sc in range(2):
                            eng.wait_ge(semh["te"], base_t + 2 * sc + 2)
                            eng.wait_ge(semh["se"], base_s + sc + 1)
                            i_ = eng.scalar_tensor_tensor(
                                H_T(oc)[:, sc * 512:sc * 512 + 512],
                                PS[ROT4[2 * sc + 1]],
                                f0s[:, NDH + oc:NDH + oc + 1], SILT(sc),
                                ALU.mult, ALU.mult)
                            inc(i_, "ve", 1)
                            pm_consumed(ROT4[2 * sc], "ve")
                            pm_consumed(ROT4[2 * sc + 1], "ve")

                    sec("tensor", t_12)
                    freed[f"w1{oc % 2}"] = ("te", semc["te"])
                    freed[f"w2{oc % 2}"] = ("te", semc["te"])
                    sec("scalar", s_sil)
                    sec("vector", v_h)

                set_wstbar()
                for stage in range(2):
                    wname = "f0_h1t" if stage == 0 else "f0_h2t"
                    soff = 2 * NDH if stage == 0 else 3 * NDH
                    srcT = H_T if stage == 0 else G_T
                    dstT = G_T if stage == 0 else H_T
                    for oc in range(NDH):
                        def g_wh(eng, oc=oc, wname=wname):
                            fetch(eng, WS((oc % 2) * 8192, NDH * 128).rearrange(
                                "p (i c) -> p i c", i=NDH), io[wname][oc],
                                f"wh{oc % 2}")
                        sec("gpsimd", g_wh)

                        def t_h(eng, oc=oc, srcT=srcT):
                            fwait(eng, f"wh{oc % 2}")
                            if oc == 0:
                                wait(eng, "ve")
                                wait(eng, "se")
                            for sc in range(2):
                                bank = ROT4[(2 * oc + sc) % 4]
                                pm_produce(eng, bank)
                                i_ = None
                                for ic in range(NDH):
                                    i_ = eng.matmul(
                                        PS[bank],
                                        WS((oc % 2) * 8192, NDH * 128).rearrange(
                                            "p (i c) -> p i c", i=NDH)[:, ic, :],
                                        srcT(ic)[:, sc * 512:sc * 512 + 512],
                                        start=(ic == 0), stop=(ic == NDH - 1))
                                inc(i_, "te", 1)

                        def s_g(eng, oc=oc, soff=soff, dstT=dstT):
                            base = semc["te"] - 2
                            for sc in range(2):
                                bank = ROT4[(2 * oc + sc) % 4]
                                eng.wait_ge(semh["te"], base + sc + 1)
                                i_ = eng.activation(
                                    dstT(oc)[:, sc * 512:sc * 512 + 512],
                                    PS[bank], AF.Gelu,
                                    scale=f0s[:, soff + oc:soff + oc + 1])
                                inc(i_, "se", 1)
                                pm_consumed(bank, "se")

                        sec("tensor", t_h)
                        freed[f"wh{oc % 2}"] = ("te", semc["te"])
                        sec("scalar", s_g)

                set_wstbar()
                seW0 = semc["se"]
                for op_ in range(4):
                    def g_w3(eng, op_=op_):
                        for k in range(2):
                            fetch(eng, WS(k * 8192, NDH * 128)
                                  .rearrange("p (i c) -> p i c", i=NDH),
                                  io["f0_w3t"][2 * op_ + k], f"w3{k}")
                    sec("gpsimd", g_w3)

                    for k in range(2):
                        ob = 2 * op_ + k
                        for sc in range(2):
                            bank = ROT4[(2 * ob + sc) % 4]

                            def t_3(eng, k=k, sc=sc, op_=op_, bank=bank):
                                fwait(eng, "w30", "w31")
                                if k == 0 and sc == 0 and op_ == 0:
                                    wait(eng, "se")
                                pm_produce(eng, bank)
                                i_ = None
                                for hc in range(NDH):
                                    i_ = eng.matmul(
                                        PS[bank],
                                        WS(k * 8192, NDH * 128).rearrange(
                                            "p (i c) -> p i c", i=NDH)[:, hc, :],
                                        H_T(hc)[:, sc * 512:sc * 512 + 512],
                                        start=(hc == 0), stop=(hc == NDH - 1))
                                inc(i_, "te", 1)

                            def s_3(eng, ob=ob, sc=sc, bank=bank):
                                wait(eng, "te")
                                i_ = eng.activation(
                                    Y_FMF(ob)[:, sc * 512:sc * 512 + 512],
                                    PS[bank], AF.Identity,
                                    scale=f0s[:, 4 * NDH + ob:4 * NDH + ob + 1])
                                inc(i_, "se", 1)
                                pm_consumed(bank, "se")

                            sec("tensor", t_3)
                            sec("scalar", s_3)
                    freed["w30"] = ("te", semc["te"])
                    freed["w31"] = ("te", semc["te"])

                veW0 = semc["ve"]
                TRB = [("av", "ax"), ("A0", "A1")]
                for tb in range(8):
                    tc = hb + tb

                    def d_xin(eng, tc=tc, tb=tb):
                        wait(eng, "xl")
                        wait(eng, f"xo{tb % 2}")
                        i_ = eng.dma_start(
                            out=XIO_F(tb % 2)[:],
                            in_=xres[tc * 128:tc * 128 + 128, :])
                        inc(i_, f"xi{tb % 2}", 16)
                    sec("sync", d_xin)
                    sem_of(f"xi{tb % 2}")
                    dxv = semc[f"xi{tb % 2}"]

                    b0n, b1n = TRB[tb % 2]

                    def t_w3tr(eng, tb=tb, b0n=b0n, b1n=b1n, veW0=veW0,
                               seW0=seW0):
                        if tb == 0:
                            eng.wait_ge(semh["se"], seW0 + 16)
                            pm_produce(eng, "av")
                            pm_produce(eng, "ax")
                        if tb == 1:
                            pm_produce(eng, "A0")
                            pm_produce(eng, "A1")
                        if tb >= 2:
                            eng.wait_ge(semh["ve"], veW0 + tb - 1)
                        i_ = None
                        for ob in range(8):
                            dst = PS[b0n] if ob < 4 else PS[b1n]
                            i_ = eng.transpose(
                                dst[:, (ob % 4) * 128:(ob % 4) * 128 + 128],
                                Y_FMF(ob)[:, tb * 128:tb * 128 + 128],
                                identf[:])
                        inc(i_, "te", 1)

                    def v_3n(eng, tb=tb, dxv=dxv, b0n=b0n, b1n=b1n):
                        eng.wait_ge(semh[f"xi{tb % 2}"], dxv)
                        wait(eng, "te")
                        eng.tensor_tensor(XIO_F(tb % 2)[:, 0:512],
                                          XIO_F(tb % 2)[:, 0:512],
                                          PS[b0n], ALU.add)
                        i_ = eng.tensor_tensor(XIO_F(tb % 2)[:, 512:1024],
                                               XIO_F(tb % 2)[:, 512:1024],
                                               PS[b1n], ALU.add)
                        inc(i_, "ve", 1)

                    def d_xout(eng, tc=tc, tb=tb):
                        wait(eng, "ve")
                        i_ = eng.dma_start(
                            out=xres[tc * 128:tc * 128 + 128, :],
                            in_=XIO_F(tb % 2)[:])
                        inc(i_, f"xo{tb % 2}", 16)

                    sec("tensor", t_w3tr)
                    sec("vector", v_3n)
                    sec("sync", d_xout)
                freed["av"] = ("ve", semc["ve"])
                freed["ax"] = ("ve", semc["ve"])
                freed["A0"] = ("ve", semc["ve"])
                freed["A1"] = ("ve", semc["ve"])

            def d_xre(eng):
                wait(eng, "te")
                wait(eng, "xo0")
                wait(eng, "xo1")
                i_ = eng.dma_start(
                    out=ar1.f32(0, NT * D).rearrange("p (i d) -> p i d", i=NT),
                    in_=xres.rearrange("(i p) d -> p i d", p=128))
                inc(i_, "xl", 16)
            sec("sync", d_xre)

        def emit_moe():
            set_wstbar()
            ve0 = emit_ln(ar2.f32(0, D))
            lnve = semc["ve"]

            def d_spill4(eng, lnve=lnve):
                eng.wait_ge(semh["ve"], lnve)
                wait(eng, "se")
                i_ = eng.dma_start(
                    out=xres.rearrange("(i p) d -> p i d", p=128),
                    in_=ar1.f32(0, NT * D).rearrange("p (i d) -> p i d", i=NT))
                inc(i_, "xl", 16)
                i_ = eng.dma_start(
                    out=xag_in.rearrange("(i p) d -> p i d", p=128),
                    in_=ar3.bf16(0, NT * D).rearrange("p (i d) -> p i d", i=NT))
                inc(i_, "xg", 16)
            sec("sync", d_spill4)
            sem_of("xl"); sem_of("xg")
            spillv = semc["xl"]
            xgv = semc["xg"]

            def g_agx(eng, xgv=xgv):
                eng.wait_ge(semh["xg"], xgv)
                i_ = eng.collective_compute(
                    "AllGather", ALU.bypass, replica_groups=[list(range(8))],
                    ins=[xag_in_t.ap().opt()], outs=[xall_t.ap().opt()])
                inc(i_, "ag", 1)
            sec("gpsimd", g_agx)
            sem_of("ag")
            agx_v = semc["ag"]

            veE0 = semc["ve"]

            def t_tr4(eng, veE0=veE0, lnve=lnve):
                for ti in range(NT):
                    eng.wait_ge(semh["ve"], lnve - NT + ti + 1)
                    eng.wait_ge(semh["ve"], veE0 + ti)
                    i_ = None
                    for dc in range(ND):
                        i_ = eng.transpose(
                            ptr[:, dc * 128:dc * 128 + 128],
                            XH_TM(ti)[:, dc * 128:dc * 128 + 128], ident[:])
                    inc(i_, "te", 1)

            def v_ev4(eng, spillv=spillv, xgv=xgv):
                base = semc["te"] - NT
                for ti in range(NT):
                    if ti == 0:
                        eng.wait_ge(semh["xl"], spillv)
                        eng.wait_ge(semh["xg"], xgv)
                    eng.wait_ge(semh["te"], base + ti + 1)
                    i_ = None
                    for dc in range(ND):
                        i_ = eng.tensor_copy(
                            XHM_FM(dc)[:, ti * 128:ti * 128 + 128],
                            ptr[:, dc * 128:dc * 128 + 128])
                    inc(i_, "ve", 1)

            sec("tensor", t_tr4)
            sec("vector", v_ev4)
            xhfm_ve = semc["ve"]

            def g_rwt(eng):
                fetch(eng, wst.bf16(12 * KB, ND * 8).rearrange(
                    "p (i c) -> p i c", i=ND),
                    io["m_rwt"].rearrange("(i p) e -> p i e", p=128), "rwt")
            sec("gpsimd", g_rwt)

            veR0 = semc["ve"]
            seR0 = semc["se"]

            def t_rt(eng, xhfm_ve=xhfm_ve, veR0=veR0):
                fwait(eng, "rwt")
                eng.wait_ge(semh["ve"], xhfm_ve)
                for tc in range(NT):
                    bank = ["ax", "ex"][tc % 2]
                    if tc < 2:
                        pm_produce(eng, bank)
                    else:
                        eng.wait_ge(semh["ve"], veR0 + (tc - 2) * 5 + 1)
                    i_ = None
                    for dc in range(ND):
                        i_ = eng.matmul(
                            PS[bank][:, 0:8],
                            XHM_FM(dc)[:, tc * 128:tc * 128 + 128],
                            wst.bf16(12 * KB, ND * 8).rearrange(
                                "p (i c) -> p i c", i=ND)[:, dc, :],
                            start=(dc == 0), stop=(dc == ND - 1))
                    inc(i_, "te", 1)

            def v_sm1(eng, veR0=veR0, seR0=seR0):
                base = semc["te"] - NT
                for tc in range(NT):
                    bank = ["ax", "ex"][tc % 2]
                    eng.wait_ge(semh["te"], base + tc + 1)
                    if tc:
                        eng.wait_ge(semh["se"], seR0 + tc)
                    eng.tensor_copy(sm2[:, C_LG:C_LG + 8], PS[bank][:, 0:8])
                    eng.drain()
                    i_ = eng.reduce_max(small[:, S_NMAX:S_NMAX + 1],
                                        sm2[:, C_LG:C_LG + 8], axis=AX.X,
                                        negate=True)
                    inc(i_, "ve", 1)
                    pm_consumed(bank, "ve")
                    eng.wait_ge(semh["se"], seR0 + tc + 1)
                    eng.reciprocal(small[:, S_PSTAR + tc:S_PSTAR + tc + 1],
                                   sm2[:, C_Z8:C_Z8 + 1])
                    eng.drain()
                    eng.max(sm2[:, C_MX8:C_MX8 + 8], sm2[:, C_LG:C_LG + 8])
                    eng.drain()
                    eng.max_index(smallu[:, 0:8], sm2[:, C_MX8:C_MX8 + 8],
                                  sm2[:, C_LG:C_LG + 8])
                    eng.drain()
                    i_ = eng.tensor_copy(
                        small[:, S_ESTAR + tc:S_ESTAR + tc + 1],
                        smallu[:, 0:1])
                    inc(i_, "ve", 4)

            def s_sm(eng, veR0=veR0):
                for tc in range(NT):
                    eng.wait_ge(semh["ve"], veR0 + tc * 5 + 1)
                    i_ = eng.activation(sm2[:, C_PROB:C_PROB + 8],
                                        sm2[:, C_LG:C_LG + 8], AF.Exp,
                                        bias=small[:, S_NMAX:S_NMAX + 1],
                                        accum_out=sm2[:, C_Z8:C_Z8 + 1])
                    inc(i_, "se", 1)

            sec("tensor", t_rt)
            sec("scalar", s_sm)
            sec("vector", v_sm1)
            argve = semc["ve"]

            def t_er(eng, argve=argve):
                eng.wait_ge(semh["ve"], argve)
                pm_produce(eng, "ax")
                i_ = eng.transpose(pax[0:16, 0:128],
                                   small[:, S_ESTAR:S_ESTAR + 16], identf[:])
                inc(i_, "te", 1)
            sec("tensor", t_er)

            def v_er(eng):
                wait(eng, "te")
                i_ = eng.tensor_copy(er16[:, 0:128], pax[0:16, 0:128])
                inc(i_, "ve", 1)
                pm_consumed("ax", "ve")
            sec("vector", v_er)
            erve = semc["ve"]

            def d_er(eng, erve=erve):
                eng.wait_ge(semh["ve"], erve)
                i_ = eng.dma_start(
                    out=bcm[0:1, :].rearrange("o (t p) -> o t p", p=128),
                    in_=er16[:, 0:128])
                inc(i_, "mo", 16)
                wait(eng, "mo")
                i_ = eng.dma_start(out=EROW8,
                                   in_=bcm[0:1, :].broadcast_to([8, T]))
                inc(i_, "mo", 16)
            sec("sync", d_er)
            sem_of("mo")
            erow_mo = semc["mo"]

            def v_oh(eng, erow_mo=erow_mo):
                eng.wait_ge(semh["mo"], erow_mo)
                wait(eng, "se")
                eng.tensor_scalar(ONEHOT, EROW8, ciota[0:8, 0:1], None,
                                  ALU.is_equal)
                eng.drain()
                eng.tensor_tensor_scan(CNTI, ONEHOT, ONEHOT, 0.0, ALU.add,
                                       ALU.max)
                eng.drain()
                i_ = eng.tensor_tensor(TMP8, ONEHOT, CNTI, ALU.mult)
                inc(i_, "ve", 1)
            sec("vector", v_oh)
            scanve = semc["ve"]

            def g_cc(eng, scanve=scanve):
                eng.wait_ge(semh["ve"], scanve)
                i_ = eng.dma_start(out=cc_in.ap().unsqueeze(1),
                                   in_=CNTI[:, T - 1:T])
                inc(i_, "ci", 16)
                wait(eng, "ci")
                i_ = eng.collective_compute(
                    "AllGather", ALU.bypass,
                    replica_groups=[list(range(8))],
                    ins=[cc_in.ap().opt()], outs=[cc_out.ap().opt()])
                inc(i_, "ag", 1)
                wait(eng, "ag")
                i_ = eng.dma_start(
                    out=agg[:], in_=cc_out.ap().rearrange("(r e) -> r e", e=8))
                inc(i_, "co", 16)
            sec("gpsimd", g_cc)
            sem_of("co")
            aggco = semc["co"]

            def t_off(eng, aggco=aggco):
                eng.wait_ge(semh["co"], aggco)
                wait(eng, "pp")
                pm_produce(eng, "ax")
                i_ = eng.matmul(pax[0:8, 0:1], agg[:], rmk[:],
                                start=True, stop=True)
                inc(i_, "te", 1)
                pm_produce(eng, "ex")
                i_ = eng.matmul(pex[0:1, 0:1], esel_sb[:], ciota[0:8, 0:1],
                                start=True, stop=True)
                inc(i_, "te", 1)
            sec("tensor", t_off)

            def v_off(eng):
                wait(eng, "te")
                eng.tensor_copy(off8[:], pax[0:8, 0:1])
                eng.drain()
                eng.tensor_scalar(small[0:1, S_TMP + 8:S_TMP + 9],
                                  pex[0:1, 0:1], float(8 * T), None,
                                  ALU.mult)
                eng.drain()
                i_ = eng.tensor_copy(smallu[0:1, 40:41],
                                     small[0:1, S_TMP + 8:S_TMP + 9])
                inc(i_, "ve", 1)
                pm_consumed("ax", "ve")
                pm_consumed("ex", "ve")
            sec("vector", v_off)

            veQ0 = semc["ve"]

            def t_q(eng, veQ0=veQ0):
                wait(eng, "ve")
                for q in range(4):
                    bank = ["ax", "ex"][q % 2]
                    if q < 2:
                        pm_produce(eng, bank)
                    else:
                        eng.wait_ge(semh["ve"], veQ0 + q - 1)
                    eng.matmul(PS[bank][0:1, 0:512], ones8[:],
                               TMP8[:, q * 512:q * 512 + 512],
                               start=True, stop=True)
                    i_ = eng.matmul(PS[bank][32:33, 0:512], off8[:],
                                    ONEHOT[:, q * 512:q * 512 + 512],
                                    start=True, stop=True)
                    inc(i_, "te", 1)

            def v_q(eng):
                base = semc["te"] - 4
                for q in range(4):
                    bank = ["ax", "ex"][q % 2]
                    eng.wait_ge(semh["te"], base + q + 1)
                    eng.tensor_copy(RANKR[:, q * 512:q * 512 + 512],
                                    PS[bank][0:1, 0:512])
                    eng.drain()
                    i_ = eng.tensor_copy(OFFR[:, q * 512:q * 512 + 512],
                                         PS[bank][32:33, 0:512])
                    inc(i_, "ve", 1)
                    pm_consumed(bank, "ve")
            sec("tensor", t_q)
            sec("vector", v_q)
            rowve = semc["ve"]

            def d_rc(eng, rowve=rowve):
                eng.wait_ge(semh["ve"], rowve)
                i_ = eng.dma_start(
                    out=er16[:, 0:128],
                    in_=RANKR.rearrange("o (t p) -> o t p", p=128))
                inc(i_, "mo", 16)
                i_ = eng.dma_start(
                    out=er16[:, 128:256],
                    in_=OFFR.rearrange("o (t p) -> o t p", p=128))
                inc(i_, "mo", 16)
                i_ = eng.dma_start(
                    out=small[:, S_TMP + 9:S_TMP + 10],
                    in_=io["cbase"].unsqueeze(0).broadcast_to([128, 1]))
                inc(i_, "mo", 16)
            sec("sync", d_rc)
            rc_mo = semc["mo"]

            def t_rc(eng, rc_mo=rc_mo):
                eng.wait_ge(semh["mo"], rc_mo)
                pm_produce(eng, "ax")
                eng.transpose(pax[:, 0:16], er16[0:16, 0:128],
                              identf[0:16, 0:16])
                i_ = eng.transpose(pax[:, 16:32], er16[0:16, 128:256],
                                   identf[0:16, 0:16])
                inc(i_, "te", 1)
            sec("tensor", t_rc)

            def v_rc(eng):
                wait(eng, "te")
                eng.tensor_copy(sm2[:, C_RANK:C_RANK + 16], pax[:, 0:16])
                eng.drain()
                i_ = eng.tensor_copy(sm2[:, C_OFFT:C_OFFT + 16], pax[:, 16:32])
                inc(i_, "ve", 1)
                pm_consumed("ax", "ve")
            sec("vector", v_rc)
            rc_ve = semc["ve"]

            def v_cols(eng, rc_ve=rc_ve):
                eng.wait_ge(semh["ve"], rc_ve)
                eng.tensor_scalar(sm2[:, C_RANK:C_RANK + 16],
                                  sm2[:, C_RANK:C_RANK + 16], 1.0, None,
                                  ALU.subtract)
                eng.drain()
                eng.tensor_tensor(sm2[:, C_OFFT:C_OFFT + 16],
                                  sm2[:, C_OFFT:C_OFFT + 16],
                                  sm2[:, C_RANK:C_RANK + 16], ALU.add)
                eng.drain()
                eng.tensor_scalar(sm2[:, C_KEPT:C_KEPT + 16],
                                  sm2[:, C_OFFT:C_OFFT + 16],
                                  float(CAPACITY), None, ALU.is_lt)
                eng.drain()
                eng.tensor_tensor(small[:, S_G:S_G + 16],
                                  small[:, S_PSTAR:S_PSTAR + 16],
                                  sm2[:, C_KEPT:C_KEPT + 16], ALU.mult)
                eng.drain()
                eng.tensor_scalar(sm2[:, C_SLOTP:C_SLOTP + 16],
                                  small[:, S_ESTAR:S_ESTAR + 16],
                                  float(8 * T), None, ALU.mult)
                eng.drain()
                eng.tensor_tensor(sm2[:, C_SLOTP:C_SLOTP + 16],
                                  sm2[:, C_SLOTP:C_SLOTP + 16],
                                  sm2[:, C_OFFT:C_OFFT + 16], ALU.add)
                eng.drain()
                eng.tensor_scalar(sm2[:, C_SLSH:C_SLSH + 16],
                                  sm2[:, C_OFFT:C_OFFT + 16],
                                  float(CAPACITY - 1), None, ALU.min)
                eng.drain()
                eng.tensor_scalar(sm2[:, C_KEPT:C_KEPT + 16],
                                  small[:, S_ESTAR:S_ESTAR + 16],
                                  float(CAPACITY), None, ALU.mult)
                eng.drain()
                eng.tensor_tensor(sm2[:, C_SLSH:C_SLSH + 16],
                                  sm2[:, C_SLSH:C_SLSH + 16],
                                  sm2[:, C_KEPT:C_KEPT + 16], ALU.add)
                eng.drain()
                eng.tensor_scalar(sm2[:, C_LG:C_LG + 16], ciota[:, 0:16],
                                  small[:, S_TMP + 9:S_TMP + 10], None,
                                  ALU.add)
                eng.drain()
                eng.tensor_copy(smallu[:, 8:24], sm2[:, C_SLOTP:C_SLOTP + 16])
                eng.drain()
                i_ = eng.tensor_copy(smallu[:, 24:40],
                                     sm2[:, C_SLSH:C_SLSH + 16])
                inc(i_, "ve", 1)
            sec("vector", v_cols)
            colve = semc["ve"]

            def g_idx(eng, colve=colve):
                for k in range(4):
                    i_ = eng.dma_start(
                        out=idxb[k * 32768:(k + 1) * 32768].rearrange(
                            "(p c) -> p c", p=128),
                        in_=ZEROT[:, 0:256])
                    inc(i_, "ix", 16)
                wait(eng, "ix")
                eng.wait_ge(semh["ve"], colve)
                for tc in range(NT):
                    i_ = eng.indirect_dma_start(
                        out=idxb.unsqueeze(1),
                        out_offset=bass.IndirectOffsetOnAxis(
                            smallu[:, 8 + tc:9 + tc], 0),
                        in_=sm2[:, C_LG + tc:C_LG + tc + 1], in_offset=None)
                    inc(i_, "ix", 16)
                wait(eng, "ix")
                i_ = eng.collective_compute(
                    "AllReduce", ALU.add, replica_groups=[list(range(8))],
                    ins=[idxb_t.ap().opt()], outs=[idxall_t.ap().opt()])
                inc(i_, "ag", 1)
            sec("gpsimd", g_idx)
            sem_of("ag")
            idx_ag = semc["ag"]

            def g_ld(eng, idx_ag=idx_ag):
                eng.wait_ge(semh["ag"], idx_ag)
                wait(eng, "ve")
                eng.reg_load(greg, smallu[0:1, 40:41])
                i_ = eng.dma_start(
                    out=i24[:],
                    in_=bass.AP(idxall_t, greg, [[128, 24], [1, 128]]))
                inc(i_, "il", 16)
            sec("gpsimd", g_ld)
            sem_of("il")
            il_v = semc["il"]

            def t_i24(eng, il_v=il_v):
                eng.wait_ge(semh["il"], il_v)
                pm_produce(eng, "ax")
                i_ = eng.transpose(pax[:, 0:24], i24[0:24, 0:128],
                                   identf[0:24, 0:24])
                inc(i_, "te", 1)
            sec("tensor", t_i24)

            def v_i24(eng):
                wait(eng, "te")
                i_ = eng.tensor_copy(u24[:], pax[:, 0:24])
                inc(i_, "ve", 1)
                pm_consumed("ax", "ve")
            sec("vector", v_i24)
            u24ve = semc["ve"]

            def g_gx(eng, u24ve=u24ve, agx_v=agx_v):
                eng.wait_ge(semh["ve"], u24ve)
                eng.wait_ge(semh["ag"], agx_v)
                wait(eng, "te")
                for sc in range(NCC):
                    i_ = eng.indirect_dma_start(
                        out=XB_TM(sc), out_offset=None,
                        in_=xall,
                        in_offset=bass.IndirectOffsetOnAxis(
                            u24[:, sc:sc + 1], 0))
                    inc(i_, "gx", 16)
            sec("gpsimd", g_gx)
            sem_of("gx")
            gx_all = semc["gx"]

            veB0 = semc["ve"]

            def t_trb(eng, gx_all=gx_all, veB0=veB0):
                for ti in range(NCC):
                    eng.wait_ge(semh["gx"], gx_all)
                    eng.wait_ge(semh["ve"], veB0 + ti)
                    i_ = None
                    for dc in range(ND):
                        i_ = eng.transpose(
                            ptr[:, dc * 128:dc * 128 + 128],
                            XB_TM(ti)[:, dc * 128:dc * 128 + 128], ident[:])
                    inc(i_, "te", 1)

            def v_evb(eng):
                base = semc["te"] - NCC
                for ti in range(NCC):
                    if ti == 0:
                        wait(eng, "mo")
                        wait(eng, "ix")
                    eng.wait_ge(semh["te"], base + ti + 1)
                    i_ = None
                    for dc in range(ND):
                        i_ = eng.tensor_copy(
                            XB_FM(dc)[:, ti * 128:ti * 128 + 128],
                            ptr[:, dc * 128:dc * 128 + 128])
                    inc(i_, "ve", 1)

            sec("tensor", t_trb)
            sec("vector", v_evb)
            xbve = semc["ve"]

            if DBG_MOE:
                def g_dbg(eng, xbve=xbve):
                    eng.wait_ge(semh["ve"], xbve)
                    i_ = eng.dma_start(
                        out=dbg_xb.rearrange("(s p) d -> p s d", p=128),
                        in_=ar1.bf16(0, 4 * 1024).rearrange(
                            "p (s d) -> p s d", s=4))
                    inc(i_, "dbg", 16)
                    i_ = eng.dma_start(out=dbg_u24, in_=u24[:])
                    inc(i_, "dbg", 16)
                    i_ = eng.dma_start(out=dbg_i24, in_=i24[:])
                    inc(i_, "dbg", 16)
                    i_ = eng.dma_start(
                        out=er16[0:16, 0:128],
                        in_=bass.AP(idxall_t, 0, [[128, 16], [1, 128]]))
                    inc(i_, "dbg", 16)
                    eng.reg_load(greg, smallu[0:1, 40:41])
                    i_ = eng.dma_start(
                        out=er16[0:16, 128:256],
                        in_=bass.AP(idxall_t, greg, [[128, 16], [1, 128]]))
                    inc(i_, "dbg", 16)
                    wait(eng, "dbg")
                    i_ = eng.dma_start(out=dbg_h[0:16, 0:256], in_=er16[:])
                    inc(i_, "dbg", 16)
                    i_ = eng.dma_start(out=dbg_h[100:101, 0:1],
                                       in_=smallu[0:1, 40:41])
                    inc(i_, "dbg", 16)
                    i_ = eng.dma_start(out=dbg_h[101:102, 0:1],
                                       in_=small[0:1, 91:92])
                    inc(i_, "dbg", 16)
                    wait(eng, "dbg")
                sec("gpsimd", g_dbg)

            def d_b2(eng):
                wait(eng, "ve")
                i_ = eng.dma_start(out=B2BC, in_=io["m_b2"].unsqueeze(0)
                                   .broadcast_to([128, D]))
                inc(i_, "b2", 16)
            sec("sync", d_b2)
            sem_of("b2")
            b2v = semc["b2"]

            set_wstbar()
            YB8 = ["A0", "A1", "B0", "B1", "av", "ax", "ex", "trf"]
            for grp in range(3):
                teG0 = semc["te"]
                seH0 = semc["se"]
                veG0 = semc["ve"]
                for fq in range(8):
                    def g_w1(eng, grp=grp, fq=fq):
                        fetch(eng, WS((fq % 2) * 8192, ND * 512).rearrange(
                            "p (i c) -> p i c", i=ND),
                            io["m_w1"][fq], f"mw{fq % 2}")
                    sec("gpsimd", g_w1)

                    def t_w1(eng, grp=grp, fq=fq, seH0=seH0, xbve=xbve,
                             veG0=veG0):
                        fwait(eng, f"mw{fq % 2}")
                        if grp == 0 and fq == 0:
                            eng.wait_ge(semh["ve"], xbve)
                        if grp > 0 and fq == 0:
                            eng.wait_ge(semh["ve"], veG0)
                        for fc in range(4):
                            for hh in range(2):
                                k = fq * 8 + fc * 2 + hh
                                bank = ROT4[k % 4]
                                if grp == 0 and fq == 0 and k < 4:
                                    pm_produce(eng, bank)
                                elif k >= 4:
                                    eng.wait_ge(semh["se"], seH0 + k - 3)
                                i_ = None
                                for dc in range(ND):
                                    i_ = eng.matmul(
                                        PS[bank],
                                        WS((fq % 2) * 8192, ND * 512).rearrange(
                                            "p (i c) -> p i c", i=ND)[
                                            :, dc, fc * 128:fc * 128 + 128],
                                        XB_FM(dc)[:, grp * 1024 + hh * 512:
                                                  grp * 1024 + hh * 512 + 512],
                                        start=(dc == 0), stop=(dc == ND - 1))
                                inc(i_, "te", 1)
                    sec("tensor", t_w1)
                    freed[f"mw{fq % 2}"] = ("te", semc["te"])

                    def s_h(eng, grp=grp, fq=fq):
                        if grp == 0 and fq == 0:
                            wait(eng, "pp")
                        base = semc["te"] - 8
                        for fc in range(4):
                            for hh in range(2):
                                k = fc * 2 + hh
                                bank = ROT4[(fq * 8 + k) % 4]
                                eng.wait_ge(semh["te"], base + k + 1)
                                fcg = fq * 4 + fc
                                i_ = eng.activation(
                                    HB_T(fcg)[:, hh * 512:hh * 512 + 512],
                                    PS[bank], AF.Gelu,
                                    bias=mb1[:, fcg:fcg + 1])
                                inc(i_, "se", 1)
                                pm_consumed(bank, "se")
                    sec("scalar", s_h)
                hready = semc["se"]

                for dh in range(2):
                    for fq2 in range(4):
                        def g_w2(eng, grp=grp, dh=dh, fq2=fq2):
                            fetch(eng, WS((fq2 % 2) * 8192, 8 * 512).rearrange(
                                "p (i c) -> p i c", i=8),
                                io["m_w2"][dh, fq2], f"mw{fq2 % 2}")
                        sec("gpsimd", g_w2)

                        def t_w2(eng, grp=grp, dh=dh, fq2=fq2, hready=hready):
                            fwait(eng, f"mw{fq2 % 2}")
                            eng.wait_ge(semh["se"], hready)
                            if fq2 == 0:
                                for cc in range(8):
                                    pm_produce(eng, YB8[cc])
                            for cc in range(8):
                                i_ = None
                                for fc in range(8):
                                    fcg = fq2 * 8 + fc
                                    i_ = eng.matmul(
                                        PS[YB8[cc]],
                                        HB_T(fcg)[:, cc * 128:cc * 128 + 128],
                                        WS((fq2 % 2) * 8192, 8 * 512).rearrange(
                                            "p (i c) -> p i c", i=8)[:, fc, :],
                                        start=(fq2 == 0 and fc == 0),
                                        stop=(fq2 == 3 and fc == 7))
                                inc(i_, "te", 1)
                        sec("tensor", t_w2)
                        freed[f"mw{fq2 % 2}"] = ("te", semc["te"])

                    def v_eo(eng, grp=grp, dh=dh, b2v=b2v):
                        eng.wait_ge(semh["b2"], b2v)
                        wait(eng, "eg")
                        wait(eng, "te")
                        for cc in range(8):
                            i_ = eng.tensor_tensor(
                                EOT3(cc)[:, dh * 512:dh * 512 + 512],
                                PS[YB8[cc]], B2BC[:, dh * 512:dh * 512 + 512],
                                ALU.add)
                            eng.drain()
                            inc(i_, "ve", 1)
                            pm_consumed(YB8[cc], "ve")
                    sec("vector", v_eo)

                def d_eog(eng, grp=grp):
                    sem_of("eg")
                    wait(eng, "ve")
                    i_ = eng.dma_start(
                        out=eog[grp * 1024:(grp + 1) * 1024, :].rearrange(
                            "(c p) d -> p c d", p=128),
                        in_=ar3.bf16(0, 8 * D).rearrange(
                            "p (c d) -> p c d", c=8))
                    inc(i_, "eg", 16)
                sec("sync", d_eog)

            def g_ageo(eng):
                wait(eng, "eg")
                i_ = eng.collective_compute(
                    "AllGather", ALU.bypass, replica_groups=[list(range(8))],
                    ins=[eog_t.ap().opt()], outs=[eoall_t.ap().opt()])
                inc(i_, "ag", 1)
            sec("gpsimd", g_ageo)
            eo_ag = semc["ag"]

            ysb = lambda b: ar3.bf16(16 * KB + b * 2048, D)
            out_ot = {0: 0, 1: 0}
            sem_of("o0"); sem_of("o1"); sem_of("yg")

            def g_gy(eng, eo_ag=eo_ag):
                eng.wait_ge(semh["ag"], eo_ag)
                wait(eng, "ve")
                for tc in range(NT):
                    if tc >= 2:
                        eng.wait_ge(semh["ve"], ygve0 + tc - 1)
                    i_ = eng.indirect_dma_start(
                        out=ysb(tc % 2), out_offset=None,
                        in_=eoall,
                        in_offset=bass.IndirectOffsetOnAxis(
                            smallu[:, 24 + tc:25 + tc], 0))
                    inc(i_, "yg", 16)
                    wait(eng, "yg")
                    i_ = eng.sem_inc(semh["ygr"], 1)
                    semc["ygr"] += 1
            sem_of("ygr")
            ygv0 = semc["yg"]
            ygve0 = semc["ve"]
            sec("gpsimd", g_gy)

            for tc in range(NT):
                def d_xin4(eng, tc=tc, po=out_ot[tc % 2]):
                    wait(eng, "xl")
                    wait(eng, "ve")
                    if po:
                        eng.wait_ge(semh[f"o{tc % 2}"], po)
                    i_ = eng.dma_start(
                        out=XIO_M(tc % 2),
                        in_=xres[tc * 128:tc * 128 + 128, :])
                    inc(i_, f"x4{tc % 2}", 16)
                sec("sync", d_xin4)
                sem_of(f"x4{tc % 2}")
                x4v = semc[f"x4{tc % 2}"]

                def v_y(eng, tc=tc, x4v=x4v):
                    eng.wait_ge(semh[f"x4{tc % 2}"], x4v)
                    eng.wait_ge(semh["ygr"], tc + 1)
                    i_ = eng.scalar_tensor_tensor(
                        XIO_M(tc % 2), ysb(tc % 2),
                        small[:, S_G + tc:S_G + tc + 1],
                        XIO_M(tc % 2), ALU.mult, ALU.add)
                    inc(i_, "ve", 1)
                sec("vector", v_y)

                def d_out4(eng, tc=tc):
                    wait(eng, "ve")
                    i_ = eng.dma_start(
                        out=out_t[tc * 128:tc * 128 + 128, :],
                        in_=XIO_M(tc % 2))
                    inc(i_, f"o{tc % 2}", 16)
                sec("sync", d_out4)
                out_ot[tc % 2] = semc[f"o{tc % 2}"]

            def d_fin(eng):
                wait(eng, "o0")
                wait(eng, "o1")
            sec("sync", d_fin)

        def emit_out_from_ar1():
            def d_out(eng):
                wait(eng, "ve")
                wait(eng, "te")
                wait(eng, "se")
                wait(eng, "xl")
                i_ = eng.dma_start(
                    out=out_t.rearrange("(i p) d -> p i d", p=128),
                    in_=ar1.f32(0, NT * D).rearrange("p (i d) -> p i d", i=NT))
                inc(i_, "ot", 16)
                wait(eng, "ot")
            sec("sync", d_out)

        if upto == "ln1":
            ve0 = emit_ln(ar2.f32(0, D))
            def v_cp(eng, ve0=ve0):
                wait(eng, "ve")
                for ti in range(NT):
                    i_ = eng.tensor_copy(X_T(ti), XH_TM(ti))
                    inc(i_, "ve", 1)
            sec("vector", v_cp)
            emit_out_from_ar1()
        elif upto == "attn0":
            emit_attention(0)
            emit_out_from_ar1()
        elif upto == "ffn0":
            emit_attention(0)
            emit_ffn()
            emit_out_from_ar1()
        elif upto == "attn1":
            emit_attention(0)
            emit_ffn()
            emit_attention(1)
            emit_out_from_ar1()
        else:
            emit_attention(0)
            emit_ffn()
            emit_attention(1)
            emit_moe()

    return nc


def kernel(**inputs):
    nc = build_nc(UPTO)
    in_maps = [prep_inputs(b, **inputs) for b in range(8)]
    res = run_bass_kernel_spmd(nc, in_maps, core_ids=list(range(8)))
    return np.stack([r["out"] for r in res.results], axis=0)


# version 06 (final)
# speedup vs baseline: 1.0781x; 1.0063x over previous
import contextlib
import numpy as np
import ml_dtypes
import concourse.bass as bass
import concourse.mybir as mybir
from concourse.bass_utils import run_bass_kernel_spmd

F32 = mybir.dt.float32
BF16 = mybir.dt.bfloat16
U32 = mybir.dt.uint32
AF = mybir.ActivationFunctionType
ALU = mybir.AluOpType
AX = mybir.AxisListType

T = 2048
D = 1024
NT = T // 128
ND = D // 128
H = 16
HD = 64
NP = 8
SCALE = HD ** -0.5
DH = 4096
NDH = DH // 128
E = 8
CC = 384
SLOTS = E * CC
NCC = SLOTS // 128
CAPACITY = 3072
EPS = 1e-5
KB = 1024

UPTO = "full"
DBG_MOE = False



def _blk_lhs(w):
    I, O = w.shape
    return np.ascontiguousarray(
        w.reshape(I // 128, 128, O // 128, 128).transpose(2, 1, 0, 3))


def _blk_rhs(w, ncol):
    I, O = w.shape
    return np.ascontiguousarray(
        w.reshape(I // 128, 128, O // ncol, ncol).transpose(2, 1, 0, 3))


def _cols(b):
    return np.ascontiguousarray(b.reshape(-1, 128).T)


def prep_inputs(core, **ins):
    g = lambda n: np.asarray(ins[n])
    d = {}
    d["x"] = np.ascontiguousarray(g("L0_attn_x")[core])
    d["rmask"] = (np.arange(8) < core).astype(np.float32)
    d["a0_wq"] = _blk_lhs(g("L0_attn_Wq"))
    d["a0_wk"] = _blk_lhs(g("L0_attn_Wk"))
    d["a0_wv"] = _blk_lhs(g("L0_attn_Wv"))
    d["a0_wo"] = _blk_rhs(g("L0_attn_Wo"), 256)
    d["a0_bqkv"] = np.concatenate(
        [_cols(g("L0_attn_bq")), _cols(g("L0_attn_bk")), _cols(g("L0_attn_bv"))],
        axis=1)
    d["a0_bo"] = np.asarray(g("L0_attn_bo"))
    d["f0_w1t"] = _blk_lhs(np.ascontiguousarray(g("L0_ffn_w1_w").T))
    d["f0_w2t"] = _blk_lhs(np.ascontiguousarray(g("L0_ffn_w2_w").T))
    d["f0_h1t"] = _blk_lhs(np.ascontiguousarray(g("L0_ffn_h1_w").T))
    d["f0_h2t"] = _blk_lhs(np.ascontiguousarray(g("L0_ffn_h2_w").T))
    d["f0_w3t"] = _blk_rhs(np.ascontiguousarray(g("L0_ffn_w3_w").T), 128)
    d["f0_s"] = np.concatenate(
        [_cols(g("L0_ffn_w1_s")), _cols(g("L0_ffn_w2_s")),
         _cols(g("L0_ffn_h1_s")), _cols(g("L0_ffn_h2_s")),
         _cols(g("L0_ffn_w3_s"))], axis=1)
    d["a1_wq"] = _blk_lhs(g("L1_attn_Wq"))
    d["a1_wl"] = _blk_lhs(g("L1_attn_Wl"))
    d["a1_wk"] = _blk_lhs(g("L1_attn_Wk"))
    d["a1_wv"] = _blk_lhs(g("L1_attn_Wv"))
    d["a1_wo"] = _blk_rhs(g("L1_attn_Wo"), 256)
    d["a1_bqkv"] = np.concatenate(
        [_cols(g("L1_attn_bq")), _cols(g("L1_attn_bk")), _cols(g("L1_attn_bv")),
         _cols(g("L1_attn_bl"))], axis=1)
    d["a1_bo"] = np.asarray(g("L1_attn_bo"))
    d["m_rwt"] = np.ascontiguousarray(g("L1_ffn_router_w").T)
    w1 = g("L1_ffn_w1")[core]
    d["m_w1"] = np.ascontiguousarray(
        w1.reshape(ND, 128, 8, 512).transpose(2, 1, 0, 3))
    w2 = g("L1_ffn_w2")[core]
    d["m_w2"] = np.ascontiguousarray(
        w2.reshape(4, 8, 128, 2, 512).transpose(3, 0, 2, 1, 4))
    d["m_b1"] = np.ascontiguousarray(
        g("L1_ffn_b1")[core].reshape(NDH, 128).T)
    d["m_b2"] = np.ascontiguousarray(g("L1_ffn_b2")[core])
    esel = np.zeros(8, np.float32); esel[core] = 1.0
    d["esel"] = esel
    d["cbase"] = np.array([core * T], np.float32)
    return d



def build_nc(upto="full"):
    nc = bass.Bass("TRN2", target_bir_lowering=False)
    io = {}

    def ein(name, shape):
        io[name] = nc.dram_tensor(name, list(shape), F32, kind="ExternalInput").ap()

    ein("x", [T, D]); ein("rmask", [8])
    ein("a0_wq", [NP, 128, ND, 128]); ein("a0_wk", [NP, 128, ND, 128])
    ein("a0_wv", [NP, 128, ND, 128]); ein("a0_wo", [4, 128, ND, 256])
    ein("a0_bqkv", [128, 3 * ND]); ein("a0_bo", [D])
    ein("f0_w1t", [NDH, 128, ND, 128]); ein("f0_w2t", [NDH, 128, ND, 128])
    ein("f0_h1t", [NDH, 128, NDH, 128]); ein("f0_h2t", [NDH, 128, NDH, 128])
    ein("f0_w3t", [8, 128, NDH, 128])
    ein("f0_s", [128, 4 * NDH + 8])
    ein("a1_wq", [NP, 128, ND, 128]); ein("a1_wl", [2, 128, ND, 128])
    ein("a1_wk", [NP, 128, 2, 128]); ein("a1_wv", [NP, 128, 2, 128])
    ein("a1_wo", [4, 128, ND, 256])
    ein("a1_bqkv", [128, 3 * ND + 2]); ein("a1_bo", [D])
    ein("m_rwt", [D, E])
    ein("m_w1", [8, 128, ND, 512])
    ein("m_w2", [2, 4, 128, 8, 512])
    ein("m_b1", [128, NDH]); ein("m_b2", [D])
    ein("esel", [8]); ein("cbase", [1])

    out_t = nc.dram_tensor("out", [T, D], F32, kind="ExternalOutput").ap()
    xres = nc.dram_tensor("xres", [T, D], F32, kind="Internal").ap()
    bcz = nc.dram_tensor("bcz", [H, T], F32, kind="Internal").ap()
    bcb = nc.dram_tensor("bcb", [H], F32, kind="Internal").ap()
    bcm = nc.dram_tensor("bcm", [2, T], F32, kind="Internal").ap()
    cc_in = nc.dram_tensor("cc_in", [E], F32, kind="Internal")
    cc_out = nc.dram_tensor("cc_out", [8 * E], F32, kind="Internal")
    xag_in_t = nc.dram_tensor("xag_in", [T, D], BF16, kind="Internal")
    xag_in = xag_in_t.ap()
    xall_t = nc.dram_tensor("xall", [8 * T, D], BF16, kind="Internal")
    xall = xall_t.ap()
    idxb_t = nc.dram_tensor("idxb", [E * 8 * T], F32, kind="Internal")
    idxb = idxb_t.ap()
    idxall_t = nc.dram_tensor("idxall", [E * 8 * T], F32, kind="Internal")
    eog_t = nc.dram_tensor("eog", [CAPACITY, D], BF16, kind="Internal")
    eog = eog_t.ap()
    eoall_t = nc.dram_tensor("eoall", [8 * CAPACITY, D], BF16, kind="Internal")
    eoall = eoall_t.ap()
    if DBG_MOE:
        dbg_xb = nc.dram_tensor("dbg_xb", [512, D], F32, kind="ExternalOutput").ap()
        dbg_u24 = nc.dram_tensor("dbg_u24", [128, 24], U32, kind="ExternalOutput").ap()
        dbg_h = nc.dram_tensor("dbg_h", [128, 1024], F32, kind="ExternalOutput").ap()
        dbg_i24 = nc.dram_tensor("dbg_i24", [24, 128], F32, kind="ExternalOutput").ap()

    st = contextlib.ExitStack()
    with st:
        class Arena:
            def __init__(self, name, kib):
                self.nbytes = kib * KB
                self.t = st.enter_context(
                    nc.sbuf_tensor(name, [128, self.nbytes // 4], F32))

            def f32(self, off, n, parts=slice(None)):
                assert off % 4 == 0 and off + 4 * n <= self.nbytes, (off, n)
                return self.t.ap()[parts, off // 4: off // 4 + n]

            def bf16(self, off, n, parts=slice(None)):
                assert off % 2 == 0 and off + 2 * n <= self.nbytes, (off, n)
                return self.t.ap().bitcast(BF16)[parts, off // 2: off // 2 + n]

        ar1 = Arena("ar1", 64)
        ar2 = Arena("ar2", 64)
        ar3 = Arena("ar3", 32)
        ar4 = Arena("ar4", 22)
        wst = Arena("wst", 16)

        sbt = lambda name, shape, dt=F32: st.enter_context(
            nc.sbuf_tensor(name, shape, dt))
        ident = sbt("ident", [128, 128], BF16)
        identf = sbt("identf", [128, 128], F32)
        iota512 = sbt("iota512", [128, 512], F32)
        ciota = sbt("ciota", [128, NCC], F32)
        ones2c = sbt("ones2c", [128, 2], BF16)
        b_a0 = sbt("b_a0", [128, 3 * ND], F32)
        b_a1 = sbt("b_a1", [128, 3 * ND + 2], F32)
        f0s = sbt("f0s", [128, 4 * NDH + 8], F32)
        mb1 = sbt("mb1", [128, NDH], F32)
        stats = sbt("stats", [128, 2 * NT], F32)
        mrs = sbt("mrs", [128, 2 * NT], F32)
        bh2 = sbt("bh2", [128, 2], F32)
        small = sbt("small", [128, 96], F32)
        smallu = sbt("smallu", [128, 44], U32)
        sm2 = sbt("sm2", [128, 112], F32)
        er16 = sbt("er16", [16, 256], F32)
        i24 = sbt("i24", [24, 128], F32)
        u24 = sbt("u24", [128, 24], U32)
        esel_sb = sbt("esel_sb", [8, 1], F32)
        ZEROT = sbt("zerot", [128, 256], F32)
        cst = sbt("cst", [128, 2], F32)
        rmk = sbt("rmk", [8, 1], F32)
        agg = sbt("agg", [8, 8], F32)
        off8 = sbt("off8", [8, 1], F32)
        ones8 = sbt("ones8", [8, 1], F32)

        greg = st.enter_context(nc.gpsimd.register("greg"))
        psA = st.enter_context(nc.psum_tensor("psA", [128, 1024], F32))
        psB = st.enter_context(nc.psum_tensor("psB", [128, 1024], F32))
        pav = st.enter_context(nc.psum_tensor("pav", [128, 512], F32))
        pax = st.enter_context(nc.psum_tensor("pax", [128, 512], F32))
        pex = st.enter_context(nc.psum_tensor("pex", [128, 512], F32))
        ptr = st.enter_context(nc.psum_tensor("ptr", [128, 1024], BF16))
        PS = {"A0": psA[:, 0:512], "A1": psA[:, 512:1024],
              "B0": psB[:, 0:512], "B1": psB[:, 512:1024],
              "av": pav[:], "ax": pax[:], "ex": pex[:],
              "trf": ptr.ap().bitcast(F32)}
        ROT4 = ["A0", "A1", "B0", "B1"]

        SEMS = ["te", "ve", "se", "ge"]
        semh = {n: st.enter_context(nc.semaphore(n)) for n in SEMS}
        semc = {n: 0 for n in SEMS}

        def sem_of(name):
            if name not in semh:
                semh[name] = st.enter_context(nc.semaphore(name))
                semc[name] = 0
            return semh[name]

        def inc(instr, name, n=1):
            sem_of(name)
            semc[name] += n
            instr.then_inc(semh[name], n)
            return semc[name]

        def wait(eng, name, val=None):
            sem_of(name)
            eng.wait_ge(semh[name], semc[name] if val is None else val)

        freed = {}

        def pm_produce(eng, bank):
            f = freed.pop(bank, None)
            if f is not None:
                eng.wait_ge(semh[f[0]], f[1])

        def pm_consumed(bank, sem, val=None):
            freed[bank] = (sem, semc[sem] if val is None else val)

        blk = st.enter_context(nc.Block())

        def sec(name, fn):
            getattr(blk, name)(fn)

        X_T = lambda ti: ar1.f32(ti * 4096, D)
        XH_TM = lambda ti: ar3.bf16(ti * 2048, D)
        XH_FM = lambda dc: ar2.bf16(dc * 4096, T)
        QSL = lambda buf: ar2.bf16(32 * KB + buf * 8192, T)
        KSL = lambda buf: ar2.bf16(32 * KB + buf * 8192 + 4096, T)
        ZRT = ar2.f32(48 * KB, T, parts=slice(64, 65))
        LArow = lambda lc: ar2.bf16(56 * KB + lc * 4096, T)
        VAUG = lambda tc: ar4.bf16(tc * 272, 136)
        PP = lambda buf: ar4.bf16(4608 + buf * 2048, 1024)
        QKSQ = ar4.bf16(8704, 2048)
        ZBC = ar4.f32(12800, 2048)
        AV_FM = lambda dc: ar3.bf16(dc * 4096, T)
        BOBC = ar2.f32(56 * KB, D)
        BTMP = ar2.bf16(48 * KB, 2048, parts=slice(0, 64))
        MASKS = lambda m: wst.bf16(12 * KB + m * 1024, 512)
        H_T = lambda oc: ar1.bf16(oc * 2048, 1024)
        G_T = lambda oc: ar2.bf16(oc * 2048, 1024)
        XHF_H = lambda dc: ar2.bf16(dc * 2048, 1024)
        XIO_F = lambda b: ar4.f32(b * 4096, D)
        Y_FMF = lambda ob: ar2.f32(ob * 4096, 1024)
        SILT = lambda b: ar4.bf16(12288 + b * 1024, 512)
        XHM_FM = lambda dc: ar1.bf16(dc * 4096, T)
        XB_TM = lambda sc: ar1.bf16(sc * 2048, D)
        XB_FM = lambda dc: ar2.bf16(dc * 6144, SLOTS)
        HB_T = lambda fcg: ar1.bf16(fcg * 2048, 1024)
        EOT3 = lambda cc: ar3.bf16(cc * 2048, D)
        EO_T = lambda cc: ar1.bf16(cc * 2048, 1024)
        D_T = lambda tc: ar1.bf16(48 * KB + tc * 1024, 512)
        XGF = lambda dc: ar2.bf16(dc * 6144, SLOTS)
        B2BC = ar2.f32(48 * KB, D)
        ONEHOT = ar2.f32(0, T, parts=slice(0, 8))
        CNTI = ar2.f32(8 * KB, T, parts=slice(0, 8))
        EROW8 = ar2.f32(16 * KB, T, parts=slice(0, 8))
        TMP8 = ar2.f32(24 * KB, T, parts=slice(0, 8))
        RANKR = ar2.f32(32 * KB, T, parts=slice(0, 1))
        OFFR = ar2.f32(40 * KB, T, parts=slice(32, 33))
        XIO_M = lambda b: ar2.f32(56 * KB + b * 4096, D)
        HH_T = lambda fc: ar4.bf16(fc * 768, CC)
        SLOTBC = ar4.f32(12288, 2048)
        RT = lambda b: ar4.bf16(20480 + b * 1024, 512)
        RWTS = wst.bf16(12 * KB, ND * 8)
        WS = lambda off, n: wst.bf16(off, n)
        WSF = lambda off, n: wst.f32(off, n)

        S_PSTAR, S_ESTAR, S_G, S_NMAX, S_TMP = 0, 16, 64, 80, 83
        C_LG, C_PROB, C_Z8, C_MX8 = 0, 8, 16, 24
        C_RANK, C_OFFT, C_KEPT, C_SLOTP, C_SLSH = 32, 48, 64, 80, 96

        def p0_sync(eng):
            i_ = eng.dma_start(
                out=ar1.f32(0, NT * D).rearrange("p (i d) -> p i d", i=NT),
                in_=io["x"].rearrange("(i p) d -> p i d", p=128))
            inc(i_, "xl", 16)
            i_ = eng.dma_start(out=b_a0[:], in_=io["a0_bqkv"]); inc(i_, "pp", 16)
            i_ = eng.dma_start(out=b_a1[:], in_=io["a1_bqkv"]); inc(i_, "pp", 16)
            i_ = eng.dma_start(out=f0s[:], in_=io["f0_s"]); inc(i_, "pp", 16)
            i_ = eng.dma_start(out=mb1[:], in_=io["m_b1"]); inc(i_, "pp", 16)
            i_ = eng.dma_start(out=esel_sb[:], in_=io["esel"].unsqueeze(1))
            inc(i_, "pp", 16)
            i_ = eng.dma_start(out=rmk[:], in_=io["rmask"].unsqueeze(1))
            inc(i_, "pp", 16)

        def p0_gp(eng):
            eng.memset(ident[:], 1.0)
            eng.memset(identf[:], 1.0)
            eng.drain()
            eng.affine_select(ident[:], ident[:], pattern=[[-1, 128]],
                              compare_op=ALU.is_equal, fill=0.0, base=0,
                              channel_multiplier=1)
            eng.drain()
            eng.affine_select(identf[:], identf[:], pattern=[[-1, 128]],
                              compare_op=ALU.is_equal, fill=0.0, base=0,
                              channel_multiplier=1)
            eng.memset(ones2c[:], 0.0)
            eng.drain()
            eng.memset(ones2c[0:64, 0:1], 1.0)
            eng.drain()
            eng.memset(ones2c[64:128, 1:2], 1.0)
            eng.memset(ones8[:], 1.0)
            eng.memset(ZEROT[:], 0.0)
            eng.memset(cst[:, 0:1], float(EPS))
            eng.memset(cst[:, 1:2], float(SCALE))
            eng.memset(wst.bf16(12 * KB, 2048), 1.0)
            eng.drain()
            for m in range(4):
                eng.affine_select(MASKS(m), MASKS(m), pattern=[[1, 512]],
                                  compare_op=ALU.is_ge, fill=0.0,
                                  base=-128 * m, channel_multiplier=-1)
            eng.iota(iota512[:], pattern=[[1, 512]], base=0, channel_multiplier=0,
                     allow_small_or_imprecise_dtypes=True)
            i_ = eng.iota(ciota[:], pattern=[[128, NCC]], base=0,
                          channel_multiplier=1,
                          allow_small_or_imprecise_dtypes=True)
            inc(i_, "ge", 1)

        sec("sync", p0_sync)
        sec("gpsimd", p0_gp)

        def emit_ln(sq_ap):
            sem_of("xl")
            dxv, vev, tev = semc["xl"], semc["ve"], semc["te"]

            def s_sq(eng):
                eng.wait_ge(semh["xl"], dxv)
                eng.wait_ge(semh["ve"], vev)
                eng.wait_ge(semh["te"], tev)
                i_ = None
                for ti in range(NT):
                    if ti:
                        eng.drain()
                    i_ = eng.activation(sq_ap, X_T(ti), AF.Square,
                                        accum_out=stats[:, NT + ti:NT + ti + 1])
                inc(i_, "se", 1)

            def v_stats(eng):
                eng.wait_ge(semh["xl"], dxv)
                wait(eng, "se")
                for ti in range(NT):
                    eng.reduce_sum(stats[:, ti:ti + 1], X_T(ti), axis=AX.X)
                eng.drain()
                eng.tensor_scalar(mrs[:, 0:NT], stats[:, 0:NT], 1.0 / D, None,
                                  ALU.mult)
                eng.drain()
                eng.tensor_scalar(stats[:, NT:2 * NT], stats[:, NT:2 * NT],
                                  1.0 / D, None, ALU.mult)
                eng.drain()
                eng.tensor_tensor(stats[:, 0:NT], mrs[:, 0:NT], mrs[:, 0:NT],
                                  ALU.mult)
                eng.drain()
                i_ = eng.tensor_tensor(stats[:, NT:2 * NT], stats[:, NT:2 * NT],
                                       stats[:, 0:NT], ALU.subtract)
                inc(i_, "ve", 1)

            def s_rstd(eng):
                wait(eng, "ve")
                eng.wait_ge(semh["ge"], 1)
                i_ = eng.activation(mrs[:, NT:2 * NT], stats[:, NT:2 * NT],
                                    AF.Sqrt, bias=cst[:, 0:1], scale=1.0)
                inc(i_, "se", 1)

            def v_apply(eng):
                wait(eng, "se")
                eng.reciprocal(mrs[:, NT:2 * NT], mrs[:, NT:2 * NT])
                eng.drain()
                for ti in range(NT):
                    i_ = eng.tensor_scalar(XH_TM(ti), X_T(ti), mrs[:, ti:ti + 1],
                                           mrs[:, NT + ti:NT + ti + 1],
                                           ALU.subtract, ALU.mult)
                    inc(i_, "ve", 1)

            sec("scalar", s_sq)
            sec("vector", v_stats)
            sec("scalar", s_rstd)
            sec("vector", v_apply)
            return semc["ve"] - NT

        def emit_tm2fm(ve_base, src, dst_fm, ntiles=NT, dchunks=ND):
            veE0 = semc["ve"]

            def t_tr(eng):
                for ti in range(ntiles):
                    eng.wait_ge(semh["ve"], ve_base + ti + 1)
                    eng.wait_ge(semh["ve"], veE0 + ti)
                    i_ = None
                    for dc in range(dchunks):
                        i_ = eng.transpose(
                            ptr[:, dc * 128:dc * 128 + 128],
                            src(ti)[:, dc * 128:dc * 128 + 128], ident[:])
                    inc(i_, "te", 1)

            def v_ev(eng):
                base = semc["te"] - ntiles
                for ti in range(ntiles):
                    eng.wait_ge(semh["te"], base + ti + 1)
                    i_ = None
                    for dc in range(dchunks):
                        i_ = eng.tensor_copy(
                            dst_fm(dc)[:, ti * 128:ti * 128 + 128],
                            ptr[:, dc * 128:dc * 128 + 128])
                    inc(i_, "ve", 1)

            sec("tensor", t_tr)
            sec("vector", v_ev)

        wstbar = [0, 0]

        def set_wstbar():
            wstbar[0] = semc["te"]
            wstbar[1] = semc["ve"]

        def fetch(eng, dst_ap, src_ap, bufkey):
            if wstbar[0]:
                eng.wait_ge(semh["te"], wstbar[0])
            if wstbar[1]:
                eng.wait_ge(semh["ve"], wstbar[1])
            f = freed.pop(bufkey, None)
            if f is not None:
                eng.wait_ge(semh[f[0]], f[1])
            i_ = eng.dma_start(out=dst_ap, in_=src_ap)
            return inc(i_, "F" + bufkey, 16)

        def fwait(eng, *keys):
            for k in keys:
                wait(eng, "F" + k)

        def emit_attention(L):
            pfx = "a0_" if L == 0 else "a1_"
            bqk = b_a0 if L == 0 else b_a1
            causal = (L == 0)

            set_wstbar()
            ve0 = emit_ln(ar2.f32(0, D))
            emit_tm2fm(ve0, XH_TM, XH_FM)

            if L == 1:
                def g_wl(eng):
                    for lc in range(2):
                        fetch(eng, WS(lc * 2048, ND * 128).rearrange(
                            "p (i c) -> p i c", i=ND), io["a1_wl"][lc], f"wq{lc}")
                sec("gpsimd", g_wl)

                seL0 = semc["se"]

                def t_lat(eng, seL0=seL0):
                    fwait(eng, "wq0", "wq1")
                    wait(eng, "ve")
                    for lc in range(2):
                        for sc in range(4):
                            k = lc * 4 + sc
                            bank = ROT4[k % 4]
                            if k < 4:
                                pm_produce(eng, bank)
                            else:
                                eng.wait_ge(semh["se"], seL0 + k - 3)
                            i_ = None
                            for ic in range(ND):
                                i_ = eng.matmul(
                                    PS[bank],
                                    WS(lc * 2048, ND * 128).rearrange(
                                        "p (i c) -> p i c", i=ND)[:, ic, :],
                                    XH_FM(ic)[:, sc * 512:sc * 512 + 512],
                                    start=(ic == 0), stop=(ic == ND - 1))
                            inc(i_, "te", 1)

                def s_lat(eng):
                    base = semc["te"] - 8
                    for k in range(8):
                        lc, sc = divmod(k, 4)
                        bank = ROT4[k % 4]
                        eng.wait_ge(semh["te"], base + k + 1)
                        i_ = eng.activation(
                            LArow(lc)[:, sc * 512:sc * 512 + 512], PS[bank],
                            AF.Identity, bias=bqk[:, 3 * ND + lc:3 * ND + lc + 1],
                            scale=1.0)
                        inc(i_, "se", 1)
                        pm_consumed(bank, "se")

                sec("tensor", t_lat)
                freed["wq0"] = ("te", semc["te"])
                freed["wq1"] = ("te", semc["te"])
                sec("scalar", s_lat)

            for c in range(NP):
                buf = c % 2
                nkv = ND if L == 0 else 2
                kvsrc = XH_FM if L == 0 else LArow

                def g_w(eng, c=c, buf=buf, nkv=nkv):
                    fetch(eng, WS(buf * 2048, ND * 128).rearrange(
                        "p (i c) -> p i c", i=ND), io[pfx + "wq"][c], f"wq{buf}")
                    fetch(eng, WS(4096 + buf * 2048, nkv * 128).rearrange(
                        "p (i c) -> p i c", i=nkv), io[pfx + "wk"][c], f"wk{buf}")
                    fetch(eng, WS(4 * 2048 + buf * 2048, nkv * 128).rearrange(
                        "p (i c) -> p i c", i=nkv), io[pfx + "wv"][c], f"wv{buf}")
                sec("gpsimd", g_w)

                seQ0 = semc["se"]

                def t_qk(eng, c=c, buf=buf, nkv=nkv, kvsrc=kvsrc, seQ0=seQ0):
                    fwait(eng, f"wq{buf}", f"wk{buf}")
                    if c == 0:
                        wait(eng, "ve")
                        if L == 1:
                            wait(eng, "se")
                    for which in range(2):
                        nic = ND if which == 0 else nkv
                        srcf = XH_FM if which == 0 else kvsrc
                        woff = buf * 2048 if which == 0 else 4096 + buf * 2048
                        for sc in range(4):
                            k = which * 4 + sc
                            bank = ROT4[k % 4]
                            if k < 4:
                                pm_produce(eng, bank)
                            else:
                                eng.wait_ge(semh["se"], seQ0 + k - 3)
                            i_ = None
                            for ic in range(nic):
                                i_ = eng.matmul(
                                    PS[bank],
                                    WS(woff, nic * 128).rearrange(
                                        "p (i c) -> p i c", i=nic)[:, ic, :],
                                    srcf(ic)[:, sc * 512:sc * 512 + 512],
                                    start=(ic == 0), stop=(ic == nic - 1))
                            inc(i_, "te", 1)

                def s_qk(eng, c=c, buf=buf):
                    wait(eng, "pp")
                    base = semc["te"] - 8
                    for k in range(8):
                        which, sc = divmod(k, 4)
                        bank = ROT4[k % 4]
                        dst = QSL(buf) if which == 0 else KSL(buf)
                        eng.wait_ge(semh["te"], base + k + 1)
                        i_ = eng.activation(
                            dst[:, sc * 512:sc * 512 + 512], PS[bank], AF.Identity,
                            bias=bqk[:, which * ND + c:which * ND + c + 1],
                            scale=1.0)
                        inc(i_, "se", 1)
                        pm_consumed(bank, "se")

                sec("tensor", t_qk)
                freed[f"wq{buf}"] = ("te", semc["te"])
                freed[f"wk{buf}"] = ("te", semc["te"])
                sec("scalar", s_qk)
                qk_se = semc["se"]

                def s_sq_q(eng, buf=buf):
                    wait(eng, "se")
                    i_ = eng.activation(QKSQ, QSL(buf), AF.Square)
                    inc(i_, "se", 1)
                sec("scalar", s_sq_q)

                def t_nq_fix(eng):
                    wait(eng, "se")
                    for bank in ("A0", "A1", "B0", "B1"):
                        pm_produce(eng, bank)
                    i_ = None
                    for sc in range(4):
                        dst = psA if sc < 2 else psB
                        i_ = eng.matmul(dst[0:2, (sc % 2) * 512:(sc % 2) * 512 + 512],
                                        ones2c[:], QKSQ[:, sc * 512:sc * 512 + 512],
                                        start=True, stop=True)
                    inc(i_, "te", 1)

                def v_nq(eng, which=0):
                    wait(eng, "te")
                    eng.reduce_max(small[0:2, S_TMP + which:S_TMP + which + 1],
                                   psA[0:2, :], axis=AX.X)
                    eng.drain()
                    i_ = eng.reduce_max(small[0:2, S_TMP + 2 + which:S_TMP + 3 + which],
                                        psB[0:2, :], axis=AX.X)
                    eng.drain()
                    i_ = eng.tensor_tensor(
                        small[0:2, S_TMP + which:S_TMP + which + 1],
                        small[0:2, S_TMP + which:S_TMP + which + 1],
                        small[0:2, S_TMP + 2 + which:S_TMP + 3 + which], ALU.max)
                    inc(i_, "ve", 1)
                    for bank in ("A0", "A1", "B0", "B1"):
                        pm_consumed(bank, "ve")

                def s_sq_k(eng, buf=buf):
                    wait(eng, "ve")
                    i_ = eng.activation(QKSQ, KSL(buf), AF.Square)
                    inc(i_, "se", 1)

                sec("tensor", t_nq_fix)
                sec("vector", lambda eng: v_nq(eng, 0))
                sec("scalar", s_sq_k)
                sec("tensor", t_nq_fix)
                sec("vector", lambda eng: v_nq(eng, 1))

                def v_bh(eng):
                    wait(eng, "ve")
                    i_ = eng.tensor_tensor(small[0:2, S_TMP + 4:S_TMP + 5],
                                           small[0:2, S_TMP:S_TMP + 1],
                                           small[0:2, S_TMP + 1:S_TMP + 2],
                                           ALU.mult)
                    inc(i_, "ve", 1)

                def s_bh(eng):
                    wait(eng, "ve")
                    i_ = eng.activation(small[0:2, S_TMP + 5:S_TMP + 6],
                                        small[0:2, S_TMP + 4:S_TMP + 5], AF.Sqrt)
                    inc(i_, "se", 1)

                def v_bh2(eng):
                    wait(eng, "se")
                    i_ = eng.tensor_scalar(small[0:2, S_TMP + 6:S_TMP + 7],
                                           small[0:2, S_TMP + 5:S_TMP + 6],
                                           -float(SCALE), None, ALU.mult)
                    inc(i_, "ve", 1)

                sec("vector", v_bh)
                sec("scalar", s_bh)
                sec("vector", v_bh2)
                vev = semc["ve"]

                def d_bh(eng, c=c, vev=vev):
                    eng.wait_ge(semh["ve"], vev)
                    i_ = eng.dma_start(out=bcb[2 * c:2 * c + 2],
                                       in_=small[0:2, S_TMP + 6:S_TMP + 7].squeeze(1))
                    inc(i_, "bb", 16)
                    wait(eng, "bb")
                    i_ = eng.dma_start(
                        out=bh2[:], in_=bcb[2 * c:2 * c + 2].unsqueeze(0)
                        .broadcast_to([128, 2]))
                    inc(i_, "bb", 16)
                sec("sync", d_bh)
                sem_of("bb")
                bh_dxv = semc["bb"]

                def v_vone(eng):
                    wait(eng, "te")
                    i_ = eng.memset(ar4.bf16(0, 16 * 136), 1.0)
                    inc(i_, "ve", 1)
                sec("vector", v_vone)
                vsetv = semc["ve"]

                veV0 = semc["ve"]

                def t_v(eng, c=c, buf=buf, nkv=nkv, kvsrc=kvsrc, vsetv=vsetv,
                        veV0=veV0):
                    fwait(eng, f"wv{buf}")
                    eng.wait_ge(semh["ve"], vsetv)
                    for tc in range(NT):
                        bank = ["av", "ax"][tc % 2]
                        if tc < 2:
                            pm_produce(eng, bank)
                        else:
                            eng.wait_ge(semh["ve"], veV0 + tc - 1)
                        i_ = None
                        for ic in range(nkv):
                            i_ = eng.matmul(
                                PS[bank][:, 0:128],
                                kvsrc(ic)[:, tc * 128:tc * 128 + 128],
                                WS(4 * 2048 + buf * 2048, nkv * 128).rearrange(
                                    "p (i c) -> p i c", i=nkv)[:, ic, :],
                                start=(ic == 0), stop=(ic == nkv - 1))
                        inc(i_, "te", 1)

                def v_vev(eng, c=c):
                    base = semc["te"] - NT
                    for tc in range(NT):
                        bank = ["av", "ax"][tc % 2]
                        eng.wait_ge(semh["te"], base + tc + 1)
                        eng.tensor_copy(VAUG(tc)[:, 0:64], PS[bank][:, 0:64])
                        eng.drain()
                        i_ = eng.tensor_copy(VAUG(tc)[:, 65:129],
                                             PS[bank][:, 64:128])
                        inc(i_, "ve", 1)
                        pm_consumed(bank, "ve")

                sec("tensor", t_v)
                freed[f"wv{buf}"] = ("te", semc["te"])
                sec("vector", v_vev)
                vaug_ve = semc["ve"]

                for h in range(2):
                    hr = slice(64 * h, 64 * h + 64)
                    for qc in range(4):
                        nkc = 4 * qc + 4 if causal else NT
                        ngrp = nkc // 2
                        pv_cnt = 0
                        exp_ses = []
                        mask_ves = []
                        pvb = "av" if (4 * h + qc) % 2 == 0 else "ex"
                        pvt = pav if (4 * h + qc) % 2 == 0 else pex
                        zoff = 48 * KB + ((4 * h + qc) % 4) * 2048

                        def t_s(eng, c=c, buf=buf, h=h, hr=hr, qc=qc, nkc=nkc,
                                ngrp=ngrp, bh_dxv=bh_dxv, vaug_ve=vaug_ve,
                                pvb=pvb, pvt=pvt, qk_se=qk_se):
                            if h == 0 and qc == 0:
                                eng.wait_ge(semh["se"], qk_se)
                            pm_produce(eng, pvb)

                            def emit_S(g):
                                gb = "A" if g % 2 == 0 else "B"
                                if g < 2:
                                    pm_produce(eng, gb + "0")
                                    pm_produce(eng, gb + "1")
                                else:
                                    eng.wait_ge(semh["se"], exp_ses[g - 2])
                                ps_full = psA if g % 2 == 0 else psB
                                i_ = None
                                for j in range(2):
                                    kc = 2 * g + j
                                    i_ = eng.matmul(
                                        ps_full[:, j * 512:j * 512 + 512],
                                        KSL(buf)[hr, kc * 128:kc * 128 + 128],
                                        QSL(buf)[hr, qc * 512:qc * 512 + 512],
                                        start=True, stop=True)
                                inc(i_, "te", 1)

                            emit_S(0)
                            for g in range(ngrp):
                                if g + 1 < ngrp:
                                    emit_S(g + 1)
                                if g == 0:
                                    eng.wait_ge(semh["ve"], vaug_ve)
                                eng.wait_ge(semh["se"], exp_ses[g])
                                if mask_ves[g]:
                                    eng.wait_ge(semh["ve"], mask_ves[g])
                                i_ = None
                                for j in range(2):
                                    kc = 2 * g + j
                                    i_ = eng.matmul(
                                        pvt[0:65, :],
                                        VAUG(kc)[:, 65 * h:65 * h + 65],
                                        PP(g % 2)[:, j * 512:j * 512 + 512],
                                        start=(kc == 0), stop=(kc == nkc - 1))
                                inc(i_, "te", 1)

                        te0 = semc["te"]
                        se0 = semc["se"]
                        ve0_ = semc["ve"]
                        def s_exp2(eng, c=c, h=h, qc=qc, ngrp=ngrp, te0=te0,
                                   bh_dxv=bh_dxv):
                            for g in range(ngrp):
                                if g == 0:
                                    eng.wait_ge(semh["bb"], bh_dxv)
                                eng.wait_ge(semh["te"],
                                            te0 + (1 if g == 0 else 2 * g))
                                ps_full = psA if g % 2 == 0 else psB
                                i_ = eng.activation(PP(g % 2), ps_full[:],
                                                    AF.Exp,
                                                    bias=bh2[:, h:h + 1],
                                                    scale=cst[:, 1:2])
                                inc(i_, "se", 1)
                        def v_mask(eng, qc=qc, ngrp=ngrp, se0=se0, causal=causal):
                            if not causal:
                                return
                            for gi, g in enumerate((ngrp - 2, ngrp - 1)):
                                eng.wait_ge(semh["se"], se0 + g + 1)
                                i_ = None
                                for j in range(2):
                                    m = 2 * gi + j
                                    i_ = eng.tensor_tensor(
                                        PP(g % 2)[:, j * 512:j * 512 + 512],
                                        PP(g % 2)[:, j * 512:j * 512 + 512],
                                        MASKS(m), ALU.mult)
                                inc(i_, "ve", 1)
                        sec("scalar", s_exp2)
                        sec("vector", v_mask)
                        exp_ses.extend(se0 + g + 1 for g in range(ngrp))
                        mask_ves.extend([0] * ngrp)
                        if causal:
                            mask_ves[ngrp - 2] = ve0_ + 1
                            mask_ves[ngrp - 1] = ve0_ + 2
                        sec("tensor", t_s)
                        for gb, gg in (("A", ngrp - 2 if ngrp >= 2 else 0),
                                       ("B", ngrp - 1)):
                            if gg >= 0 and gg < ngrp:
                                tgt = ("A" if gg % 2 == 0 else "B")
                                freed[tgt + "0"] = ("se", exp_ses[gg])
                                freed[tgt + "1"] = ("se", exp_ses[gg])

                        def v_av(eng, c=c, h=h, qc=qc, pvb=pvb, pvt=pvt,
                                 zoff=zoff):
                            wait(eng, "te")
                            wait(eng, "zr")
                            eng.reciprocal(ar2.f32(zoff, 512,
                                                   parts=slice(64, 65)),
                                           pvt[64:65, :])
                            eng.drain()
                            if h == 0:
                                i_ = eng.tensor_copy(
                                    AV_FM(c)[0:64, qc * 512:qc * 512 + 512],
                                    pvt[0:64, :])
                            else:
                                i_ = eng.tensor_copy(
                                    BTMP[:, qc * 512:qc * 512 + 512],
                                    pvt[0:64, :])
                            inc(i_, "ve", 1)
                            pm_consumed(pvb, "ve")
                        sec("vector", v_av)

                        def d_z(eng, c=c, h=h, qc=qc, zoff=zoff):
                            wait(eng, "ve")
                            i_ = eng.dma_start(
                                out=bcz[2 * c + h, qc * 512:qc * 512 + 512]
                                .unsqueeze(0),
                                in_=ar2.f32(zoff, 512,
                                            parts=slice(64, 65)))
                            inc(i_, "zr", 16)
                        sec("sync", d_z)

                def d_shift(eng, c=c):
                    wait(eng, "ve")
                    wait(eng, "zr")
                    i_ = eng.dma_start(out=ar3.bf16(c * 4096, T,
                                                    parts=slice(64, 128)),
                                       in_=BTMP[:])
                    inc(i_, "sh", 16)
                    i_ = eng.dma_start(
                        out=ar4.f32(12800, 2048, parts=slice(0, 64)),
                        in_=bcz[2 * c].unsqueeze(0).broadcast_to([64, T]))
                    inc(i_, "sh", 16)
                    i_ = eng.dma_start(
                        out=ar4.f32(12800, 2048, parts=slice(64, 128)),
                        in_=bcz[2 * c + 1].unsqueeze(0).broadcast_to([64, T]))
                    inc(i_, "sh", 16)
                sec("sync", d_shift)
                sem_of("sh")
                dxv2 = semc["sh"]

                def v_norm(eng, c=c, dxv2=dxv2):
                    eng.wait_ge(semh["sh"], dxv2)
                    eng.tensor_tensor(AV_FM(c), AV_FM(c), ZBC, ALU.mult)
                    eng.drain()
                    i_ = eng.tensor_scalar(AV_FM(c), AV_FM(c),
                                           bqk[:, 2 * ND + c:2 * ND + c + 1],
                                           None, ALU.add)
                    inc(i_, "ve", 1)
                sec("vector", v_norm)

            set_wstbar()

            def s_bobc(eng):
                wait(eng, "ve")
                wait(eng, "te")
                wait(eng, "sh")
                i_ = eng.dma_start(out=BOBC, in_=io[pfx + "bo"].unsqueeze(0)
                                   .broadcast_to([128, D]))
                inc(i_, "bc", 16)
            sec("sync", s_bobc)
            sem_of("bc")
            bobc_dxv = semc["bc"]

            for oq in range(4):
                def g_wo(eng, oq=oq):
                    fetch(eng, WS((oq % 2) * 4096, ND * 256).rearrange(
                        "p (i c) -> p i c", i=ND), io[pfx + "wo"][oq],
                        f"wo{oq % 2}")
                sec("gpsimd", g_wo)

                veO0 = semc["ve"]

                def t_o(eng, oq=oq, veO0=veO0):
                    fwait(eng, f"wo{oq % 2}")
                    if oq == 0:
                        wait(eng, "ve")
                    for tc in range(NT):
                        bank = ROT4[tc % 4]
                        if tc < 4:
                            pm_produce(eng, bank)
                        else:
                            eng.wait_ge(semh["ve"], veO0 + tc - 3)
                        i_ = None
                        for dc in range(ND):
                            i_ = eng.matmul(
                                PS[bank][:, 0:256],
                                AV_FM(dc)[:, tc * 128:tc * 128 + 128],
                                WS((oq % 2) * 4096, ND * 256).rearrange(
                                    "p (i c) -> p i c", i=ND)[:, dc, :],
                                start=(dc == 0), stop=(dc == ND - 1))
                        inc(i_, "te", 1)

                def v_o(eng, oq=oq, bobc_dxv=bobc_dxv):
                    eng.wait_ge(semh["bc"], bobc_dxv)
                    base = semc["te"] - NT
                    for tc in range(NT):
                        bank = ROT4[tc % 4]
                        eng.wait_ge(semh["te"], base + tc + 1)
                        eng.tensor_tensor(X_T(tc)[:, oq * 256:oq * 256 + 256],
                                          X_T(tc)[:, oq * 256:oq * 256 + 256],
                                          PS[bank][:, 0:256], ALU.add)
                        eng.drain()
                        i_ = eng.tensor_tensor(
                            X_T(tc)[:, oq * 256:oq * 256 + 256],
                            X_T(tc)[:, oq * 256:oq * 256 + 256],
                            BOBC[:, oq * 256:oq * 256 + 256], ALU.add)
                        inc(i_, "ve", 1)
                        pm_consumed(bank, "ve")

                sec("tensor", t_o)
                freed[f"wo{oq % 2}"] = ("te", semc["te"])
                sec("vector", v_o)

        def emit_ffn():
            ve0 = emit_ln(ar2.f32(0, D))
            lnve = semc["ve"]

            def d_spill(eng, lnve=lnve):
                eng.wait_ge(semh["ve"], lnve)
                wait(eng, "se")
                i_ = eng.dma_start(
                    out=xres.rearrange("(i p) d -> p i d", p=128),
                    in_=ar1.f32(0, NT * D).rearrange("p (i d) -> p i d", i=NT))
                inc(i_, "xl", 16)
            sec("sync", d_spill)
            sem_of("xl")
            spill_dxv = semc["xl"]

            for half in range(2):
                hb = half * 8
                set_wstbar()
                veT0 = semc["ve"]

                def t_trh(eng, half=half, hb=hb, spill_dxv=spill_dxv, veT0=veT0):
                    if half == 0:
                        eng.wait_ge(semh["xl"], spill_dxv)
                    else:
                        wait(eng, "se")
                    for ti in range(8):
                        eng.wait_ge(semh["ve"], veT0 + ti)
                        i_ = None
                        for dc in range(ND):
                            i_ = eng.transpose(
                                ptr[:, dc * 128:dc * 128 + 128],
                                XH_TM(hb + ti)[:, dc * 128:dc * 128 + 128],
                                ident[:])
                        inc(i_, "te", 1)

                def v_trh(eng, half=half):
                    base = semc["te"] - 8
                    for ti in range(8):
                        eng.wait_ge(semh["te"], base + ti + 1)
                        i_ = None
                        for dc in range(ND):
                            i_ = eng.tensor_copy(
                                XHF_H(dc)[:, ti * 128:ti * 128 + 128],
                                ptr[:, dc * 128:dc * 128 + 128])
                        inc(i_, "ve", 1)

                sec("tensor", t_trh)
                sec("vector", v_trh)
                trve = semc["ve"]

                for oc in range(NDH):
                    def g_w12(eng, oc=oc):
                        fetch(eng, WS((oc % 2) * 2048, ND * 128).rearrange(
                            "p (i c) -> p i c", i=ND), io["f0_w1t"][oc],
                            f"w1{oc % 2}")
                        fetch(eng, WS(4096 + (oc % 2) * 2048, ND * 128).rearrange(
                            "p (i c) -> p i c", i=ND), io["f0_w2t"][oc],
                            f"w2{oc % 2}")
                    sec("gpsimd", g_w12)

                    def t_12(eng, oc=oc, trve=trve):
                        fwait(eng, f"w1{oc % 2}", f"w2{oc % 2}")
                        if oc == 0:
                            eng.wait_ge(semh["ve"], trve)
                        for sc in range(2):
                            for w in range(2):
                                bank = ROT4[2 * sc + w]
                                pm_produce(eng, bank)
                                i_ = None
                                for ic in range(ND):
                                    i_ = eng.matmul(
                                        PS[bank],
                                        WS(w * 4096 + (oc % 2) * 2048,
                                           ND * 128).rearrange(
                                            "p (i c) -> p i c", i=ND)[:, ic, :],
                                        XHF_H(ic)[:, sc * 512:sc * 512 + 512],
                                        start=(ic == 0), stop=(ic == ND - 1))
                                inc(i_, "te", 1)

                    def s_sil(eng, oc=oc):
                        if oc == 0:
                            wait(eng, "pp")
                        base = semc["te"] - 4
                        for sc in range(2):
                            eng.wait_ge(semh["te"], base + 2 * sc + 1)
                            i_ = eng.activation(
                                SILT(sc), PS[ROT4[2 * sc]], AF.Silu,
                                scale=f0s[:, oc:oc + 1])
                            inc(i_, "se", 1)

                    def v_h(eng, oc=oc, half=half):
                        base_t = semc["te"] - 4
                        base_s = semc["se"] - 2
                        for sc in range(2):
                            eng.wait_ge(semh["te"], base_t + 2 * sc + 2)
                            eng.wait_ge(semh["se"], base_s + sc + 1)
                            i_ = eng.scalar_tensor_tensor(
                                H_T(oc)[:, sc * 512:sc * 512 + 512],
                                PS[ROT4[2 * sc + 1]],
                                f0s[:, NDH + oc:NDH + oc + 1], SILT(sc),
                                ALU.mult, ALU.mult)
                            inc(i_, "ve", 1)
                            pm_consumed(ROT4[2 * sc], "ve")
                            pm_consumed(ROT4[2 * sc + 1], "ve")

                    sec("tensor", t_12)
                    freed[f"w1{oc % 2}"] = ("te", semc["te"])
                    freed[f"w2{oc % 2}"] = ("te", semc["te"])
                    sec("scalar", s_sil)
                    sec("vector", v_h)

                set_wstbar()
                for stage in range(2):
                    wname = "f0_h1t" if stage == 0 else "f0_h2t"
                    soff = 2 * NDH if stage == 0 else 3 * NDH
                    srcT = H_T if stage == 0 else G_T
                    dstT = G_T if stage == 0 else H_T
                    for oc in range(NDH):
                        def g_wh(eng, oc=oc, wname=wname):
                            fetch(eng, WS((oc % 2) * 8192, NDH * 128).rearrange(
                                "p (i c) -> p i c", i=NDH), io[wname][oc],
                                f"wh{oc % 2}")
                        sec("gpsimd", g_wh)

                        def t_h(eng, oc=oc, srcT=srcT):
                            fwait(eng, f"wh{oc % 2}")
                            if oc == 0:
                                wait(eng, "ve")
                                wait(eng, "se")
                            for sc in range(2):
                                bank = ROT4[(2 * oc + sc) % 4]
                                pm_produce(eng, bank)
                                i_ = None
                                for ic in range(NDH):
                                    i_ = eng.matmul(
                                        PS[bank],
                                        WS((oc % 2) * 8192, NDH * 128).rearrange(
                                            "p (i c) -> p i c", i=NDH)[:, ic, :],
                                        srcT(ic)[:, sc * 512:sc * 512 + 512],
                                        start=(ic == 0), stop=(ic == NDH - 1))
                                inc(i_, "te", 1)

                        def s_g(eng, oc=oc, soff=soff, dstT=dstT):
                            base = semc["te"] - 2
                            for sc in range(2):
                                bank = ROT4[(2 * oc + sc) % 4]
                                eng.wait_ge(semh["te"], base + sc + 1)
                                i_ = eng.activation(
                                    dstT(oc)[:, sc * 512:sc * 512 + 512],
                                    PS[bank], AF.Gelu,
                                    scale=f0s[:, soff + oc:soff + oc + 1])
                                inc(i_, "se", 1)
                                pm_consumed(bank, "se")

                        sec("tensor", t_h)
                        freed[f"wh{oc % 2}"] = ("te", semc["te"])
                        sec("scalar", s_g)

                set_wstbar()
                seW0 = semc["se"]
                for op_ in range(4):
                    def g_w3(eng, op_=op_):
                        for k in range(2):
                            fetch(eng, WS(k * 8192, NDH * 128)
                                  .rearrange("p (i c) -> p i c", i=NDH),
                                  io["f0_w3t"][2 * op_ + k], f"w3{k}")
                    sec("gpsimd", g_w3)

                    for k in range(2):
                        ob = 2 * op_ + k
                        for sc in range(2):
                            bank = ROT4[(2 * ob + sc) % 4]

                            def t_3(eng, k=k, sc=sc, op_=op_, bank=bank):
                                fwait(eng, "w30", "w31")
                                if k == 0 and sc == 0 and op_ == 0:
                                    wait(eng, "se")
                                pm_produce(eng, bank)
                                i_ = None
                                for hc in range(NDH):
                                    i_ = eng.matmul(
                                        PS[bank],
                                        WS(k * 8192, NDH * 128).rearrange(
                                            "p (i c) -> p i c", i=NDH)[:, hc, :],
                                        H_T(hc)[:, sc * 512:sc * 512 + 512],
                                        start=(hc == 0), stop=(hc == NDH - 1))
                                inc(i_, "te", 1)

                            def s_3(eng, ob=ob, sc=sc, bank=bank):
                                wait(eng, "te")
                                i_ = eng.activation(
                                    Y_FMF(ob)[:, sc * 512:sc * 512 + 512],
                                    PS[bank], AF.Identity,
                                    scale=f0s[:, 4 * NDH + ob:4 * NDH + ob + 1])
                                inc(i_, "se", 1)
                                pm_consumed(bank, "se")

                            sec("tensor", t_3)
                            sec("scalar", s_3)
                    freed["w30"] = ("te", semc["te"])
                    freed["w31"] = ("te", semc["te"])

                veW0 = semc["ve"]
                TRB = [("av", "ax"), ("A0", "A1")]
                for tb in range(8):
                    tc = hb + tb

                    def d_xin(eng, tc=tc, tb=tb):
                        wait(eng, "xl")
                        wait(eng, f"xo{tb % 2}")
                        i_ = eng.dma_start(
                            out=XIO_F(tb % 2)[:],
                            in_=xres[tc * 128:tc * 128 + 128, :])
                        inc(i_, f"xi{tb % 2}", 16)
                    sec("sync", d_xin)
                    sem_of(f"xi{tb % 2}")
                    dxv = semc[f"xi{tb % 2}"]

                    b0n, b1n = TRB[tb % 2]

                    def t_w3tr(eng, tb=tb, b0n=b0n, b1n=b1n, veW0=veW0,
                               seW0=seW0):
                        if tb == 0:
                            eng.wait_ge(semh["se"], seW0 + 16)
                            pm_produce(eng, "av")
                            pm_produce(eng, "ax")
                        if tb == 1:
                            pm_produce(eng, "A0")
                            pm_produce(eng, "A1")
                        if tb >= 2:
                            eng.wait_ge(semh["ve"], veW0 + tb - 1)
                        i_ = None
                        for ob in range(8):
                            dst = PS[b0n] if ob < 4 else PS[b1n]
                            i_ = eng.transpose(
                                dst[:, (ob % 4) * 128:(ob % 4) * 128 + 128],
                                Y_FMF(ob)[:, tb * 128:tb * 128 + 128],
                                identf[:])
                        inc(i_, "te", 1)

                    def v_3n(eng, tb=tb, dxv=dxv, b0n=b0n, b1n=b1n):
                        eng.wait_ge(semh[f"xi{tb % 2}"], dxv)
                        wait(eng, "te")
                        eng.tensor_tensor(XIO_F(tb % 2)[:, 0:512],
                                          XIO_F(tb % 2)[:, 0:512],
                                          PS[b0n], ALU.add)
                        i_ = eng.tensor_tensor(XIO_F(tb % 2)[:, 512:1024],
                                               XIO_F(tb % 2)[:, 512:1024],
                                               PS[b1n], ALU.add)
                        inc(i_, "ve", 1)

                    def d_xout(eng, tc=tc, tb=tb):
                        wait(eng, "ve")
                        i_ = eng.dma_start(
                            out=xres[tc * 128:tc * 128 + 128, :],
                            in_=XIO_F(tb % 2)[:])
                        inc(i_, f"xo{tb % 2}", 16)

                    sec("tensor", t_w3tr)
                    sec("vector", v_3n)
                    sec("sync", d_xout)
                freed["av"] = ("ve", semc["ve"])
                freed["ax"] = ("ve", semc["ve"])
                freed["A0"] = ("ve", semc["ve"])
                freed["A1"] = ("ve", semc["ve"])

            def d_xre(eng):
                wait(eng, "te")
                wait(eng, "xo0")
                wait(eng, "xo1")
                i_ = eng.dma_start(
                    out=ar1.f32(0, NT * D).rearrange("p (i d) -> p i d", i=NT),
                    in_=xres.rearrange("(i p) d -> p i d", p=128))
                inc(i_, "xl", 16)
            sec("sync", d_xre)

        def emit_moe():
            set_wstbar()
            ve0 = emit_ln(ar2.f32(0, D))
            lnve = semc["ve"]

            def d_spill4(eng, lnve=lnve):
                eng.wait_ge(semh["ve"], lnve)
                wait(eng, "se")
                i_ = eng.dma_start(
                    out=xres.rearrange("(i p) d -> p i d", p=128),
                    in_=ar1.f32(0, NT * D).rearrange("p (i d) -> p i d", i=NT))
                inc(i_, "xl", 16)
                i_ = eng.dma_start(
                    out=xag_in.rearrange("(i p) d -> p i d", p=128),
                    in_=ar3.bf16(0, NT * D).rearrange("p (i d) -> p i d", i=NT))
                inc(i_, "xg", 16)
            sec("sync", d_spill4)
            sem_of("xl"); sem_of("xg")
            spillv = semc["xl"]
            xgv = semc["xg"]

            def g_agx(eng, xgv=xgv):
                eng.wait_ge(semh["xg"], xgv)
                i_ = eng.collective_compute(
                    "AllGather", ALU.bypass, replica_groups=[list(range(8))],
                    ins=[xag_in_t.ap().opt()], outs=[xall_t.ap().opt()])
                inc(i_, "ag", 1)
            sec("gpsimd", g_agx)
            sem_of("ag")
            agx_v = semc["ag"]

            veE0 = semc["ve"]

            def t_tr4(eng, veE0=veE0, lnve=lnve):
                for ti in range(NT):
                    eng.wait_ge(semh["ve"], lnve - NT + ti + 1)
                    eng.wait_ge(semh["ve"], veE0 + ti)
                    i_ = None
                    for dc in range(ND):
                        i_ = eng.transpose(
                            ptr[:, dc * 128:dc * 128 + 128],
                            XH_TM(ti)[:, dc * 128:dc * 128 + 128], ident[:])
                    inc(i_, "te", 1)

            def v_ev4(eng, spillv=spillv, xgv=xgv):
                base = semc["te"] - NT
                for ti in range(NT):
                    if ti == 0:
                        eng.wait_ge(semh["xl"], spillv)
                        eng.wait_ge(semh["xg"], xgv)
                    eng.wait_ge(semh["te"], base + ti + 1)
                    i_ = None
                    for dc in range(ND):
                        i_ = eng.tensor_copy(
                            XHM_FM(dc)[:, ti * 128:ti * 128 + 128],
                            ptr[:, dc * 128:dc * 128 + 128])
                    inc(i_, "ve", 1)

            sec("tensor", t_tr4)
            sec("vector", v_ev4)
            xhfm_ve = semc["ve"]

            def g_rwt(eng):
                fetch(eng, wst.bf16(12 * KB, ND * 8).rearrange(
                    "p (i c) -> p i c", i=ND),
                    io["m_rwt"].rearrange("(i p) e -> p i e", p=128), "rwt")
            sec("gpsimd", g_rwt)

            veR0 = semc["ve"]
            seR0 = semc["se"]

            def t_rt(eng, xhfm_ve=xhfm_ve, veR0=veR0):
                fwait(eng, "rwt")
                eng.wait_ge(semh["ve"], xhfm_ve)
                for tc in range(NT):
                    bank = ["ax", "ex"][tc % 2]
                    if tc < 2:
                        pm_produce(eng, bank)
                    else:
                        eng.wait_ge(semh["ve"], veR0 + (tc - 2) * 5 + 1)
                    i_ = None
                    for dc in range(ND):
                        i_ = eng.matmul(
                            PS[bank][:, 0:8],
                            XHM_FM(dc)[:, tc * 128:tc * 128 + 128],
                            wst.bf16(12 * KB, ND * 8).rearrange(
                                "p (i c) -> p i c", i=ND)[:, dc, :],
                            start=(dc == 0), stop=(dc == ND - 1))
                    inc(i_, "te", 1)

            def v_sm1(eng, veR0=veR0, seR0=seR0):
                base = semc["te"] - NT
                for tc in range(NT):
                    bank = ["ax", "ex"][tc % 2]
                    eng.wait_ge(semh["te"], base + tc + 1)
                    if tc:
                        eng.wait_ge(semh["se"], seR0 + tc)
                    eng.tensor_copy(sm2[:, C_LG:C_LG + 8], PS[bank][:, 0:8])
                    eng.drain()
                    i_ = eng.reduce_max(small[:, S_NMAX:S_NMAX + 1],
                                        sm2[:, C_LG:C_LG + 8], axis=AX.X,
                                        negate=True)
                    inc(i_, "ve", 1)
                    pm_consumed(bank, "ve")
                    eng.wait_ge(semh["se"], seR0 + tc + 1)
                    eng.reciprocal(small[:, S_PSTAR + tc:S_PSTAR + tc + 1],
                                   sm2[:, C_Z8:C_Z8 + 1])
                    eng.drain()
                    eng.max(sm2[:, C_MX8:C_MX8 + 8], sm2[:, C_LG:C_LG + 8])
                    eng.drain()
                    eng.max_index(smallu[:, 0:8], sm2[:, C_MX8:C_MX8 + 8],
                                  sm2[:, C_LG:C_LG + 8])
                    eng.drain()
                    i_ = eng.tensor_copy(
                        small[:, S_ESTAR + tc:S_ESTAR + tc + 1],
                        smallu[:, 0:1])
                    inc(i_, "ve", 4)

            def s_sm(eng, veR0=veR0):
                for tc in range(NT):
                    eng.wait_ge(semh["ve"], veR0 + tc * 5 + 1)
                    i_ = eng.activation(sm2[:, C_PROB:C_PROB + 8],
                                        sm2[:, C_LG:C_LG + 8], AF.Exp,
                                        bias=small[:, S_NMAX:S_NMAX + 1],
                                        accum_out=sm2[:, C_Z8:C_Z8 + 1])
                    inc(i_, "se", 1)

            sec("tensor", t_rt)
            sec("scalar", s_sm)
            sec("vector", v_sm1)
            argve = semc["ve"]

            def t_er(eng, argve=argve):
                eng.wait_ge(semh["ve"], argve)
                pm_produce(eng, "ax")
                i_ = eng.transpose(pax[0:16, 0:128],
                                   small[:, S_ESTAR:S_ESTAR + 16], identf[:])
                inc(i_, "te", 1)
            sec("tensor", t_er)

            def v_er(eng):
                wait(eng, "te")
                i_ = eng.tensor_copy(er16[:, 0:128], pax[0:16, 0:128])
                inc(i_, "ve", 1)
                pm_consumed("ax", "ve")
            sec("vector", v_er)
            erve = semc["ve"]

            def d_er(eng, erve=erve):
                eng.wait_ge(semh["ve"], erve)
                i_ = eng.dma_start(
                    out=bcm[0:1, :].rearrange("o (t p) -> o t p", p=128),
                    in_=er16[:, 0:128])
                inc(i_, "mo", 16)
                wait(eng, "mo")
                i_ = eng.dma_start(out=EROW8,
                                   in_=bcm[0:1, :].broadcast_to([8, T]))
                inc(i_, "mo", 16)
            sec("sync", d_er)
            sem_of("mo")
            erow_mo = semc["mo"]

            def v_oh(eng, erow_mo=erow_mo):
                eng.wait_ge(semh["mo"], erow_mo)
                wait(eng, "se")
                eng.tensor_scalar(ONEHOT, EROW8, ciota[0:8, 0:1], None,
                                  ALU.is_equal)
                eng.drain()
                eng.tensor_tensor_scan(CNTI, ONEHOT, ONEHOT, 0.0, ALU.add,
                                       ALU.max)
                eng.drain()
                i_ = eng.tensor_tensor(TMP8, ONEHOT, CNTI, ALU.mult)
                inc(i_, "ve", 1)
            sec("vector", v_oh)
            scanve = semc["ve"]

            def g_cc(eng, scanve=scanve):
                eng.wait_ge(semh["ve"], scanve)
                i_ = eng.dma_start(out=cc_in.ap().unsqueeze(1),
                                   in_=CNTI[:, T - 1:T])
                inc(i_, "ci", 16)
                wait(eng, "ci")
                i_ = eng.collective_compute(
                    "AllGather", ALU.bypass,
                    replica_groups=[list(range(8))],
                    ins=[cc_in.ap().opt()], outs=[cc_out.ap().opt()])
                inc(i_, "ag", 1)
                wait(eng, "ag")
                i_ = eng.dma_start(
                    out=agg[:], in_=cc_out.ap().rearrange("(r e) -> r e", e=8))
                inc(i_, "co", 16)
            sec("gpsimd", g_cc)
            sem_of("co")
            aggco = semc["co"]

            def t_off(eng, aggco=aggco):
                eng.wait_ge(semh["co"], aggco)
                wait(eng, "pp")
                pm_produce(eng, "ax")
                i_ = eng.matmul(pax[0:8, 0:1], agg[:], rmk[:],
                                start=True, stop=True)
                inc(i_, "te", 1)
                pm_produce(eng, "ex")
                i_ = eng.matmul(pex[0:1, 0:1], esel_sb[:], ciota[0:8, 0:1],
                                start=True, stop=True)
                inc(i_, "te", 1)
            sec("tensor", t_off)

            def v_off(eng):
                wait(eng, "te")
                eng.tensor_copy(off8[:], pax[0:8, 0:1])
                eng.drain()
                eng.tensor_scalar(small[0:1, S_TMP + 8:S_TMP + 9],
                                  pex[0:1, 0:1], float(8 * T), None,
                                  ALU.mult)
                eng.drain()
                i_ = eng.tensor_copy(smallu[0:1, 40:41],
                                     small[0:1, S_TMP + 8:S_TMP + 9])
                inc(i_, "ve", 1)
                pm_consumed("ax", "ve")
                pm_consumed("ex", "ve")
            sec("vector", v_off)

            veQ0 = semc["ve"]

            def t_q(eng, veQ0=veQ0):
                wait(eng, "ve")
                for q in range(4):
                    bank = ["ax", "ex"][q % 2]
                    if q < 2:
                        pm_produce(eng, bank)
                    else:
                        eng.wait_ge(semh["ve"], veQ0 + q - 1)
                    eng.matmul(PS[bank][0:1, 0:512], ones8[:],
                               TMP8[:, q * 512:q * 512 + 512],
                               start=True, stop=True)
                    i_ = eng.matmul(PS[bank][32:33, 0:512], off8[:],
                                    ONEHOT[:, q * 512:q * 512 + 512],
                                    start=True, stop=True)
                    inc(i_, "te", 1)

            def v_q(eng):
                base = semc["te"] - 4
                for q in range(4):
                    bank = ["ax", "ex"][q % 2]
                    eng.wait_ge(semh["te"], base + q + 1)
                    eng.tensor_copy(RANKR[:, q * 512:q * 512 + 512],
                                    PS[bank][0:1, 0:512])
                    eng.drain()
                    i_ = eng.tensor_copy(OFFR[:, q * 512:q * 512 + 512],
                                         PS[bank][32:33, 0:512])
                    inc(i_, "ve", 1)
                    pm_consumed(bank, "ve")
            sec("tensor", t_q)
            sec("vector", v_q)
            rowve = semc["ve"]

            def d_rc(eng, rowve=rowve):
                eng.wait_ge(semh["ve"], rowve)
                i_ = eng.dma_start(
                    out=er16[:, 0:128],
                    in_=RANKR.rearrange("o (t p) -> o t p", p=128))
                inc(i_, "mo", 16)
                i_ = eng.dma_start(
                    out=er16[:, 128:256],
                    in_=OFFR.rearrange("o (t p) -> o t p", p=128))
                inc(i_, "mo", 16)
                i_ = eng.dma_start(
                    out=small[:, S_TMP + 9:S_TMP + 10],
                    in_=io["cbase"].unsqueeze(0).broadcast_to([128, 1]))
                inc(i_, "mo", 16)
            sec("sync", d_rc)
            rc_mo = semc["mo"]

            def t_rc(eng, rc_mo=rc_mo):
                eng.wait_ge(semh["mo"], rc_mo)
                pm_produce(eng, "ax")
                eng.transpose(pax[:, 0:16], er16[0:16, 0:128],
                              identf[0:16, 0:16])
                i_ = eng.transpose(pax[:, 16:32], er16[0:16, 128:256],
                                   identf[0:16, 0:16])
                inc(i_, "te", 1)
            sec("tensor", t_rc)

            def v_rc(eng):
                wait(eng, "te")
                eng.tensor_copy(sm2[:, C_RANK:C_RANK + 16], pax[:, 0:16])
                eng.drain()
                i_ = eng.tensor_copy(sm2[:, C_OFFT:C_OFFT + 16], pax[:, 16:32])
                inc(i_, "ve", 1)
                pm_consumed("ax", "ve")
            sec("vector", v_rc)
            rc_ve = semc["ve"]

            def v_cols(eng, rc_ve=rc_ve):
                eng.wait_ge(semh["ve"], rc_ve)
                eng.tensor_scalar(sm2[:, C_RANK:C_RANK + 16],
                                  sm2[:, C_RANK:C_RANK + 16], 1.0, None,
                                  ALU.subtract)
                eng.drain()
                eng.tensor_tensor(sm2[:, C_OFFT:C_OFFT + 16],
                                  sm2[:, C_OFFT:C_OFFT + 16],
                                  sm2[:, C_RANK:C_RANK + 16], ALU.add)
                eng.drain()
                eng.tensor_scalar(sm2[:, C_KEPT:C_KEPT + 16],
                                  sm2[:, C_OFFT:C_OFFT + 16],
                                  float(CAPACITY), None, ALU.is_lt)
                eng.drain()
                eng.tensor_tensor(small[:, S_G:S_G + 16],
                                  small[:, S_PSTAR:S_PSTAR + 16],
                                  sm2[:, C_KEPT:C_KEPT + 16], ALU.mult)
                eng.drain()
                eng.tensor_scalar(sm2[:, C_SLOTP:C_SLOTP + 16],
                                  small[:, S_ESTAR:S_ESTAR + 16],
                                  float(8 * T), None, ALU.mult)
                eng.drain()
                eng.tensor_tensor(sm2[:, C_SLOTP:C_SLOTP + 16],
                                  sm2[:, C_SLOTP:C_SLOTP + 16],
                                  sm2[:, C_OFFT:C_OFFT + 16], ALU.add)
                eng.drain()
                eng.tensor_scalar(sm2[:, C_SLSH:C_SLSH + 16],
                                  sm2[:, C_OFFT:C_OFFT + 16],
                                  float(CAPACITY - 1), None, ALU.min)
                eng.drain()
                eng.tensor_scalar(sm2[:, C_KEPT:C_KEPT + 16],
                                  small[:, S_ESTAR:S_ESTAR + 16],
                                  float(CAPACITY), None, ALU.mult)
                eng.drain()
                eng.tensor_tensor(sm2[:, C_SLSH:C_SLSH + 16],
                                  sm2[:, C_SLSH:C_SLSH + 16],
                                  sm2[:, C_KEPT:C_KEPT + 16], ALU.add)
                eng.drain()
                eng.tensor_scalar(sm2[:, C_LG:C_LG + 16], ciota[:, 0:16],
                                  small[:, S_TMP + 9:S_TMP + 10], None,
                                  ALU.add)
                eng.drain()
                eng.tensor_copy(smallu[:, 8:24], sm2[:, C_SLOTP:C_SLOTP + 16])
                eng.drain()
                i_ = eng.tensor_copy(smallu[:, 24:40],
                                     sm2[:, C_SLSH:C_SLSH + 16])
                inc(i_, "ve", 1)
            sec("vector", v_cols)
            colve = semc["ve"]

            def g_idx(eng, colve=colve):
                for k in range(4):
                    i_ = eng.dma_start(
                        out=idxb[k * 32768:(k + 1) * 32768].rearrange(
                            "(p c) -> p c", p=128),
                        in_=ZEROT[:, 0:256])
                    inc(i_, "ix", 16)
                wait(eng, "ix")
                eng.wait_ge(semh["ve"], colve)
                for tc in range(NT):
                    i_ = eng.indirect_dma_start(
                        out=idxb.unsqueeze(1),
                        out_offset=bass.IndirectOffsetOnAxis(
                            smallu[:, 8 + tc:9 + tc], 0),
                        in_=sm2[:, C_LG + tc:C_LG + tc + 1], in_offset=None)
                    inc(i_, "ix", 16)
                wait(eng, "ix")
                i_ = eng.collective_compute(
                    "AllReduce", ALU.add, replica_groups=[list(range(8))],
                    ins=[idxb_t.ap().opt()], outs=[idxall_t.ap().opt()])
                inc(i_, "ag", 1)
            sec("gpsimd", g_idx)
            sem_of("ag")
            idx_ag = semc["ag"]

            def g_ld(eng, idx_ag=idx_ag):
                eng.wait_ge(semh["ag"], idx_ag)
                wait(eng, "ve")
                eng.reg_load(greg, smallu[0:1, 40:41])
                i_ = eng.dma_start(
                    out=i24[:],
                    in_=bass.AP(idxall_t, greg, [[128, 24], [1, 128]]))
                inc(i_, "il", 16)
            sec("gpsimd", g_ld)
            sem_of("il")
            il_v = semc["il"]

            def t_i24(eng, il_v=il_v):
                eng.wait_ge(semh["il"], il_v)
                pm_produce(eng, "ax")
                i_ = eng.transpose(pax[:, 0:24], i24[0:24, 0:128],
                                   identf[0:24, 0:24])
                inc(i_, "te", 1)
            sec("tensor", t_i24)

            def v_i24(eng):
                wait(eng, "te")
                i_ = eng.tensor_copy(u24[:], pax[:, 0:24])
                inc(i_, "ve", 1)
                pm_consumed("ax", "ve")
            sec("vector", v_i24)
            u24ve = semc["ve"]

            def g_gx(eng, u24ve=u24ve, agx_v=agx_v):
                eng.wait_ge(semh["ve"], u24ve)
                eng.wait_ge(semh["ag"], agx_v)
                wait(eng, "te")
                for sc in range(NCC):
                    i_ = eng.indirect_dma_start(
                        out=XB_TM(sc), out_offset=None,
                        in_=xall,
                        in_offset=bass.IndirectOffsetOnAxis(
                            u24[:, sc:sc + 1], 0))
                    inc(i_, "gx", 16)
            sec("gpsimd", g_gx)
            sem_of("gx")
            gx_all = semc["gx"]

            veB0 = semc["ve"]

            def t_trb(eng, gx_all=gx_all, veB0=veB0):
                for ti in range(NCC):
                    eng.wait_ge(semh["gx"], gx_all)
                    eng.wait_ge(semh["ve"], veB0 + ti)
                    i_ = None
                    for dc in range(ND):
                        i_ = eng.transpose(
                            ptr[:, dc * 128:dc * 128 + 128],
                            XB_TM(ti)[:, dc * 128:dc * 128 + 128], ident[:])
                    inc(i_, "te", 1)

            def v_evb(eng):
                base = semc["te"] - NCC
                for ti in range(NCC):
                    if ti == 0:
                        wait(eng, "mo")
                        wait(eng, "ix")
                    eng.wait_ge(semh["te"], base + ti + 1)
                    i_ = None
                    for dc in range(ND):
                        i_ = eng.tensor_copy(
                            XB_FM(dc)[:, ti * 128:ti * 128 + 128],
                            ptr[:, dc * 128:dc * 128 + 128])
                    inc(i_, "ve", 1)

            sec("tensor", t_trb)
            sec("vector", v_evb)
            xbve = semc["ve"]

            if DBG_MOE:
                def g_dbg(eng, xbve=xbve):
                    eng.wait_ge(semh["ve"], xbve)
                    i_ = eng.dma_start(
                        out=dbg_xb.rearrange("(s p) d -> p s d", p=128),
                        in_=ar1.bf16(0, 4 * 1024).rearrange(
                            "p (s d) -> p s d", s=4))
                    inc(i_, "dbg", 16)
                    i_ = eng.dma_start(out=dbg_u24, in_=u24[:])
                    inc(i_, "dbg", 16)
                    i_ = eng.dma_start(out=dbg_i24, in_=i24[:])
                    inc(i_, "dbg", 16)
                    i_ = eng.dma_start(
                        out=er16[0:16, 0:128],
                        in_=bass.AP(idxall_t, 0, [[128, 16], [1, 128]]))
                    inc(i_, "dbg", 16)
                    eng.reg_load(greg, smallu[0:1, 40:41])
                    i_ = eng.dma_start(
                        out=er16[0:16, 128:256],
                        in_=bass.AP(idxall_t, greg, [[128, 16], [1, 128]]))
                    inc(i_, "dbg", 16)
                    wait(eng, "dbg")
                    i_ = eng.dma_start(out=dbg_h[0:16, 0:256], in_=er16[:])
                    inc(i_, "dbg", 16)
                    i_ = eng.dma_start(out=dbg_h[100:101, 0:1],
                                       in_=smallu[0:1, 40:41])
                    inc(i_, "dbg", 16)
                    i_ = eng.dma_start(out=dbg_h[101:102, 0:1],
                                       in_=small[0:1, 91:92])
                    inc(i_, "dbg", 16)
                    wait(eng, "dbg")
                sec("gpsimd", g_dbg)

            def d_b2(eng):
                wait(eng, "ve")
                i_ = eng.dma_start(out=B2BC, in_=io["m_b2"].unsqueeze(0)
                                   .broadcast_to([128, D]))
                inc(i_, "b2", 16)
            sec("sync", d_b2)
            sem_of("b2")
            b2v = semc["b2"]

            set_wstbar()
            YB8 = ["A0", "A1", "B0", "B1", "av", "ax", "ex", "trf"]
            for grp in range(3):
                teG0 = semc["te"]
                seH0 = semc["se"]
                veG0 = semc["ve"]
                for fq in range(8):
                    def g_w1(eng, grp=grp, fq=fq):
                        fetch(eng, WS((fq % 2) * 8192, ND * 512).rearrange(
                            "p (i c) -> p i c", i=ND),
                            io["m_w1"][fq], f"mw{fq % 2}")
                    sec("gpsimd", g_w1)

                    def t_w1(eng, grp=grp, fq=fq, seH0=seH0, xbve=xbve,
                             veG0=veG0):
                        fwait(eng, f"mw{fq % 2}")
                        if grp == 0 and fq == 0:
                            eng.wait_ge(semh["ve"], xbve)
                        if grp > 0 and fq == 0:
                            eng.wait_ge(semh["ve"], veG0)
                        for fc in range(4):
                            for hh in range(2):
                                k = fq * 8 + fc * 2 + hh
                                bank = ROT4[k % 4]
                                if grp == 0 and fq == 0 and k < 4:
                                    pm_produce(eng, bank)
                                elif k >= 4:
                                    eng.wait_ge(semh["se"], seH0 + k - 3)
                                i_ = None
                                for dc in range(ND):
                                    i_ = eng.matmul(
                                        PS[bank],
                                        WS((fq % 2) * 8192, ND * 512).rearrange(
                                            "p (i c) -> p i c", i=ND)[
                                            :, dc, fc * 128:fc * 128 + 128],
                                        XB_FM(dc)[:, grp * 1024 + hh * 512:
                                                  grp * 1024 + hh * 512 + 512],
                                        start=(dc == 0), stop=(dc == ND - 1))
                                inc(i_, "te", 1)
                    sec("tensor", t_w1)
                    freed[f"mw{fq % 2}"] = ("te", semc["te"])

                    def s_h(eng, grp=grp, fq=fq):
                        if grp == 0 and fq == 0:
                            wait(eng, "pp")
                        base = semc["te"] - 8
                        for fc in range(4):
                            for hh in range(2):
                                k = fc * 2 + hh
                                bank = ROT4[(fq * 8 + k) % 4]
                                eng.wait_ge(semh["te"], base + k + 1)
                                fcg = fq * 4 + fc
                                i_ = eng.activation(
                                    HB_T(fcg)[:, hh * 512:hh * 512 + 512],
                                    PS[bank], AF.Gelu,
                                    bias=mb1[:, fcg:fcg + 1])
                                inc(i_, "se", 1)
                                pm_consumed(bank, "se")
                    sec("scalar", s_h)
                hready = semc["se"]

                for dh in range(2):
                    for fq2 in range(4):
                        def g_w2(eng, grp=grp, dh=dh, fq2=fq2):
                            fetch(eng, WS((fq2 % 2) * 8192, 8 * 512).rearrange(
                                "p (i c) -> p i c", i=8),
                                io["m_w2"][dh, fq2], f"mw{fq2 % 2}")
                        sec("gpsimd", g_w2)

                        def t_w2(eng, grp=grp, dh=dh, fq2=fq2, hready=hready):
                            fwait(eng, f"mw{fq2 % 2}")
                            eng.wait_ge(semh["se"], hready)
                            if fq2 == 0:
                                for cc in range(8):
                                    pm_produce(eng, YB8[cc])
                            for cc in range(8):
                                i_ = None
                                for fc in range(8):
                                    fcg = fq2 * 8 + fc
                                    i_ = eng.matmul(
                                        PS[YB8[cc]],
                                        HB_T(fcg)[:, cc * 128:cc * 128 + 128],
                                        WS((fq2 % 2) * 8192, 8 * 512).rearrange(
                                            "p (i c) -> p i c", i=8)[:, fc, :],
                                        start=(fq2 == 0 and fc == 0),
                                        stop=(fq2 == 3 and fc == 7))
                                inc(i_, "te", 1)
                        sec("tensor", t_w2)
                        freed[f"mw{fq2 % 2}"] = ("te", semc["te"])

                    def v_eo(eng, grp=grp, dh=dh, b2v=b2v):
                        eng.wait_ge(semh["b2"], b2v)
                        wait(eng, "eg")
                        wait(eng, "te")
                        for cc in range(8):
                            i_ = eng.tensor_tensor(
                                EOT3(cc)[:, dh * 512:dh * 512 + 512],
                                PS[YB8[cc]], B2BC[:, dh * 512:dh * 512 + 512],
                                ALU.add)
                            eng.drain()
                            inc(i_, "ve", 1)
                            pm_consumed(YB8[cc], "ve")
                    sec("vector", v_eo)

                def d_eog(eng, grp=grp):
                    sem_of("eg")
                    wait(eng, "ve")
                    i_ = eng.dma_start(
                        out=eog[grp * 1024:(grp + 1) * 1024, :].rearrange(
                            "(c p) d -> p c d", p=128),
                        in_=ar3.bf16(0, 8 * D).rearrange(
                            "p (c d) -> p c d", c=8))
                    inc(i_, "eg", 16)
                sec("sync", d_eog)

            def g_ageo(eng):
                wait(eng, "eg")
                i_ = eng.collective_compute(
                    "AllGather", ALU.bypass, replica_groups=[list(range(8))],
                    ins=[eog_t.ap().opt()], outs=[eoall_t.ap().opt()])
                inc(i_, "ag", 1)
            sec("gpsimd", g_ageo)
            eo_ag = semc["ag"]

            ysb = lambda b: ar3.bf16(16 * KB + b * 2048, D)
            out_ot = {0: 0, 1: 0}
            sem_of("o0"); sem_of("o1"); sem_of("yg")

            def g_gy(eng, eo_ag=eo_ag):
                eng.wait_ge(semh["ag"], eo_ag)
                wait(eng, "ve")
                for tc in range(NT):
                    if tc >= 2:
                        eng.wait_ge(semh["ve"], ygve0 + tc - 1)
                    i_ = eng.indirect_dma_start(
                        out=ysb(tc % 2), out_offset=None,
                        in_=eoall,
                        in_offset=bass.IndirectOffsetOnAxis(
                            smallu[:, 24 + tc:25 + tc], 0))
                    inc(i_, "yg", 16)
                    wait(eng, "yg")
                    i_ = eng.sem_inc(semh["ygr"], 1)
                    semc["ygr"] += 1
            sem_of("ygr")
            ygv0 = semc["yg"]
            ygve0 = semc["ve"]
            sec("gpsimd", g_gy)

            for tc in range(NT):
                def d_xin4(eng, tc=tc, po=out_ot[tc % 2]):
                    wait(eng, "xl")
                    wait(eng, "ve")
                    if po:
                        eng.wait_ge(semh[f"o{tc % 2}"], po)
                    i_ = eng.dma_start(
                        out=XIO_M(tc % 2),
                        in_=xres[tc * 128:tc * 128 + 128, :])
                    inc(i_, f"x4{tc % 2}", 16)
                sec("sync", d_xin4)
                sem_of(f"x4{tc % 2}")
                x4v = semc[f"x4{tc % 2}"]

                def v_y(eng, tc=tc, x4v=x4v):
                    eng.wait_ge(semh[f"x4{tc % 2}"], x4v)
                    eng.wait_ge(semh["ygr"], tc + 1)
                    i_ = eng.scalar_tensor_tensor(
                        XIO_M(tc % 2), ysb(tc % 2),
                        small[:, S_G + tc:S_G + tc + 1],
                        XIO_M(tc % 2), ALU.mult, ALU.add)
                    inc(i_, "ve", 1)
                sec("vector", v_y)

                def d_out4(eng, tc=tc):
                    wait(eng, "ve")
                    i_ = eng.dma_start(
                        out=out_t[tc * 128:tc * 128 + 128, :],
                        in_=XIO_M(tc % 2))
                    inc(i_, f"o{tc % 2}", 16)
                sec("sync", d_out4)
                out_ot[tc % 2] = semc[f"o{tc % 2}"]

            def d_fin(eng):
                wait(eng, "o0")
                wait(eng, "o1")
            sec("sync", d_fin)

        def emit_out_from_ar1():
            def d_out(eng):
                wait(eng, "ve")
                wait(eng, "te")
                wait(eng, "se")
                wait(eng, "xl")
                i_ = eng.dma_start(
                    out=out_t.rearrange("(i p) d -> p i d", p=128),
                    in_=ar1.f32(0, NT * D).rearrange("p (i d) -> p i d", i=NT))
                inc(i_, "ot", 16)
                wait(eng, "ot")
            sec("sync", d_out)

        if upto == "ln1":
            ve0 = emit_ln(ar2.f32(0, D))
            def v_cp(eng, ve0=ve0):
                wait(eng, "ve")
                for ti in range(NT):
                    i_ = eng.tensor_copy(X_T(ti), XH_TM(ti))
                    inc(i_, "ve", 1)
            sec("vector", v_cp)
            emit_out_from_ar1()
        elif upto == "attn0":
            emit_attention(0)
            emit_out_from_ar1()
        elif upto == "ffn0":
            emit_attention(0)
            emit_ffn()
            emit_out_from_ar1()
        elif upto == "attn1":
            emit_attention(0)
            emit_ffn()
            emit_attention(1)
            emit_out_from_ar1()
        else:
            emit_attention(0)
            emit_ffn()
            emit_attention(1)
            emit_moe()

    return nc


def kernel(**inputs):
    nc = build_nc(UPTO)
    in_maps = [prep_inputs(b, **inputs) for b in range(8)]
    res = run_bass_kernel_spmd(nc, in_maps, core_ids=list(range(8)))
    return np.stack([r["out"] for r in res.results], axis=0)
